# version 02
# baseline (speedup 1.0000x reference)
import contextlib
import numpy as np
import concourse.bass as bass
import concourse.mybir as mybir
from concourse.bass_utils import run_bass_kernel_spmd

F32 = mybir.dt.float32
BF16 = mybir.dt.bfloat16
U8 = mybir.dt.uint8
AF = mybir.ActivationFunctionType
ALU = mybir.AluOpType
AXX = mybir.AxisListType.X

D = 2048
S = 4096
T = 2048
HID = 5632
NH = 16
NCH = D // 128
EXPS = float(1.0 / np.sqrt(128.0))
EXPB = -12.0
EPS = 1e-5

ENGS = ("sync", "vector", "scalar", "tensor", "gpsimd")


class Plan:

    def __init__(self, sems, counts, marks, entry_guards=()):
        self.sems = sems
        self.counts = counts
        self.marks = marks
        self.ev = {e: [] for e in ENGS}
        self.entry = {e: list(entry_guards) for e in ENGS}

    def do(self, eng, fn, deps=(), sig=None, inc=1):
        evl = self.ev[eng]
        mk = self.marks[eng]
        alldeps = self.entry[eng] + list(deps)
        self.entry[eng] = []
        order = {"vector": "dve", "scalar": "act"}.get(eng)
        if eng == "gpsimd" and inc != 16:
            order = "pool"
        if order is not None:
            alldeps = alldeps + [(order, self.counts.get(order, 0))]
            if sig is None:
                sig = order
            assert sig == order, (eng, sig)
        best = {}
        for d in alldeps:
            if d is None:
                continue
            s, v = d
            if v > best.get(s, 0):
                best[s] = v
        for s, v in best.items():
            if v > 0 and mk.get(s, 0) < v:
                evl.append(("w", s, v))
                mk[s] = v
        tok = None
        if sig is not None:
            self.counts[sig] = self.counts.get(sig, 0) + inc
            tok = (sig, self.counts[sig])
        evl.append(("d", fn, sig, inc))
        return tok

    def run(self, nc):
        with nc.Block() as blk:
            for eng in ENGS:
                items = self.ev[eng]
                if not items:
                    continue

                def closure(e, items=items):
                    for it in items:
                        if it[0] == "w":
                            e.wait_ge(self.sems[it[1]], it[2])
                        else:
                            ins = it[1](e)
                            if it[2] is not None:
                                ins.then_inc(self.sems[it[2]], it[3])

                getattr(blk, eng)(closure)


def build_program(debug=False):
    nc = bass.Bass("TRN2", target_bir_lowering=False)
    KIND = "ExternalOutput" if debug else "Internal"

    xT = nc.dram_tensor("xT", [D, S], F32, kind="ExternalInput").ap()
    wqkv_s = nc.dram_tensor("wqkv_s", [12 * 128, 8192], F32, kind="ExternalInput").ap()
    wo_s = nc.dram_tensor("wo_s", [4 * 128, 8192], F32, kind="ExternalInput").ap()
    wg_s = nc.dram_tensor("wg_s", [11 * 128, 8192], U8, kind="ExternalInput").ap()
    wu_s = nc.dram_tensor("wu_s", [11 * 128, 8192], U8, kind="ExternalInput").ap()
    wd_s = nc.dram_tensor("wd_s", [4 * 128, 22528], U8, kind="ExternalInput").ap()
    sg_s = nc.dram_tensor("sg_s", [128, 44], F32, kind="ExternalInput").ap()
    su_s = nc.dram_tensor("su_s", [128, 44], F32, kind="ExternalInput").ap()
    sd_s = nc.dram_tensor("sd_s", [128, 16], F32, kind="ExternalInput").ap()

    q_dram = nc.dram_tensor("q_dram", [D, S], BF16, kind=KIND).ap()
    k_dram = nc.dram_tensor("k_dram", [D, S], BF16, kind=KIND).ap()
    v_dram = nc.dram_tensor("v_dram", [D, T], BF16, kind=KIND).ap()
    x2_dram = nc.dram_tensor("x2_dram", [D, T], F32, kind=KIND).ap()
    h_dram = nc.dram_tensor("h_dram", [HID, T], BF16, kind=KIND).ap()
    csum_dram = nc.dram_tensor("csum_dram", [NH, T], F32, kind=KIND).ap()

    outT = nc.dram_tensor("outT", [D, T], F32, kind="ExternalOutput").ap()
    wobf_dbg = (nc.dram_tensor("wobf_dbg", [128, 4 * 8192], BF16, kind="ExternalOutput").ap()
                if debug else None)
    wdbg = (nc.dram_tensor("wdbg", [128, 32], F32, kind="ExternalOutput").ap()
            if debug else None)
    valt_dbg = (nc.dram_tensor("valt_dbg", [128, 8192], BF16, kind="ExternalOutput").ap()
                if debug else None)

    counts = {}
    marks = {e: {} for e in ENGS}

    def st_guard():
        return [(s, counts.get(s, 0)) for s in
                ("sq0", "sq1", "sc", "s20", "s21", "sh0", "sh1", "so0", "so1")]

    with contextlib.ExitStack() as g:
        semnames = ("pe", "act", "dve", "pool", "ldi", "lx0", "lx1", "lw",
                    "sq0", "sq1", "lq0", "lq1", "sc", "lv0", "lv1", "lp0",
                    "lp1", "lc0", "lc1", "s20", "s21", "lg", "lu", "sh0",
                    "sh1", "lwd0", "lwd1", "lh0", "lh1", "l20", "l21",
                    "so0", "so1")
        sems = {n: g.enter_context(nc.semaphore(n)) for n in semnames}
        STORE_SEMS = ("sq0", "sq1", "sc", "s20", "s21", "sh0", "sh1",
                      "so0", "so1")

        ones_col = g.enter_context(nc.sbuf_tensor("ones_col", [128, 1], BF16))
        ones_row = g.enter_context(nc.sbuf_tensor("ones_row", [1, 128], F32))
        consts = g.enter_context(nc.sbuf_tensor("consts", [128, 4], F32))
        sg = g.enter_context(nc.sbuf_tensor("sg", [128, 44], F32))
        su = g.enter_context(nc.sbuf_tensor("su", [128, 44], F32))
        sd = g.enter_context(nc.sbuf_tensor("sd", [128, 16], F32))
        srow = g.enter_context(nc.sbuf_tensor("srow", [1, T], F32))
        s2row = g.enter_context(nc.sbuf_tensor("s2row", [1, T], F32))
        mrow = g.enter_context(nc.sbuf_tensor("mrow", [1, T], F32))
        rstdrow = g.enter_context(nc.sbuf_tensor("rstdrow", [1, T], F32))
        mrstdrow = g.enter_context(nc.sbuf_tensor("mrstdrow", [1, T], F32))
        cstage = g.enter_context(nc.sbuf_tensor("cstage", [1, T], F32))
        vrow = cstage
        csrow = g.enter_context(nc.sbuf_tensor("csrow", [1, 2 * 512], F32))
        dparts = g.enter_context(nc.sbuf_tensor("dparts", [128, 16], F32))
        dstage = g.enter_context(nc.sbuf_tensor("dstage", [128, 16], F32))
        actscr = g.enter_context(nc.sbuf_tensor("actscr", [128, 4], F32))
        denb = g.enter_context(nc.sbuf_tensor("denb", [128, 2], F32))
        povn = g.enter_context(nc.sbuf_tensor("povn", [128, 2], F32))
        pscr = g.enter_context(nc.sbuf_tensor("pscr", [128, 2], F32))
        wcolf = g.enter_context(nc.sbuf_tensor("wcolf", [128, 2], F32))
        wcolb = g.enter_context(nc.sbuf_tensor("wcolb", [128, 2], BF16))

        CEXPB, CEXPS, CEPS, CNHALF = 0, 1, 2, 3

        p = Plan(sems, counts, marks)
        t_ones = p.do("vector", lambda e: e.memset(ones_col[:], 1.0))
        p.do("vector", lambda e: e.memset(ones_row[:], 1.0))
        p.do("vector", lambda e: e.memset(consts[:, CEXPB:CEXPB + 1], EXPB))
        p.do("vector", lambda e: e.memset(consts[:, CEXPS:CEXPS + 1], EXPS))
        p.do("vector", lambda e: e.memset(consts[:, CEPS:CEPS + 1], EPS))
        t_cn = p.do("vector", lambda e: e.memset(consts[:, CNHALF:CNHALF + 1], -0.5),
                    sig="dve")
        p.do("sync", lambda e: e.dma_start(out=sg[:], in_=sg_s), sig="ldi", inc=16)
        p.do("sync", lambda e: e.dma_start(out=su[:], in_=su_s), sig="ldi", inc=16)
        t_scl = p.do("sync", lambda e: e.dma_start(out=sd[:], in_=sd_s), sig="ldi", inc=16)
        p.run(nc)

        def ln_rows(p, t_sum_done, t_sq_done, pb_s, pb_q):
            t_rows = []
            for gi in range(4):
                t_rows.append(p.do("vector", lambda e, gi=gi: e.tensor_copy(
                    out=srow[0:1, gi * 512:(gi + 1) * 512], in_=pb_s[gi][0:1, :]),
                    deps=[t_sum_done], sig="dve"))
                t_rows.append(p.do("vector", lambda e, gi=gi: e.tensor_copy(
                    out=s2row[0:1, gi * 512:(gi + 1) * 512], in_=pb_q[gi][0:1, :]),
                    deps=[t_sq_done], sig="dve"))
            p.do("vector", lambda e: e.tensor_scalar_mul(
                mrow[0:1, :], srow[0:1, :], 1.0 / D), deps=[t_rows[-1]])
            p.do("vector", lambda e: e.tensor_scalar_mul(
                vrow[0:1, :], s2row[0:1, :], 1.0 / D))
            p.do("vector", lambda e: e.tensor_tensor(
                srow[0:1, :], mrow[0:1, :], mrow[0:1, :], op=ALU.mult))
            t_var = p.do("vector", lambda e: e.tensor_tensor(
                vrow[0:1, :], vrow[0:1, :], srow[0:1, :], op=ALU.subtract),
                sig="dve")
            t_ln = p.do("scalar", lambda e: e.activation(
                s2row[0:1, :], vrow[0:1, :], AF.Ln,
                bias=consts[0:1, CEPS:CEPS + 1]), deps=[t_var], sig="act")
            t_rstd = p.do("scalar", lambda e: e.activation(
                rstdrow[0:1, :], s2row[0:1, :], AF.Exp,
                scale=consts[0:1, CNHALF:CNHALF + 1]), deps=[t_ln], sig="act")
            t_mrstd = p.do("vector", lambda e: e.tensor_tensor(
                mrstdrow[0:1, :], mrow[0:1, :], rstdrow[0:1, :], op=ALU.mult),
                deps=[t_rstd], sig="dve")
            return t_rows[-1], t_rstd, t_mrstd

        def ln_bcasts(p, t_rstd, t_mrstd, t_banks_free, pb):
            t_bc = []
            for gi in range(4):
                tm = p.do("tensor", lambda e, gi=gi: e.matmul(
                    pb[2 * gi][:, :], ones_row[:],
                    rstdrow[0:1, gi * 512:(gi + 1) * 512], start=True, stop=True),
                    deps=[t_rstd, t_banks_free], sig="pe")
                t_bc.append(tm)
                tm = p.do("tensor", lambda e, gi=gi: e.matmul(
                    pb[2 * gi + 1][:, :], ones_row[:],
                    mrstdrow[0:1, gi * 512:(gi + 1) * 512], start=True, stop=True),
                    deps=[t_mrstd], sig="pe")
                t_bc.append(tm)
            return t_bc[-1]

        qkv_last_store = [None]

        def phase_ab(half):
            with contextlib.ExitStack() as sc:
                XLNT = sc.enter_context(nc.sbuf_tensor(f"XLNT{half}", [128, NCH * T], BF16))
                XCH = sc.enter_context(nc.sbuf_tensor(f"XCH{half}", [128, 2 * T], F32))
                NRM = sc.enter_context(nc.sbuf_tensor(f"NRM{half}", [128, 512], F32))
                WF32 = sc.enter_context(nc.sbuf_tensor(f"WF32{half}", [128, 4096], F32))
                WBF = sc.enter_context(nc.sbuf_tensor(f"WBF{half}", [128, 2 * 8192], BF16))
                QSTG = sc.enter_context(nc.sbuf_tensor(f"QSTG{half}", [128, 2 * 2048], BF16))
                pb = [sc.enter_context(nc.psum_tensor(f"pb{half}_{i}", [128, 512], F32))
                      for i in range(8)]
                XBF = lambda par: WBF[:, par * 2048:(par + 1) * 2048]
                SQB = lambda par: WBF[:, 4096 + par * 2048: 4096 + (par + 1) * 2048]

                p = Plan(sems, counts, marks, entry_guards=st_guard())
                cols = slice(half * T, (half + 1) * T)

                t_ld, t_cast, t_sq, t_summ, t_sqm = {}, {}, {}, {}, {}
                for i in range(NCH):
                    par = i % 2
                    po, pe_ = par * T, (par + 1) * T
                    deps = [t_ones, t_cn]
                    if i >= 2:
                        deps += [t_cast[i - 2]]
                    t_ld[i] = p.do("sync", lambda e, i=i, po=po, pe_=pe_: e.dma_start(
                        out=XCH[:, po:pe_], in_=xT[i * 128:(i + 1) * 128, cols]),
                        deps=deps, sig=f"lx{i % 2}", inc=16)
                    deps = [t_ld[i]]
                    if i >= 2:
                        deps += [t_summ[i - 2], t_sq[i - 2]]
                    t_cast[i] = p.do("vector", lambda e, par=par, po=po, pe_=pe_:
                                     e.tensor_copy(out=XBF(par), in_=XCH[:, po:pe_]),
                                     deps=deps, sig="dve")
                    deps = [t_cast[i]]
                    if i >= 2:
                        deps += [t_sqm[i - 2]]
                    t_sq[i] = p.do("gpsimd", lambda e, par=par: e.tensor_tensor(
                        out=SQB(par), in0=XBF(par), in1=XBF(par), op=ALU.mult),
                        deps=deps, sig="pool")
                    for gi in range(4):
                        tok = p.do("tensor", lambda e, i=i, par=par, gi=gi: e.matmul(
                            pb[gi][0:1, :], ones_col[:],
                            XBF(par)[:, gi * 512:(gi + 1) * 512],
                            start=(i == 0), stop=(i == NCH - 1),
                            skip_group_check=True),
                            deps=[t_cast[i]] if gi == 0 else [],
                            sig="pe" if gi == 3 else None)
                    t_summ[i] = tok
                    for gi in range(4):
                        tok = p.do("tensor", lambda e, i=i, par=par, gi=gi: e.matmul(
                            pb[4 + gi][0:1, :], ones_col[:],
                            SQB(par)[:, gi * 512:(gi + 1) * 512],
                            start=(i == 0), stop=(i == NCH - 1),
                            skip_group_check=True),
                            deps=[t_sq[i]] if gi == 0 else [],
                            sig="pe" if gi == 3 else None)
                    t_sqm[i] = tok

                t_rows_done, t_rstd, t_mrstd = ln_rows(
                    p, t_summ[NCH - 1], t_sqm[NCH - 1], pb[0:4], pb[4:8])
                t_bc_done = ln_bcasts(p, t_rstd, t_mrstd, t_rows_done, pb)

                t_norm = {}
                for i in range(NCH):
                    par = i % 2
                    po, pe_ = par * T, (par + 1) * T
                    deps = [t_bc_done]
                    if i >= 2:
                        deps += [t_norm[i - 2]]
                    t_l2 = p.do("sync", lambda e, i=i, po=po, pe_=pe_: e.dma_start(
                        out=XCH[:, po:pe_], in_=xT[i * 128:(i + 1) * 128, cols]),
                        deps=deps, sig=f"lx{i % 2}", inc=16)
                    for g4 in range(4):
                        p.do("vector", lambda e, po=po, g4=g4: e.tensor_tensor(
                            NRM[:, :], XCH[:, po + g4 * 512:po + (g4 + 1) * 512],
                            pb[2 * g4][:, :], op=ALU.mult),
                            deps=[t_l2] if g4 == 0 else [])
                        tok = p.do("vector", lambda e, i=i, g4=g4: e.tensor_tensor(
                            XLNT[:, i * T + g4 * 512:i * T + (g4 + 1) * 512],
                            NRM[:, :], pb[2 * g4 + 1][:, :], op=ALU.subtract),
                            sig="dve" if g4 == 3 else None)
                    t_norm[i] = tok

                slabs = list(range(12)) if half == 0 else list(range(8))
                t_wcast, t_mmlast, t_ev, t_st = {}, {}, {}, {}
                for si, s in enumerate(slabs):
                    wslot = si % 2
                    for piece in range(2):
                        deps = []
                        if si == 0 and piece == 0:
                            deps += [t_summ[NCH - 1], t_sqm[NCH - 1]]
                        if si > 0 or piece > 0:
                            deps += [t_wcast.get((si, piece - 1),
                                                 t_wcast.get((si - 1, 1)))]
                        t_wl = p.do("sync", lambda e, s=s, piece=piece: e.dma_start(
                            out=WF32[:, :],
                            in_=wqkv_s[s * 128:(s + 1) * 128,
                                       piece * 4096:(piece + 1) * 4096]),
                            deps=deps, sig="lw", inc=16)
                        deps = [t_wl]
                        if si >= 2 and piece == 0:
                            deps += [t_mmlast[slabs[si - 2]]]
                        t_wcast[(si, piece)] = p.do(
                            "vector", lambda e, wslot=wslot, piece=piece: e.tensor_copy(
                                out=WBF[:, wslot * 8192 + piece * 4096:
                                        wslot * 8192 + (piece + 1) * 4096],
                                in_=WF32[:, :]),
                            deps=deps, sig="dve")
                    for fc in range(4):
                        rowi = si * 4 + fc
                        stg = rowi % 2
                        for tg in range(4):
                            key = rowi * 4 + tg
                            pp = pb[2 + key % 2]
                            deps = [t_wcast[(si, 1)], t_norm[NCH - 1]]
                            if key >= 2:
                                deps += [t_ev.get(key - 2)]
                            for c in range(NCH):
                                tok = p.do("tensor",
                                           lambda e, wslot=wslot, fc=fc, tg=tg, c=c,
                                           pp=pp: e.matmul(
                                               pp[:, :],
                                               WBF[:, wslot * 8192 + c * 512 + fc * 128:
                                                   wslot * 8192 + c * 512 + (fc + 1) * 128],
                                               XLNT[:, c * T + tg * 512:
                                                    c * T + (tg + 1) * 512],
                                               start=(c == 0), stop=(c == NCH - 1),
                                               skip_group_check=True),
                                           deps=deps if c == 0 else [],
                                           sig="pe" if c == NCH - 1 else None)
                            t_mm = tok
                            if fc == 3 and tg == 3:
                                t_mmlast[s] = t_mm
                            deps = [t_mm]
                            if rowi >= 2:
                                deps += [t_st.get(rowi - 2)]
                            t_ev[key] = p.do("scalar",
                                             lambda e, stg=stg, tg=tg, pp=pp: e.copy(
                                                 out=QSTG[:, stg * 2048 + tg * 512:
                                                          stg * 2048 + (tg + 1) * 512],
                                                 in_=pp[:, :]),
                                             deps=deps, sig="act")
                        if s < 4:
                            dst, r0, c0 = q_dram, s * 512 + fc * 128, half * T
                        elif s < 8:
                            dst, r0, c0 = k_dram, (s - 4) * 512 + fc * 128, half * T
                        else:
                            dst, r0, c0 = v_dram, (s - 8) * 512 + fc * 128, 0
                        t_st[rowi] = p.do("gpsimd",
                                          lambda e, dst=dst, r0=r0, c0=c0,
                                          stg=stg: e.dma_start(
                                              out=dst[r0:r0 + 128, c0:c0 + T],
                                              in_=QSTG[:, stg * 2048:stg * 2048 + 2048]),
                                          deps=[t_ev[rowi * 4 + 3]],
                                          sig=f"sq{rowi % 2}", inc=16)
                qkv_last_store[0] = [("sq0", counts["sq0"]), ("sq1", counts["sq1"])]
                p.run(nc)

        phase_ab(0)
        phase_ab(1)
        qkv_done = qkv_last_store[0]

        t_csum_done = None
        with contextlib.ExitStack() as sc:
            QH = sc.enter_context(nc.sbuf_tensor("QH", [128, 2 * S], BF16))
            KH = sc.enter_context(nc.sbuf_tensor("KH", [128, 2 * S], BF16))
            EOWN = sc.enter_context(nc.sbuf_tensor("EOWN", [128, 4 * T], BF16))
            ESCR = sc.enter_context(nc.sbuf_tensor("ESCR", [128, 2 * T], BF16))
            WDB = (sc.enter_context(nc.sbuf_tensor("WDB", [128, 32], F32))
                   if debug else None)
            sq = [sc.enter_context(nc.psum_tensor(f"sq{i}", [128, 1024], F32))
                  for i in range(3)]
            cs_ps = sc.enter_context(nc.psum_tensor("cs_ps", [128, 512], F32))

            p = Plan(sems, counts, marks, entry_guards=st_guard())
            t_qld, t_kld, t_head, t_csev, t_exp, t_w, t_cs = {}, {}, {}, {}, {}, {}, {}
            t_pr, t_dr, t_tail, t_wtail = {}, {}, {}, {}
            qseq = 0

            def emit_colsum(h, qc):
                gqc = h * 32 + qc
                deps = [t_w[gqc], t_w.get(gqc + 1) or t_wtail.get(h)]
                if qc == 0:
                    deps += [t_csev.get(h - 1)]
                for kc in range(4):
                    tok = p.do("tensor", lambda e, h=h, qc=qc, kc=kc: e.matmul(
                        cs_ps[32 * kc:32 * kc + 1, :],
                        wcolb[:, qc % 2:qc % 2 + 1],
                        EOWN[:, (qc % 4) * T + kc * 512:(qc % 4) * T + (kc + 1) * 512],
                        start=(qc == 0), stop=(qc == 31),
                        tile_position=(0, 32 * kc), skip_group_check=True),
                        deps=deps if kc == 0 else [],
                        sig="pe" if kc == 3 else None)
                t_cs[gqc] = tok

            for h in range(NH):
                hs = (h % 2) * S
                deps = list(qkv_done)
                if h >= 2:
                    deps += [t_head[h - 2]]
                t_qld[h] = p.do("sync", lambda e, h=h, hs=hs: e.dma_start(
                    out=QH[:, hs:hs + S], in_=q_dram[h * 128:(h + 1) * 128, :]),
                    deps=deps, sig=f"lq{h % 2}", inc=16)
                t_kld[h] = p.do("sync", lambda e, h=h, hs=hs: e.dma_start(
                    out=KH[:, hs:hs + S], in_=k_dram[h * 128:(h + 1) * 128, :]),
                    sig=f"lq{h % 2}", inc=16)

                def emit_denw(qc):
                    gqc = h * 32 + qc
                    e_tok = t_exp[(gqc + 1) * 4 - 1]
                    e_tok2 = t_exp.get((gqc + 1) * 4 + 1) or t_tail.get(h)
                    t_pr[gqc] = p.do("vector", lambda e, qc=qc: e.reduce_sum(
                        povn[:, qc % 2:qc % 2 + 1],
                        EOWN[:, (qc % 4) * T:(qc % 4 + 1) * T], axis=AXX),
                        deps=[e_tok, e_tok2] +
                        ([t_w.get(gqc - 2)] if gqc >= 2 else []),
                        sig="dve")
                    t_dr[gqc] = p.do("vector", lambda e, qc=qc: e.reduce_sum(
                        pscr[:, qc % 2:qc % 2 + 1],
                        ESCR[:, (qc % 2) * T:(qc % 2 + 1) * T], axis=AXX),
                        deps=[e_tok, e_tok2], sig="dve")
                    p.do("vector", lambda e, qc=qc: e.tensor_add(
                        denb[:, qc % 2:qc % 2 + 1], povn[:, qc % 2:qc % 2 + 1],
                        pscr[:, qc % 2:qc % 2 + 1]), deps=[t_pr[gqc]])
                    p.do("vector", lambda e, qc=qc: e.reciprocal(
                        wcolf[:, qc % 2:qc % 2 + 1], denb[:, qc % 2:qc % 2 + 1]))
                    t_w[gqc] = p.do("vector", lambda e, qc=qc: e.tensor_copy(
                        out=wcolb[:, qc % 2:qc % 2 + 1],
                        in_=wcolf[:, qc % 2:qc % 2 + 1]),
                        deps=[t_cs.get(gqc - 2)] if gqc >= 2 else [], sig="dve")
                    if debug and h == 0:
                        p.do("vector", lambda e, qc=qc: e.tensor_copy(
                            out=WDB[:, qc:qc + 1],
                            in_=wcolf[:, qc % 2:qc % 2 + 1]))

                for qc in range(32):
                    gqc = h * 32 + qc
                    for q4 in range(4):
                        buf = sq[qseq % 3]
                        deps = [t_qld[h], t_kld[h]]
                        if qseq >= 3:
                            deps += [t_exp[qseq - 3]]
                        p.do("tensor", lambda e, hs=hs, qc=qc, q4=q4, buf=buf: e.matmul(
                            buf[:, 0:512],
                            QH[:, hs + qc * 128:hs + (qc + 1) * 128],
                            KH[:, hs + q4 * 1024:hs + q4 * 1024 + 512],
                            start=True, stop=True),
                            deps=deps)
                        t_sc = p.do("tensor", lambda e, hs=hs, qc=qc, q4=q4,
                                    buf=buf: e.matmul(
                                        buf[:, 512:1024],
                                        QH[:, hs + qc * 128:hs + (qc + 1) * 128],
                                        KH[:, hs + q4 * 1024 + 512:
                                           hs + (q4 + 1) * 1024],
                                        start=True, stop=True),
                                    sig="pe")
                        if q4 < 2:
                            dtile, dsl = EOWN, ((qc % 4) * T + q4 * 1024,
                                                (qc % 4) * T + (q4 + 1) * 1024)
                        else:
                            dtile, dsl = ESCR, ((qc % 2) * T + (q4 - 2) * 1024,
                                                (qc % 2) * T + (q4 - 1) * 1024)
                        deps = [t_sc]
                        if q4 == 0 and gqc >= 4:
                            deps += [t_cs.get(gqc - 4), t_pr.get(gqc - 4)]
                        if q4 == 2 and gqc >= 2:
                            deps += [t_dr.get(gqc - 2)]
                        t_exp[qseq] = p.do("scalar",
                                           lambda e, dtile=dtile, dsl=dsl, qc=qc,
                                           q4=q4, buf=buf: e.activation(
                                               dtile[:, dsl[0]:dsl[1]], buf[:, :],
                                               AF.Exp,
                                               bias=consts[:, CEXPB:CEXPB + 1],
                                               scale=consts[:, CEXPS:CEXPS + 1]),
                                           deps=deps, sig="act")
                        qseq += 1
                    if qc >= 1:
                        emit_denw(qc - 1)
                    if qc >= 2:
                        emit_colsum(h, qc - 2)
                t_tail[h] = p.do("scalar", lambda e: e.copy(
                    out=actscr[:, 0:1], in_=consts[:, 0:1]), sig="act")
                p.do("scalar", lambda e: e.copy(
                    out=actscr[:, 1:2], in_=consts[:, 0:1]), sig="act")
                emit_denw(31)
                t_wtail[h] = p.do("vector", lambda e: e.tensor_copy(
                    out=actscr[:, 2:3], in_=consts[:, 0:1]), sig="dve")
                emit_colsum(h, 30)
                emit_colsum(h, 31)
                t_head[h] = t_cs[h * 32 + 31]
                t_cp = {}
                for kc in range(4):
                    t_cp[kc] = p.do("vector", lambda e, kc=kc: e.tensor_copy(
                        out=cstage[0:1, kc * 512:(kc + 1) * 512],
                        in_=cs_ps[32 * kc:32 * kc + 1, :]),
                        deps=([t_head[h]] if kc == 0 else []) +
                        ([t_csev.get(h - 1)] if kc == 0 else []),
                        sig="dve" if kc == 3 else None)
                t_csev[h] = p.do("gpsimd", lambda e, h=h: e.dma_start(
                    out=csum_dram[h:h + 1, :], in_=cstage[0:1, :]),
                    deps=[t_cp[3]], sig="sc", inc=16)
            if debug:
                p.do("gpsimd", lambda e: e.dma_start(out=wdbg, in_=WDB[:, :]),
                     deps=[t_w[31]], sig="sc", inc=16)
            t_csum_done = t_csev[NH - 1]
            p.run(nc)

        with contextlib.ExitStack() as sc:
            WF32 = sc.enter_context(nc.sbuf_tensor("WF32d", [128, 2048], F32))
            WOBF = sc.enter_context(nc.sbuf_tensor("WOBF", [128, 4 * 8192], BF16))
            VAL = sc.enter_context(nc.sbuf_tensor("VAL", [128, 8192], BF16))
            VALT = sc.enter_context(nc.sbuf_tensor("VALT", [128, 8192], BF16))
            CBB = sc.enter_context(nc.sbuf_tensor("CBB", [128, 1024], BF16))
            XCHd = sc.enter_context(nc.sbuf_tensor("XCHd", [128, 2 * 2048], F32))
            X2ST = sc.enter_context(nc.sbuf_tensor("X2ST", [128, 2 * 2048], F32))
            X2BF = sc.enter_context(nc.sbuf_tensor("X2BF", [128, 2 * 512], BF16))
            X2SQ = sc.enter_context(nc.sbuf_tensor("X2SQ", [128, 2 * 512], BF16))
            pb = [sc.enter_context(nc.psum_tensor(f"pd{i}", [128, 512], F32))
                  for i in range(6)]

            p = Plan(sems, counts, marks, entry_guards=st_guard())
            t_wc = None
            for s in range(4):
                for piece in range(4):
                    deps = [t_wc] if t_wc else []
                    t_wl = p.do("sync", lambda e, s=s, piece=piece: e.dma_start(
                        out=WF32[:, :],
                        in_=wo_s[s * 128:(s + 1) * 128,
                                 piece * 2048:(piece + 1) * 2048]),
                        deps=deps, sig="lw", inc=16)
                    t_wc = p.do("vector", lambda e, s=s, piece=piece: e.tensor_copy(
                        out=WOBF[:, s * 8192 + piece * 2048:
                                s * 8192 + (piece + 1) * 2048],
                        in_=WF32[:, :]), deps=[t_wl], sig="dve")

            t_vld, t_val, t_x2, t_x2st, t_stat, t_bfd = {}, {}, {}, {}, {}, {}
            t_crl, t_bm = {}, {}
            t_xld = {}
            for gi in range(4):
                for h8 in range(2):
                    deps = list(qkv_done)
                    if gi >= 1:
                        deps += [t_val.get((gi - 1, h8 * 8 + 7))]
                    t_vld[(gi, h8)] = p.do("sync", lambda e, gi=gi, h8=h8: e.dma_start(
                        out=VAL.rearrange("p (c t) -> p c t", t=512)
                        [:, h8 * 8:(h8 + 1) * 8, :],
                        in_=v_dram.rearrange("(c p) t -> p c t", p=128)
                        [:, h8 * 8:(h8 + 1) * 8, gi * 512:(gi + 1) * 512]),
                        deps=deps, sig=f"lv{h8}", inc=16)
                def emit_xld(pc, gi=gi):
                    kk = gi * 4 + pc
                    deps = []
                    if kk >= 2:
                        kp = kk - 2
                        d = t_x2.get((kp // 4, (kp % 4) * 4 + 3))
                        assert d is not None
                        deps += [d]
                    t_xld[kk] = p.do("sync", lambda e, gi=gi, pc=pc: e.dma_start(
                        out=XCHd.rearrange("p (c t) -> p c t", t=512)
                        [:, (pc % 2) * 4:(pc % 2) * 4 + 4, :],
                        in_=xT.rearrange("(c p) t -> p c t", p=128)
                        [:, pc * 4:pc * 4 + 4, gi * 512:(gi + 1) * 512]),
                        deps=deps, sig=f"lp{pc % 2}", inc=16)
                for pc in range(2):
                    emit_xld(pc)
                t_cbd = {}

                def emit_val(h, gi=gi):
                    t_val[(gi, h)] = p.do("vector",
                                          lambda e, h=h: e.tensor_tensor(
                                              VALT[:, h * 512:(h + 1) * 512],
                                              VAL[:, h * 512:(h + 1) * 512],
                                              CBB[:, (h % 2) * 512:(h % 2 + 1) * 512],
                                              op=ALU.mult),
                                          deps=[t_cbd[h], t_cbd.get(h + 1),
                                                t_vld[(gi, h // 8)]], sig="dve")

                for h in range(NH):
                    deps = [t_csum_done]
                    if h >= 2:
                        deps += [t_bm.get((gi, h - 2))]
                    elif gi >= 1:
                        deps += [t_bm.get((gi - 1, NH - 2 + h))]
                    t_crl[(gi, h)] = p.do("sync", lambda e, gi=gi, h=h: e.dma_start(
                        out=csrow[0:1, (h % 2) * 512:(h % 2 + 1) * 512],
                        in_=csum_dram[h:h + 1, gi * 512:(gi + 1) * 512]),
                        deps=deps, sig=f"lc{h % 2}", inc=16)
                    deps = [t_crl[(gi, h)], t_wc]
                    if h >= 2:
                        deps += [t_val.get((gi, h - 2))]
                    elif gi >= 1:
                        deps += [t_val.get((gi - 1, NH - 2 + h))]
                    t_bm[(gi, h)] = p.do("tensor", lambda e, h=h: e.matmul(
                        pb[h % 2][:, :], ones_row[:],
                        csrow[0:1, (h % 2) * 512:(h % 2 + 1) * 512],
                        start=True, stop=True), deps=deps, sig="pe")
                    t_cbd[h] = p.do("scalar", lambda e, h=h: e.copy(
                        out=CBB[:, (h % 2) * 512:(h % 2 + 1) * 512],
                        in_=pb[h % 2][:, :]), deps=[t_bm[(gi, h)]], sig="act")
                    if h >= 1:
                        emit_val(h - 1)
                emit_val(NH - 1)
                for oc in range(NCH):
                    k = gi * NCH + oc
                    pp = pb[2 + oc % 2]
                    deps = [t_val[(gi, NH - 1)]]
                    if k >= 2:
                        deps += [t_x2.get(((k - 2) // NCH, (k - 2) % NCH))]
                    for c in range(NCH):
                        tok = p.do("tensor", lambda e, oc=oc, c=c, pp=pp: e.matmul(
                            pp[:, :],
                            WOBF[:, (oc // 4) * 8192 + c * 512 + (oc % 4) * 128:
                                 (oc // 4) * 8192 + c * 512 + (oc % 4) * 128 + 128],
                            VALT[:, c * 512:(c + 1) * 512],
                            start=(c == 0), stop=(c == NCH - 1),
                            skip_group_check=True),
                            deps=deps if c == 0 else [],
                            sig="pe" if c == NCH - 1 else None)
                    t_amm = tok
                    slot4 = (oc // 4) % 2
                    deps = [t_amm, t_xld[gi * 4 + oc // 4]]
                    if k >= 8:
                        deps += [t_x2st.get(((k - 8) // NCH, ((k - 8) % NCH) // 4))]
                    t_x2[(gi, oc)] = p.do("vector",
                                          lambda e, oc=oc, pp=pp, slot4=slot4:
                                          e.tensor_tensor(
                                              X2ST[:, slot4 * 2048 + (oc % 4) * 512:
                                                   slot4 * 2048 + (oc % 4 + 1) * 512],
                                              XCHd[:, ((oc // 4) % 2) * 2048 +
                                                   (oc % 4) * 512:
                                                   ((oc // 4) % 2) * 2048 +
                                                   (oc % 4 + 1) * 512],
                                              pp[:, :], op=ALU.add),
                                          deps=deps, sig="dve")
                    deps = [t_x2[(gi, oc)]]
                    if k >= 2:
                        deps += [t_stat.get(k - 2)]
                    t_bfd[k] = p.do("vector", lambda e, oc=oc, slot4=slot4:
                                    e.tensor_copy(
                                        out=X2BF[:, (oc % 2) * 512:(oc % 2 + 1) * 512],
                                        in_=X2ST[:, slot4 * 2048 + (oc % 4) * 512:
                                                 slot4 * 2048 + (oc % 4 + 1) * 512]),
                                    deps=deps, sig="dve")
                    t_sq2 = p.do("gpsimd", lambda e, oc=oc: e.tensor_tensor(
                        out=X2SQ[:, (oc % 2) * 512:(oc % 2 + 1) * 512],
                        in0=X2BF[:, (oc % 2) * 512:(oc % 2 + 1) * 512],
                        in1=X2BF[:, (oc % 2) * 512:(oc % 2 + 1) * 512], op=ALU.mult),
                        deps=[t_bfd[k]], sig="pool")
                    p.do("tensor", lambda e, oc=oc: e.matmul(
                        pb[4][0:1, :], ones_col[:],
                        X2BF[:, (oc % 2) * 512:(oc % 2 + 1) * 512],
                        start=(oc == 0), stop=(oc == NCH - 1),
                        skip_group_check=True), deps=[t_bfd[k]])
                    t_stat[k] = p.do("tensor", lambda e, oc=oc: e.matmul(
                        pb[5][0:1, :], ones_col[:],
                        X2SQ[:, (oc % 2) * 512:(oc % 2 + 1) * 512],
                        start=(oc == 0), stop=(oc == NCH - 1),
                        skip_group_check=True), deps=[t_sq2], sig="pe")
                    if oc % 4 == 3 and oc // 4 < 2:
                        emit_xld(oc // 4 + 2)
                    if oc % 4 == 3:
                        t_x2st[(gi, oc // 4)] = p.do(
                            "gpsimd",
                            lambda e, gi=gi, oc=oc, slot4=slot4: e.dma_start(
                                out=x2_dram.rearrange("(c p) t -> p c t", p=128)
                                [:, oc - 3:oc + 1, gi * 512:(gi + 1) * 512],
                                in_=X2ST.rearrange("p (c t) -> p c t", t=512)
                                [:, slot4 * 4:slot4 * 4 + 4, :]),
                            deps=[t_x2[(gi, oc)]], sig=f"s2{slot4}", inc=16)
                p.do("vector", lambda e, gi=gi: e.tensor_copy(
                    out=srow[0:1, gi * 512:(gi + 1) * 512], in_=pb[4][0:1, :]),
                    deps=[t_stat[gi * NCH + NCH - 1]], sig="dve")
                p.do("vector", lambda e, gi=gi: e.tensor_copy(
                    out=s2row[0:1, gi * 512:(gi + 1) * 512], in_=pb[5][0:1, :]),
                    deps=[t_stat[gi * NCH + NCH - 1]], sig="dve")
            t_x2_all = None
            x2_done = [("s20", counts["s20"]), ("s21", counts["s21"])]
            if debug:
                p.do("gpsimd", lambda e: e.dma_start(out=wobf_dbg, in_=WOBF[:, :]),
                     deps=x2_done, sig="sc", inc=16)
                p.do("gpsimd", lambda e: e.dma_start(out=valt_dbg, in_=VALT[:, :]),
                     sig="sc", inc=16)

            p.do("vector", lambda e: e.tensor_scalar_mul(
                mrow[0:1, :], srow[0:1, :], 1.0 / D))
            p.do("vector", lambda e: e.tensor_scalar_mul(
                vrow[0:1, :], s2row[0:1, :], 1.0 / D))
            p.do("vector", lambda e: e.tensor_tensor(
                srow[0:1, :], mrow[0:1, :], mrow[0:1, :], op=ALU.mult))
            t_v2 = p.do("vector", lambda e: e.tensor_tensor(
                vrow[0:1, :], vrow[0:1, :], srow[0:1, :], op=ALU.subtract),
                sig="dve")
            t_ln2 = p.do("scalar", lambda e: e.activation(
                s2row[0:1, :], vrow[0:1, :], AF.Ln,
                bias=consts[0:1, CEPS:CEPS + 1]), deps=[t_v2], sig="act")
            t_rstd2 = p.do("scalar", lambda e: e.activation(
                rstdrow[0:1, :], s2row[0:1, :], AF.Exp,
                scale=consts[0:1, CNHALF:CNHALF + 1]), deps=[t_ln2], sig="act")
            t_mrstd2 = p.do("vector", lambda e: e.tensor_tensor(
                mrstdrow[0:1, :], mrow[0:1, :], rstdrow[0:1, :], op=ALU.mult),
                deps=[t_rstd2], sig="dve")
            p.run(nc)

        t_hst = {}
        with contextlib.ExitStack() as sc:
            XLNT = sc.enter_context(nc.sbuf_tensor("XLNT2", [128, NCH * T], BF16))
            XCH = sc.enter_context(nc.sbuf_tensor("XCH2", [128, 2 * T], F32))
            NRM = sc.enter_context(nc.sbuf_tensor("NRM2", [128, 512], F32))
            WGU8 = sc.enter_context(nc.sbuf_tensor("WGU8", [128, 2 * 8192], U8))
            WGBF = sc.enter_context(nc.sbuf_tensor("WGBF", [128, 8192], BF16))
            WUBF = sc.enter_context(nc.sbuf_tensor("WUBF", [128, 8192], BF16))
            GT = sc.enter_context(nc.sbuf_tensor("GT", [128, 2 * 512], BF16))
            HST = sc.enter_context(nc.sbuf_tensor("HST", [128, 2 * 2048], BF16))
            pb = [sc.enter_context(nc.psum_tensor(f"pq{i}", [128, 512], F32))
                  for i in range(8)]

            p = Plan(sems, counts, marks, entry_guards=st_guard())
            t_bc_done = ln_bcasts(p, t_rstd2, t_mrstd2, None, pb)
            t_norm = {}
            for i in range(NCH):
                par = i % 2
                po, pe_ = par * T, (par + 1) * T
                deps = [t_bc_done] + x2_done
                if i >= 2:
                    deps += [t_norm[i - 2]]
                t_l2 = p.do("sync", lambda e, i=i, po=po, pe_=pe_: e.dma_start(
                    out=XCH[:, po:pe_], in_=x2_dram[i * 128:(i + 1) * 128, :]),
                    deps=deps, sig=f"lx{i % 2}", inc=16)
                for g4 in range(4):
                    p.do("vector", lambda e, po=po, g4=g4: e.tensor_tensor(
                        NRM[:, :], XCH[:, po + g4 * 512:po + (g4 + 1) * 512],
                        pb[2 * g4][:, :], op=ALU.mult),
                        deps=[t_l2] if g4 == 0 else [])
                    tok = p.do("vector", lambda e, i=i, g4=g4: e.tensor_tensor(
                        XLNT[:, i * T + g4 * 512:i * T + (g4 + 1) * 512],
                        NRM[:, :], pb[2 * g4 + 1][:, :], op=ALU.subtract),
                        sig="dve" if g4 == 3 else None)
                t_norm[i] = tok

            t_gc, t_uc, t_mmg, t_mmu, t_h = {}, {}, {}, {}, {}
            pend = {}

            def emit_h(kk):
                tok, t_silu, cc, gi, ppu = pend[kk]
                deps = [tok, t_silu, t_scl]
                if gi == 0 and cc >= 2:
                    deps += [t_hst.get(cc - 2)]
                t_h[kk] = p.do("vector",
                               lambda e, cc=cc, gi=gi, ppu=ppu:
                               e.scalar_tensor_tensor(
                                   HST[:, (cc % 2) * 2048 + gi * 512:
                                       (cc % 2) * 2048 + (gi + 1) * 512],
                                   ppu[:, :], su[:, cc:cc + 1],
                                   GT[:, (gi % 2) * 512:(gi % 2 + 1) * 512],
                                   op0=ALU.mult, op1=ALU.mult),
                               deps=deps, sig="dve")

            for s in range(11):
                deps = [t_gc[s - 1]] if s >= 1 else []
                t_gl = p.do("sync", lambda e, s=s: e.dma_start(
                    out=WGU8[:, 0:8192], in_=wg_s[s * 128:(s + 1) * 128, :]),
                    deps=deps, sig="lg", inc=16)
                deps = [t_gl] + ([t_mmg[s - 1]] if s >= 1 else [])
                t_gc[s] = p.do("vector", lambda e: e.tensor_scalar_add(
                    WGBF[:, :], WGU8[:, 0:8192], -128.0), deps=deps, sig="dve")
                deps = [t_uc[s - 1]] if s >= 1 else []
                t_ul = p.do("sync", lambda e, s=s: e.dma_start(
                    out=WGU8[:, 8192:16384], in_=wu_s[s * 128:(s + 1) * 128, :]),
                    deps=deps, sig="lu", inc=16)
                deps = [t_ul] + ([t_mmu[s - 1]] if s >= 1 else [])
                t_uc[s] = p.do("vector", lambda e: e.tensor_scalar_add(
                    WUBF[:, :], WGU8[:, 8192:16384], -128.0), deps=deps, sig="dve")
                for hc in range(4):
                    cc = s * 4 + hc
                    for gi in range(4):
                        kk = cc * 4 + gi
                        ppg = pb[gi % 2]
                        ppu = pb[2 + gi % 2]
                        deps = [t_gc[s], t_norm[NCH - 1]]
                        if kk >= 2:
                            deps += [t_h.get(kk - 2)]
                        for c in range(NCH):
                            tok = p.do("tensor", lambda e, hc=hc, gi=gi, c=c,
                                       ppg=ppg: e.matmul(
                                           ppg[:, :],
                                           WGBF[:, c * 512 + hc * 128:
                                                c * 512 + (hc + 1) * 128],
                                           XLNT[:, c * T + gi * 512:
                                                c * T + (gi + 1) * 512],
                                           start=(c == 0), stop=(c == NCH - 1),
                                           skip_group_check=True),
                                       deps=deps if c == 0 else [],
                                       sig="pe" if c == NCH - 1 else None)
                        t_mmg[s] = tok
                        t_silu = p.do("scalar", lambda e, cc=cc, gi=gi, ppg=ppg:
                                      e.activation(
                                          GT[:, (gi % 2) * 512:(gi % 2 + 1) * 512],
                                          ppg[:, :], AF.Silu, scale=sg[:, cc:cc + 1]),
                                      deps=[tok, t_scl], sig="act")
                        deps = [t_uc[s]]
                        for c in range(NCH):
                            tok = p.do("tensor", lambda e, hc=hc, gi=gi, c=c,
                                       ppu=ppu: e.matmul(
                                           ppu[:, :],
                                           WUBF[:, c * 512 + hc * 128:
                                                c * 512 + (hc + 1) * 128],
                                           XLNT[:, c * T + gi * 512:
                                                c * T + (gi + 1) * 512],
                                           start=(c == 0), stop=(c == NCH - 1),
                                           skip_group_check=True),
                                       deps=deps if c == 0 else [],
                                       sig="pe" if c == NCH - 1 else None)
                        t_mmu[s] = tok
                        pend[kk] = (tok, t_silu, cc, gi, ppu)
                        if kk >= 1:
                            emit_h(kk - 1)
                    if cc == 43:
                        emit_h(cc * 4 + 3)
                    if cc >= 1:
                        t_hst[cc - 1] = p.do("gpsimd", lambda e, cc=cc: e.dma_start(
                            out=h_dram[(cc - 1) * 128:cc * 128, :],
                            in_=HST[:, ((cc - 1) % 2) * 2048:
                                    ((cc - 1) % 2) * 2048 + 2048]),
                            deps=[t_h[(cc - 1) * 4 + 3]], sig=f"sh{(cc - 1) % 2}",
                            inc=16)
                    if cc == 43:
                        t_hst[43] = p.do("gpsimd", lambda e: e.dma_start(
                            out=h_dram[43 * 128:44 * 128, :],
                            in_=HST[:, (43 % 2) * 2048:(43 % 2) * 2048 + 2048]),
                            deps=[t_h[43 * 4 + 3]], sig="sh1", inc=16)
            h_done = [("sh0", counts["sh0"]), ("sh1", counts["sh1"])]
            p.run(nc)

        with contextlib.ExitStack() as sc:
            WDU8 = sc.enter_context(nc.sbuf_tensor("WDU8", [128, 2 * 5632], U8))
            WDBF = sc.enter_context(nc.sbuf_tensor("WDBF", [128, 22528], BF16))
            HT = sc.enter_context(nc.sbuf_tensor("HT", [128, 2 * 11264], BF16))
            X2CH = sc.enter_context(nc.sbuf_tensor("X2CH", [128, 2 * 2048], F32))
            OUTS = sc.enter_context(nc.sbuf_tensor("OUTS", [128, 2 * 2048], F32))
            pb = [sc.enter_context(nc.psum_tensor(f"pf{i}", [128, 512], F32))
                  for i in range(4)]

            p = Plan(sems, counts, marks, entry_guards=st_guard())
            t_out, t_ost, t_f, t_htld, t_x2ld, t_wdc = {}, {}, {}, {}, {}, {}
            t_lastread = {}
            for dg in range(4):
                for piece in range(4):
                    kk = dg * 4 + piece
                    deps = [t_scl]
                    if kk >= 2:
                        deps += [t_wdc.get(kk - 2)]
                    t_wl = p.do("sync", lambda e, dg=dg, piece=piece: e.dma_start(
                        out=WDU8[:, (piece % 2) * 5632:(piece % 2 + 1) * 5632],
                        in_=wd_s[dg * 128:(dg + 1) * 128,
                                 piece * 5632:(piece + 1) * 5632]),
                        deps=deps, sig=f"lwd{piece % 2}", inc=16)
                    deps = [t_wl]
                    if dg >= 1 and piece == 0:
                        deps += [t_f.get((dg * 4 - 1, 3))]
                    t_wdc[kk] = p.do("vector",
                                     lambda e, piece=piece: e.tensor_scalar_add(
                                         WDBF[:, piece * 5632:(piece + 1) * 5632],
                                         WDU8[:, (piece % 2) * 5632:
                                              (piece % 2 + 1) * 5632],
                                         -128.0), deps=deps, sig="dve")
                for tg in range(4):
                    it = dg * 4 + tg
                    for hb in range(2):
                        for j in range(6):
                            n = min(4, 22 - j * 4)
                            hc0 = hb * 22 + j * 4
                            deps = list(h_done)
                            if it >= 1:
                                deps += [t_lastread.get((it - 1, hb))]
                            t_htld[(it, hb)] = p.do(
                                "sync",
                                lambda e, hc0=hc0, n=n, hb=hb, j=j, tg=tg: e.dma_start(
                                    out=HT.rearrange("p (c t) -> p c t", t=512)
                                    [:, hb * 22 + j * 4:hb * 22 + j * 4 + n, :],
                                    in_=h_dram.rearrange("(c p) t -> p c t", p=128)
                                    [:, hc0:hc0 + n, tg * 512:(tg + 1) * 512]),
                                deps=deps if j == 0 else [], sig=f"lh{hb}", inc=16)
                    deps = []
                    if it >= 2:
                        deps += [t_out.get((it - 2, 3))]
                    t_x2ld[it] = p.do("sync", lambda e, dg=dg, tg=tg, it=it: e.dma_start(
                        out=X2CH.rearrange("p (c t) -> p c t", t=512)
                        [:, (it % 2) * 4:(it % 2) * 4 + 4, :],
                        in_=x2_dram.rearrange("(c p) t -> p c t", p=128)
                        [:, dg * 4:dg * 4 + 4, tg * 512:(tg + 1) * 512]),
                        deps=deps + x2_done, sig=f"l2{it % 2}", inc=16)
                    for hb in range(2):
                        for hc22 in range(22):
                            hc = hb * 22 + hc22
                            for dc in range(4):
                                deps = []
                                if hb == 0 and hc22 == 0:
                                    deps += [t_wdc[dg * 4 + 3]]
                                    if it >= 1:
                                        deps += [t_out.get((it - 1, dc))]
                                if hc22 == 0 and dc == 0:
                                    deps += [t_htld[(it, hb)]]
                                sig = None
                                if hc == 43 or (hc22 == 21 and dc == 3):
                                    sig = "pe"
                                tok = p.do("tensor",
                                           lambda e, hc=hc, dc=dc, hb=hb,
                                           hc22=hc22: e.matmul(
                                               pb[dc][:, :],
                                               WDBF[:, hc * 512 + dc * 128:
                                                    hc * 512 + (dc + 1) * 128],
                                               HT[:, hb * 11264 + hc22 * 512:
                                                  hb * 11264 + (hc22 + 1) * 512],
                                               start=(hc == 0), stop=(hc == 43),
                                               skip_group_check=True),
                                           deps=deps, sig=sig)
                                if hc22 == 21 and dc == 3:
                                    t_lastread[(it, hb)] = tok
                                if hc == 43:
                                    t_f[(it, dc)] = tok
                    for dc in range(4):
                        deps = [t_f[(it, dc)], t_x2ld[it], t_scl]
                        if it >= 2:
                            deps += [t_ost.get(it - 2)]
                        t_out[(it, dc)] = p.do("vector",
                                               lambda e, dg=dg, dc=dc, it=it:
                                               e.scalar_tensor_tensor(
                                                   OUTS[:, (it % 2) * 2048 + dc * 512:
                                                        (it % 2) * 2048 +
                                                        (dc + 1) * 512],
                                                   pb[dc][:, :],
                                                   sd[:, dg * 4 + dc:dg * 4 + dc + 1],
                                                   X2CH[:, (it % 2) * 2048 + dc * 512:
                                                        (it % 2) * 2048 +
                                                        (dc + 1) * 512],
                                                   op0=ALU.mult, op1=ALU.add),
                                               deps=deps, sig="dve")
                    t_ost[it] = p.do("gpsimd", lambda e, dg=dg, tg=tg, it=it: e.dma_start(
                        out=outT.rearrange("(c p) t -> p c t", p=128)
                        [:, dg * 4:dg * 4 + 4, tg * 512:(tg + 1) * 512],
                        in_=OUTS.rearrange("p (c t) -> p c t", t=512)
                        [:, (it % 2) * 4:(it % 2) * 4 + 4, :]),
                        deps=[t_out[(it, 3)]], sig=f"so{it % 2}", inc=16)
            p.do("gpsimd", lambda e: e.nop(),
                 deps=[("so0", counts["so0"]), ("so1", counts["so1"])])
            p.run(nc)

    return nc


_NC_CACHE = None


def kernel(L0_attn_x, L0_attn_w_qkv, L0_attn_w_o, L0_ffn_wq_gate, L0_ffn_s_gate,
           L0_ffn_wq_up, L0_ffn_s_up, L0_ffn_wq_down, L0_ffn_s_down):
    global _NC_CACHE
    x = np.asarray(L0_attn_x)
    w_qkv = np.ascontiguousarray(np.asarray(L0_attn_w_qkv), dtype=np.float32)
    w_o = np.ascontiguousarray(np.asarray(L0_attn_w_o), dtype=np.float32)

    wqkv_s = np.ascontiguousarray(
        w_qkv.reshape(16, 128, 12, 512).transpose(2, 1, 0, 3).reshape(12 * 128, 8192))
    wo_s = np.ascontiguousarray(
        w_o.reshape(16, 128, 4, 512).transpose(2, 1, 0, 3).reshape(4 * 128, 8192))

    def u8view(a):
        a = np.ascontiguousarray(np.asarray(a, dtype=np.int32))
        return a.view(np.uint8)[:, ::4]

    wg8 = u8view(L0_ffn_wq_gate)
    wu8 = u8view(L0_ffn_wq_up)
    wd8 = u8view(L0_ffn_wq_down)
    wg_s = np.ascontiguousarray(
        wg8.reshape(11, 512, 16, 128).transpose(0, 3, 2, 1).reshape(11 * 128, 8192))
    wu_s = np.ascontiguousarray(
        wu8.reshape(11, 512, 16, 128).transpose(0, 3, 2, 1).reshape(11 * 128, 8192))
    wd_s = np.ascontiguousarray(
        wd8.reshape(4, 512, 44, 128).transpose(0, 3, 2, 1).reshape(4 * 128, 22528))
    sg_s = np.ascontiguousarray(
        np.asarray(L0_ffn_s_gate, dtype=np.float32).reshape(44, 128).T)
    su_s = np.ascontiguousarray(
        np.asarray(L0_ffn_s_up, dtype=np.float32).reshape(44, 128).T)
    sd_s = np.ascontiguousarray(
        np.asarray(L0_ffn_s_down, dtype=np.float32).reshape(16, 128).T)

    in_maps = []
    for c in range(8):
        b, hf = c // 2, c % 2
        own = x[b, hf * T:(hf + 1) * T]
        par = x[b, (1 - hf) * T:(2 - hf) * T]
        xT_c = np.ascontiguousarray(
            np.concatenate([own, par], axis=0).T.astype(np.float32, copy=False))
        in_maps.append({
            "xT": xT_c, "wqkv_s": wqkv_s, "wo_s": wo_s,
            "wg_s": wg_s, "wu_s": wu_s, "wd_s": wd_s,
            "sg_s": sg_s, "su_s": su_s, "sd_s": sd_s,
        })

    if _NC_CACHE is None:
        _NC_CACHE = build_program()
    res = run_bass_kernel_spmd(_NC_CACHE, in_maps, core_ids=list(range(8)))

    out = np.empty((4, S, D), dtype=np.float32)
    for c in range(8):
        b, hf = c // 2, c % 2
        out[b, hf * T:(hf + 1) * T, :] = res.results[c]["outT"].T
    return out


# version 03
# speedup vs baseline: 1.2166x; 1.2166x over previous
import contextlib
import numpy as np
import concourse.bass as bass
import concourse.mybir as mybir
from concourse.bass_utils import run_bass_kernel_spmd

F32 = mybir.dt.float32
BF16 = mybir.dt.bfloat16
U8 = mybir.dt.uint8
AF = mybir.ActivationFunctionType
ALU = mybir.AluOpType
AXX = mybir.AxisListType.X

D = 2048
S = 4096
T = 2048
HID = 5632
NH = 16
NCH = D // 128
EXPS = float(1.0 / np.sqrt(128.0))
EXPB = -12.0
EPS = 1e-5

ENGS = ("sync", "vector", "scalar", "tensor", "gpsimd")


class Plan:

    def __init__(self, sems, counts, marks, entry_guards=()):
        self.sems = sems
        self.counts = counts
        self.marks = marks
        self.ev = {e: [] for e in ENGS}
        self.entry = {e: list(entry_guards) for e in ENGS}

    def do(self, eng, fn, deps=(), sig=None, inc=1):
        evl = self.ev[eng]
        mk = self.marks[eng]
        alldeps = self.entry[eng] + list(deps)
        self.entry[eng] = []
        order = {"vector": "dve", "scalar": "act"}.get(eng)
        if eng == "gpsimd" and inc != 16:
            order = "pool"
        if order is not None:
            alldeps = alldeps + [(order, self.counts.get(order, 0))]
            if sig is None:
                sig = order
            assert sig == order, (eng, sig)
        best = {}
        for d in alldeps:
            if d is None:
                continue
            s, v = d
            if v > best.get(s, 0):
                best[s] = v
        for s, v in best.items():
            if v > 0 and mk.get(s, 0) < v:
                evl.append(("w", s, v))
                mk[s] = v
        tok = None
        if sig is not None:
            self.counts[sig] = self.counts.get(sig, 0) + inc
            tok = (sig, self.counts[sig])
        evl.append(("d", fn, sig, inc))
        return tok

    def run(self, nc):
        with nc.Block() as blk:
            for eng in ENGS:
                items = self.ev[eng]
                if not items:
                    continue

                def closure(e, items=items):
                    for it in items:
                        if it[0] == "w":
                            e.wait_ge(self.sems[it[1]], it[2])
                        else:
                            ins = it[1](e)
                            if it[2] is not None:
                                ins.then_inc(self.sems[it[2]], it[3])

                getattr(blk, eng)(closure)


def build_program(debug=False):
    nc = bass.Bass("TRN2", target_bir_lowering=False)
    KIND = "ExternalOutput" if debug else "Internal"

    xT = nc.dram_tensor("xT", [D, S], F32, kind="ExternalInput").ap()
    wqkv_s = nc.dram_tensor("wqkv_s", [12 * 128, 8192], F32, kind="ExternalInput").ap()
    wo_s = nc.dram_tensor("wo_s", [4 * 128, 8192], F32, kind="ExternalInput").ap()
    wg_s = nc.dram_tensor("wg_s", [11 * 128, 8192], U8, kind="ExternalInput").ap()
    wu_s = nc.dram_tensor("wu_s", [11 * 128, 8192], U8, kind="ExternalInput").ap()
    wd_s = nc.dram_tensor("wd_s", [4 * 128, 22528], U8, kind="ExternalInput").ap()
    sg_s = nc.dram_tensor("sg_s", [128, 44], F32, kind="ExternalInput").ap()
    su_s = nc.dram_tensor("su_s", [128, 44], F32, kind="ExternalInput").ap()
    sd_s = nc.dram_tensor("sd_s", [128, 16], F32, kind="ExternalInput").ap()

    q_dram = nc.dram_tensor("q_dram", [D, S], BF16, kind=KIND).ap()
    k_dram = nc.dram_tensor("k_dram", [D, S], BF16, kind=KIND).ap()
    v_dram = nc.dram_tensor("v_dram", [D, T], BF16, kind=KIND).ap()
    x2_dram = nc.dram_tensor("x2_dram", [D, T], F32, kind=KIND).ap()
    h_dram = nc.dram_tensor("h_dram", [HID, T], BF16, kind=KIND).ap()
    csum_dram = nc.dram_tensor("csum_dram", [NH, T], F32, kind=KIND).ap()

    outT = nc.dram_tensor("outT", [D, T], F32, kind="ExternalOutput").ap()
    wobf_dbg = (nc.dram_tensor("wobf_dbg", [128, 4 * 8192], BF16, kind="ExternalOutput").ap()
                if debug else None)
    wdbg = (nc.dram_tensor("wdbg", [128, 32], F32, kind="ExternalOutput").ap()
            if debug else None)
    valt_dbg = (nc.dram_tensor("valt_dbg", [128, 8192], BF16, kind="ExternalOutput").ap()
                if debug else None)

    counts = {}
    marks = {e: {} for e in ENGS}

    def st_guard():
        return [(s, counts.get(s, 0)) for s in
                ("sq0", "sq1", "sc", "s20", "s21", "sh0", "sh1", "so0", "so1")]

    with contextlib.ExitStack() as g:
        semnames = ("pe", "act", "dve", "pool", "ldi", "lx0", "lx1", "lw",
                    "sq0", "sq1", "lq0", "lq1", "sc", "lv0", "lv1", "lp0",
                    "lp1", "lc0", "lc1", "s20", "s21", "lg", "lu", "sh0",
                    "sh1", "lwd0", "lwd1", "lh0", "lh1", "l20", "l21",
                    "so0", "so1")
        sems = {n: g.enter_context(nc.semaphore(n)) for n in semnames}
        STORE_SEMS = ("sq0", "sq1", "sc", "s20", "s21", "sh0", "sh1",
                      "so0", "so1")

        ones_col = g.enter_context(nc.sbuf_tensor("ones_col", [128, 1], BF16))
        ones_row = g.enter_context(nc.sbuf_tensor("ones_row", [1, 128], F32))
        consts = g.enter_context(nc.sbuf_tensor("consts", [128, 4], F32))
        sg = g.enter_context(nc.sbuf_tensor("sg", [128, 44], F32))
        su = g.enter_context(nc.sbuf_tensor("su", [128, 44], F32))
        sd = g.enter_context(nc.sbuf_tensor("sd", [128, 16], F32))
        srow = g.enter_context(nc.sbuf_tensor("srow", [1, T], F32))
        s2row = g.enter_context(nc.sbuf_tensor("s2row", [1, T], F32))
        mrow = g.enter_context(nc.sbuf_tensor("mrow", [1, T], F32))
        rstdrow = g.enter_context(nc.sbuf_tensor("rstdrow", [1, T], F32))
        mrstdrow = g.enter_context(nc.sbuf_tensor("mrstdrow", [1, T], F32))
        cstage = g.enter_context(nc.sbuf_tensor("cstage", [1, T], F32))
        vrow = cstage
        csrow = g.enter_context(nc.sbuf_tensor("csrow", [1, 2 * 512], F32))
        dparts = g.enter_context(nc.sbuf_tensor("dparts", [128, 16], F32))
        dstage = g.enter_context(nc.sbuf_tensor("dstage", [128, 16], F32))
        actscr = g.enter_context(nc.sbuf_tensor("actscr", [128, 4], F32))
        denb = g.enter_context(nc.sbuf_tensor("denb", [128, 2], F32))
        povn = g.enter_context(nc.sbuf_tensor("povn", [128, 2], F32))
        pscr = g.enter_context(nc.sbuf_tensor("pscr", [128, 2], F32))
        wcolf = g.enter_context(nc.sbuf_tensor("wcolf", [128, 2], F32))
        wcolb = g.enter_context(nc.sbuf_tensor("wcolb", [128, 2], BF16))

        CEXPB, CEXPS, CEPS, CNHALF = 0, 1, 2, 3

        p = Plan(sems, counts, marks)
        t_ones = p.do("vector", lambda e: e.memset(ones_col[:], 1.0))
        p.do("vector", lambda e: e.memset(ones_row[:], 1.0))
        p.do("vector", lambda e: e.memset(consts[:, CEXPB:CEXPB + 1], EXPB))
        p.do("vector", lambda e: e.memset(consts[:, CEXPS:CEXPS + 1], EXPS))
        p.do("vector", lambda e: e.memset(consts[:, CEPS:CEPS + 1], EPS))
        t_cn = p.do("vector", lambda e: e.memset(consts[:, CNHALF:CNHALF + 1], -0.5),
                    sig="dve")
        p.do("sync", lambda e: e.dma_start(out=sg[:], in_=sg_s), sig="ldi", inc=16)
        p.do("sync", lambda e: e.dma_start(out=su[:], in_=su_s), sig="ldi", inc=16)
        t_scl = p.do("sync", lambda e: e.dma_start(out=sd[:], in_=sd_s), sig="ldi", inc=16)
        p.run(nc)

        def ln_rows(p, t_sum_done, t_sq_done, pb_s, pb_q):
            t_rows = []
            for gi in range(4):
                t_rows.append(p.do("vector", lambda e, gi=gi: e.tensor_copy(
                    out=srow[0:1, gi * 512:(gi + 1) * 512], in_=pb_s[gi][0:1, :]),
                    deps=[t_sum_done], sig="dve"))
                t_rows.append(p.do("vector", lambda e, gi=gi: e.tensor_copy(
                    out=s2row[0:1, gi * 512:(gi + 1) * 512], in_=pb_q[gi][0:1, :]),
                    deps=[t_sq_done], sig="dve"))
            p.do("vector", lambda e: e.tensor_scalar_mul(
                mrow[0:1, :], srow[0:1, :], 1.0 / D), deps=[t_rows[-1]])
            p.do("vector", lambda e: e.tensor_scalar_mul(
                vrow[0:1, :], s2row[0:1, :], 1.0 / D))
            p.do("vector", lambda e: e.tensor_tensor(
                srow[0:1, :], mrow[0:1, :], mrow[0:1, :], op=ALU.mult))
            t_var = p.do("vector", lambda e: e.tensor_tensor(
                vrow[0:1, :], vrow[0:1, :], srow[0:1, :], op=ALU.subtract),
                sig="dve")
            t_ln = p.do("scalar", lambda e: e.activation(
                s2row[0:1, :], vrow[0:1, :], AF.Ln,
                bias=consts[0:1, CEPS:CEPS + 1]), deps=[t_var], sig="act")
            t_rstd = p.do("scalar", lambda e: e.activation(
                rstdrow[0:1, :], s2row[0:1, :], AF.Exp,
                scale=consts[0:1, CNHALF:CNHALF + 1]), deps=[t_ln], sig="act")
            t_mrstd = p.do("vector", lambda e: e.tensor_tensor(
                mrstdrow[0:1, :], mrow[0:1, :], rstdrow[0:1, :], op=ALU.mult),
                deps=[t_rstd], sig="dve")
            return t_rows[-1], t_rstd, t_mrstd

        def ln_bcasts(p, t_rstd, t_mrstd, t_banks_free, pb):
            t_bc = []
            for gi in range(4):
                tm = p.do("tensor", lambda e, gi=gi: e.matmul(
                    pb[2 * gi][:, :], ones_row[:],
                    rstdrow[0:1, gi * 512:(gi + 1) * 512], start=True, stop=True),
                    deps=[t_rstd, t_banks_free], sig="pe")
                t_bc.append(tm)
                tm = p.do("tensor", lambda e, gi=gi: e.matmul(
                    pb[2 * gi + 1][:, :], ones_row[:],
                    mrstdrow[0:1, gi * 512:(gi + 1) * 512], start=True, stop=True),
                    deps=[t_mrstd], sig="pe")
                t_bc.append(tm)
            return t_bc[-1]

        qkv_last_store = [None]

        def phase_ab(half):
            with contextlib.ExitStack() as sc:
                XLNT = sc.enter_context(nc.sbuf_tensor(f"XLNT{half}", [128, NCH * T], BF16))
                XCH = sc.enter_context(nc.sbuf_tensor(f"XCH{half}", [128, 2 * T], F32))
                NRM = sc.enter_context(nc.sbuf_tensor(f"NRM{half}", [128, 512], F32))
                WF32 = sc.enter_context(nc.sbuf_tensor(f"WF32{half}", [128, 4096], F32))
                WBF = sc.enter_context(nc.sbuf_tensor(f"WBF{half}", [128, 2 * 8192], BF16))
                QSTG = sc.enter_context(nc.sbuf_tensor(f"QSTG{half}", [128, 2 * 2048], BF16))
                pb = [sc.enter_context(nc.psum_tensor(f"pb{half}_{i}", [128, 512], F32))
                      for i in range(8)]
                XBF = lambda par: WBF[:, par * 2048:(par + 1) * 2048]
                SQB = lambda par: WBF[:, 4096 + par * 2048: 4096 + (par + 1) * 2048]

                p = Plan(sems, counts, marks, entry_guards=st_guard())
                cols = slice(half * T, (half + 1) * T)

                t_ld, t_cast, t_sq, t_summ, t_sqm = {}, {}, {}, {}, {}
                for i in range(NCH):
                    par = i % 2
                    po, pe_ = par * T, (par + 1) * T
                    deps = [t_ones, t_cn]
                    if i >= 2:
                        deps += [t_cast[i - 2]]
                    t_ld[i] = p.do("sync", lambda e, i=i, po=po, pe_=pe_: e.dma_start(
                        out=XCH[:, po:pe_], in_=xT[i * 128:(i + 1) * 128, cols]),
                        deps=deps, sig=f"lx{i % 2}", inc=16)
                    deps = [t_ld[i]]
                    if i >= 2:
                        deps += [t_summ[i - 2], t_sq[i - 2]]
                    t_cast[i] = p.do("vector", lambda e, par=par, po=po, pe_=pe_:
                                     e.tensor_copy(out=XBF(par), in_=XCH[:, po:pe_]),
                                     deps=deps, sig="dve")
                    deps = [t_cast[i]]
                    if i >= 2:
                        deps += [t_sqm[i - 2]]
                    t_sq[i] = p.do("gpsimd", lambda e, par=par: e.tensor_tensor(
                        out=SQB(par), in0=XBF(par), in1=XBF(par), op=ALU.mult),
                        deps=deps, sig="pool")
                    for gi in range(4):
                        tok = p.do("tensor", lambda e, i=i, par=par, gi=gi: e.matmul(
                            pb[gi][0:1, :], ones_col[:],
                            XBF(par)[:, gi * 512:(gi + 1) * 512],
                            start=(i == 0), stop=(i == NCH - 1),
                            skip_group_check=True),
                            deps=[t_cast[i]] if gi == 0 else [],
                            sig="pe" if gi == 3 else None)
                    t_summ[i] = tok
                    for gi in range(4):
                        tok = p.do("tensor", lambda e, i=i, par=par, gi=gi: e.matmul(
                            pb[4 + gi][0:1, :], ones_col[:],
                            SQB(par)[:, gi * 512:(gi + 1) * 512],
                            start=(i == 0), stop=(i == NCH - 1),
                            skip_group_check=True),
                            deps=[t_sq[i]] if gi == 0 else [],
                            sig="pe" if gi == 3 else None)
                    t_sqm[i] = tok

                t_rows_done, t_rstd, t_mrstd = ln_rows(
                    p, t_summ[NCH - 1], t_sqm[NCH - 1], pb[0:4], pb[4:8])
                t_bc_done = ln_bcasts(p, t_rstd, t_mrstd, t_rows_done, pb)

                t_norm = {}
                for i in range(NCH):
                    par = i % 2
                    po, pe_ = par * T, (par + 1) * T
                    deps = [t_bc_done]
                    if i >= 2:
                        deps += [t_norm[i - 2]]
                    t_l2 = p.do("sync", lambda e, i=i, po=po, pe_=pe_: e.dma_start(
                        out=XCH[:, po:pe_], in_=xT[i * 128:(i + 1) * 128, cols]),
                        deps=deps, sig=f"lx{i % 2}", inc=16)
                    for g4 in range(4):
                        p.do("vector", lambda e, po=po, g4=g4: e.tensor_tensor(
                            NRM[:, :], XCH[:, po + g4 * 512:po + (g4 + 1) * 512],
                            pb[2 * g4][:, :], op=ALU.mult),
                            deps=[t_l2] if g4 == 0 else [])
                        tok = p.do("vector", lambda e, i=i, g4=g4: e.tensor_tensor(
                            XLNT[:, i * T + g4 * 512:i * T + (g4 + 1) * 512],
                            NRM[:, :], pb[2 * g4 + 1][:, :], op=ALU.subtract),
                            sig="dve" if g4 == 3 else None)
                    t_norm[i] = tok

                slabs = list(range(12)) if half == 0 else list(range(8))
                t_wcast, t_mmlast, t_ev, t_st = {}, {}, {}, {}
                for si, s in enumerate(slabs):
                    wslot = si % 2
                    for piece in range(2):
                        deps = []
                        if si == 0 and piece == 0:
                            deps += [t_summ[NCH - 1], t_sqm[NCH - 1]]
                        if si > 0 or piece > 0:
                            deps += [t_wcast.get((si, piece - 1),
                                                 t_wcast.get((si - 1, 1)))]
                        t_wl = p.do("sync", lambda e, s=s, piece=piece: e.dma_start(
                            out=WF32[:, :],
                            in_=wqkv_s[s * 128:(s + 1) * 128,
                                       piece * 4096:(piece + 1) * 4096]),
                            deps=deps, sig="lw", inc=16)
                        deps = [t_wl]
                        if si >= 2 and piece == 0:
                            deps += [t_mmlast[slabs[si - 2]]]
                        t_wcast[(si, piece)] = p.do(
                            "vector", lambda e, wslot=wslot, piece=piece: e.tensor_copy(
                                out=WBF[:, wslot * 8192 + piece * 4096:
                                        wslot * 8192 + (piece + 1) * 4096],
                                in_=WF32[:, :]),
                            deps=deps, sig="dve")
                    for fc in range(4):
                        rowi = si * 4 + fc
                        stg = rowi % 2
                        for tg in range(4):
                            key = rowi * 4 + tg
                            pp = pb[2 + key % 2]
                            deps = [t_wcast[(si, 1)], t_norm[NCH - 1]]
                            if key >= 2:
                                deps += [t_ev.get(key - 2)]
                            for c in range(NCH):
                                tok = p.do("tensor",
                                           lambda e, wslot=wslot, fc=fc, tg=tg, c=c,
                                           pp=pp: e.matmul(
                                               pp[:, :],
                                               WBF[:, wslot * 8192 + c * 512 + fc * 128:
                                                   wslot * 8192 + c * 512 + (fc + 1) * 128],
                                               XLNT[:, c * T + tg * 512:
                                                    c * T + (tg + 1) * 512],
                                               start=(c == 0), stop=(c == NCH - 1),
                                               skip_group_check=True),
                                           deps=deps if c == 0 else [],
                                           sig="pe" if c == NCH - 1 else None)
                            t_mm = tok
                            if fc == 3 and tg == 3:
                                t_mmlast[s] = t_mm
                            deps = [t_mm]
                            if rowi >= 2:
                                deps += [t_st.get(rowi - 2)]
                            t_ev[key] = p.do("scalar",
                                             lambda e, stg=stg, tg=tg, pp=pp: e.copy(
                                                 out=QSTG[:, stg * 2048 + tg * 512:
                                                          stg * 2048 + (tg + 1) * 512],
                                                 in_=pp[:, :]),
                                             deps=deps, sig="act")
                        if s < 4:
                            dst, r0, c0 = q_dram, s * 512 + fc * 128, half * T
                        elif s < 8:
                            dst, r0, c0 = k_dram, (s - 4) * 512 + fc * 128, half * T
                        else:
                            dst, r0, c0 = v_dram, (s - 8) * 512 + fc * 128, 0
                        t_st[rowi] = p.do("gpsimd",
                                          lambda e, dst=dst, r0=r0, c0=c0,
                                          stg=stg: e.dma_start(
                                              out=dst[r0:r0 + 128, c0:c0 + T],
                                              in_=QSTG[:, stg * 2048:stg * 2048 + 2048]),
                                          deps=[t_ev[rowi * 4 + 3]],
                                          sig=f"sq{rowi % 2}", inc=16)
                qkv_last_store[0] = [("sq0", counts["sq0"]), ("sq1", counts["sq1"])]
                p.run(nc)

        phase_ab(0)
        phase_ab(1)
        qkv_done = qkv_last_store[0]

        t_csum_done = None
        with contextlib.ExitStack() as sc:
            QH = sc.enter_context(nc.sbuf_tensor("QH", [128, 2 * S], BF16))
            KH = sc.enter_context(nc.sbuf_tensor("KH", [128, 2 * S], BF16))
            EOWN = sc.enter_context(nc.sbuf_tensor("EOWN", [128, 4 * T], BF16))
            ESCR = sc.enter_context(nc.sbuf_tensor("ESCR", [128, 2 * T], BF16))
            WDB = (sc.enter_context(nc.sbuf_tensor("WDB", [128, 32], F32))
                   if debug else None)
            sq = [sc.enter_context(nc.psum_tensor(f"sq{i}", [128, 1024], F32))
                  for i in range(3)]
            cs_ps = sc.enter_context(nc.psum_tensor("cs_ps", [128, 512], F32))

            p = Plan(sems, counts, marks, entry_guards=st_guard())
            t_qld, t_kld, t_head, t_csev, t_exp, t_w, t_cs = {}, {}, {}, {}, {}, {}, {}
            t_pr, t_dr, t_tail, t_wtail = {}, {}, {}, {}
            qseq = 0

            def emit_colsum(h, qc):
                gqc = h * 32 + qc
                deps = [t_w[gqc]]
                if qc == 0:
                    deps += [t_csev.get(h - 1)]
                for kc in range(4):
                    tok = p.do("tensor", lambda e, h=h, qc=qc, kc=kc: e.matmul(
                        cs_ps[32 * kc:32 * kc + 1, :],
                        wcolb[:, qc % 2:qc % 2 + 1],
                        EOWN[:, (qc % 4) * T + kc * 512:(qc % 4) * T + (kc + 1) * 512],
                        start=(qc == 0), stop=(qc == 31),
                        tile_position=(0, 32 * kc), skip_group_check=True),
                        deps=deps if kc == 0 else [],
                        sig="pe" if kc == 3 else None)
                t_cs[gqc] = tok

            for h in range(NH):
                hs = (h % 2) * S
                deps = list(qkv_done)
                if h >= 2:
                    deps += [t_head[h - 2]]
                t_qld[h] = p.do("sync", lambda e, h=h, hs=hs: e.dma_start(
                    out=QH[:, hs:hs + S], in_=q_dram[h * 128:(h + 1) * 128, :]),
                    deps=deps, sig=f"lq{h % 2}", inc=16)
                t_kld[h] = p.do("sync", lambda e, h=h, hs=hs: e.dma_start(
                    out=KH[:, hs:hs + S], in_=k_dram[h * 128:(h + 1) * 128, :]),
                    sig=f"lq{h % 2}", inc=16)

                def emit_denw(qc):
                    gqc = h * 32 + qc
                    e_tok = t_exp[(gqc + 1) * 4 - 1]
                    t_dcp = p.do("scalar", lambda e, qc=qc: e.copy(
                        out=dstage[:, (qc % 4) * 4:(qc % 4) * 4 + 4],
                        in_=dparts[:, (qc % 4) * 4:(qc % 4) * 4 + 4]),
                        deps=[e_tok] +
                        ([t_w.get(gqc - 2)] if gqc >= 2 else []), sig="act")
                    p.do("vector", lambda e, qc=qc: e.reduce_sum(
                        denb[:, qc % 2:qc % 2 + 1],
                        dstage[:, (qc % 4) * 4:(qc % 4) * 4 + 4], axis=AXX),
                        deps=[t_dcp])
                    p.do("vector", lambda e, qc=qc: e.reciprocal(
                        wcolf[:, qc % 2:qc % 2 + 1], denb[:, qc % 2:qc % 2 + 1]))
                    t_w[gqc] = p.do("vector", lambda e, qc=qc: e.tensor_copy(
                        out=wcolb[:, qc % 2:qc % 2 + 1],
                        in_=wcolf[:, qc % 2:qc % 2 + 1]),
                        deps=[t_cs.get(gqc - 2)] if gqc >= 2 else [], sig="dve")
                    if debug and h == 0:
                        p.do("vector", lambda e, qc=qc: e.tensor_copy(
                            out=WDB[:, qc:qc + 1],
                            in_=wcolf[:, qc % 2:qc % 2 + 1]))

                for qc in range(32):
                    gqc = h * 32 + qc
                    for q4 in range(4):
                        buf = sq[qseq % 3]
                        deps = [t_qld[h], t_kld[h]]
                        if qseq >= 3:
                            deps += [t_exp[qseq - 3]]
                        p.do("tensor", lambda e, hs=hs, qc=qc, q4=q4, buf=buf: e.matmul(
                            buf[:, 0:512],
                            QH[:, hs + qc * 128:hs + (qc + 1) * 128],
                            KH[:, hs + q4 * 1024:hs + q4 * 1024 + 512],
                            start=True, stop=True),
                            deps=deps)
                        t_sc = p.do("tensor", lambda e, hs=hs, qc=qc, q4=q4,
                                    buf=buf: e.matmul(
                                        buf[:, 512:1024],
                                        QH[:, hs + qc * 128:hs + (qc + 1) * 128],
                                        KH[:, hs + q4 * 1024 + 512:
                                           hs + (q4 + 1) * 1024],
                                        start=True, stop=True),
                                    sig="pe")
                        if q4 < 2:
                            dtile, dsl = EOWN, ((qc % 4) * T + q4 * 1024,
                                                (qc % 4) * T + (q4 + 1) * 1024)
                        else:
                            dtile, dsl = ESCR, ((qc % 2) * T + (q4 - 2) * 1024,
                                                (qc % 2) * T + (q4 - 1) * 1024)
                        deps = [t_sc]
                        if q4 == 0 and gqc >= 4:
                            deps += [t_cs.get(gqc - 4)]
                        t_exp[qseq] = p.do("scalar",
                                           lambda e, dtile=dtile, dsl=dsl, qc=qc,
                                           q4=q4, buf=buf: e.activation(
                                               dtile[:, dsl[0]:dsl[1]], buf[:, :],
                                               AF.Exp,
                                               bias=consts[:, CEXPB:CEXPB + 1],
                                               scale=consts[:, CEXPS:CEXPS + 1],
                                               accum_out=dparts[:, (qc % 4) * 4 + q4:
                                                                (qc % 4) * 4 + q4 + 1]),
                                           deps=deps, sig="act")
                        qseq += 1
                    if qc >= 1:
                        emit_denw(qc - 1)
                    if qc >= 2:
                        emit_colsum(h, qc - 2)
                emit_denw(31)
                emit_colsum(h, 30)
                emit_colsum(h, 31)
                t_head[h] = t_cs[h * 32 + 31]
                t_cp = {}
                for kc in range(4):
                    t_cp[kc] = p.do("vector", lambda e, kc=kc: e.tensor_copy(
                        out=cstage[0:1, kc * 512:(kc + 1) * 512],
                        in_=cs_ps[32 * kc:32 * kc + 1, :]),
                        deps=([t_head[h]] if kc == 0 else []) +
                        ([t_csev.get(h - 1)] if kc == 0 else []),
                        sig="dve" if kc == 3 else None)
                t_csev[h] = p.do("gpsimd", lambda e, h=h: e.dma_start(
                    out=csum_dram[h:h + 1, :], in_=cstage[0:1, :]),
                    deps=[t_cp[3]], sig="sc", inc=16)
            if debug:
                p.do("gpsimd", lambda e: e.dma_start(out=wdbg, in_=WDB[:, :]),
                     deps=[t_w[31]], sig="sc", inc=16)
            t_csum_done = t_csev[NH - 1]
            p.run(nc)

        with contextlib.ExitStack() as sc:
            WF32 = sc.enter_context(nc.sbuf_tensor("WF32d", [128, 2048], F32))
            WOBF = sc.enter_context(nc.sbuf_tensor("WOBF", [128, 4 * 8192], BF16))
            VAL = sc.enter_context(nc.sbuf_tensor("VAL", [128, 8192], BF16))
            VALT = sc.enter_context(nc.sbuf_tensor("VALT", [128, 8192], BF16))
            CBB = sc.enter_context(nc.sbuf_tensor("CBB", [128, 1024], BF16))
            XCHd = sc.enter_context(nc.sbuf_tensor("XCHd", [128, 2 * 2048], F32))
            X2ST = sc.enter_context(nc.sbuf_tensor("X2ST", [128, 2 * 2048], F32))
            X2BF = sc.enter_context(nc.sbuf_tensor("X2BF", [128, 2 * 512], BF16))
            X2SQ = sc.enter_context(nc.sbuf_tensor("X2SQ", [128, 2 * 512], BF16))
            pb = [sc.enter_context(nc.psum_tensor(f"pd{i}", [128, 512], F32))
                  for i in range(6)]

            p = Plan(sems, counts, marks, entry_guards=st_guard())
            t_wc = None
            for s in range(4):
                for piece in range(4):
                    deps = [t_wc] if t_wc else []
                    t_wl = p.do("sync", lambda e, s=s, piece=piece: e.dma_start(
                        out=WF32[:, :],
                        in_=wo_s[s * 128:(s + 1) * 128,
                                 piece * 2048:(piece + 1) * 2048]),
                        deps=deps, sig="lw", inc=16)
                    t_wc = p.do("vector", lambda e, s=s, piece=piece: e.tensor_copy(
                        out=WOBF[:, s * 8192 + piece * 2048:
                                s * 8192 + (piece + 1) * 2048],
                        in_=WF32[:, :]), deps=[t_wl], sig="dve")

            t_vld, t_val, t_x2, t_x2st, t_stat, t_bfd = {}, {}, {}, {}, {}, {}
            t_crl, t_bm = {}, {}
            t_xld = {}
            for gi in range(4):
                for h8 in range(2):
                    deps = list(qkv_done)
                    if gi >= 1:
                        deps += [t_val.get((gi - 1, h8 * 8 + 7))]
                    t_vld[(gi, h8)] = p.do("sync", lambda e, gi=gi, h8=h8: e.dma_start(
                        out=VAL.rearrange("p (c t) -> p c t", t=512)
                        [:, h8 * 8:(h8 + 1) * 8, :],
                        in_=v_dram.rearrange("(c p) t -> p c t", p=128)
                        [:, h8 * 8:(h8 + 1) * 8, gi * 512:(gi + 1) * 512]),
                        deps=deps, sig=f"lv{h8}", inc=16)
                def emit_xld(pc, gi=gi):
                    kk = gi * 4 + pc
                    deps = []
                    if kk >= 2:
                        kp = kk - 2
                        d = t_x2.get((kp // 4, (kp % 4) * 4 + 3))
                        assert d is not None
                        deps += [d]
                    t_xld[kk] = p.do("sync", lambda e, gi=gi, pc=pc: e.dma_start(
                        out=XCHd.rearrange("p (c t) -> p c t", t=512)
                        [:, (pc % 2) * 4:(pc % 2) * 4 + 4, :],
                        in_=xT.rearrange("(c p) t -> p c t", p=128)
                        [:, pc * 4:pc * 4 + 4, gi * 512:(gi + 1) * 512]),
                        deps=deps, sig=f"lp{pc % 2}", inc=16)
                for pc in range(2):
                    emit_xld(pc)
                t_cbd = {}

                def emit_val(h, gi=gi):
                    t_val[(gi, h)] = p.do("vector",
                                          lambda e, h=h: e.tensor_tensor(
                                              VALT[:, h * 512:(h + 1) * 512],
                                              VAL[:, h * 512:(h + 1) * 512],
                                              CBB[:, (h % 2) * 512:(h % 2 + 1) * 512],
                                              op=ALU.mult),
                                          deps=[t_cbd[h], t_cbd.get(h + 1),
                                                t_vld[(gi, h // 8)]], sig="dve")

                for h in range(NH):
                    deps = [t_csum_done]
                    if h >= 2:
                        deps += [t_bm.get((gi, h - 2))]
                    elif gi >= 1:
                        deps += [t_bm.get((gi - 1, NH - 2 + h))]
                    t_crl[(gi, h)] = p.do("sync", lambda e, gi=gi, h=h: e.dma_start(
                        out=csrow[0:1, (h % 2) * 512:(h % 2 + 1) * 512],
                        in_=csum_dram[h:h + 1, gi * 512:(gi + 1) * 512]),
                        deps=deps, sig=f"lc{h % 2}", inc=16)
                    deps = [t_crl[(gi, h)], t_wc]
                    if h >= 2:
                        deps += [t_val.get((gi, h - 2))]
                    elif gi >= 1:
                        deps += [t_val.get((gi - 1, NH - 2 + h))]
                    t_bm[(gi, h)] = p.do("tensor", lambda e, h=h: e.matmul(
                        pb[h % 2][:, :], ones_row[:],
                        csrow[0:1, (h % 2) * 512:(h % 2 + 1) * 512],
                        start=True, stop=True), deps=deps, sig="pe")
                    t_cbd[h] = p.do("scalar", lambda e, h=h: e.copy(
                        out=CBB[:, (h % 2) * 512:(h % 2 + 1) * 512],
                        in_=pb[h % 2][:, :]), deps=[t_bm[(gi, h)]], sig="act")
                    if h >= 1:
                        emit_val(h - 1)
                emit_val(NH - 1)
                for oc in range(NCH):
                    k = gi * NCH + oc
                    pp = pb[2 + oc % 2]
                    deps = [t_val[(gi, NH - 1)]]
                    if k >= 2:
                        deps += [t_x2.get(((k - 2) // NCH, (k - 2) % NCH))]
                    for c in range(NCH):
                        tok = p.do("tensor", lambda e, oc=oc, c=c, pp=pp: e.matmul(
                            pp[:, :],
                            WOBF[:, (oc // 4) * 8192 + c * 512 + (oc % 4) * 128:
                                 (oc // 4) * 8192 + c * 512 + (oc % 4) * 128 + 128],
                            VALT[:, c * 512:(c + 1) * 512],
                            start=(c == 0), stop=(c == NCH - 1),
                            skip_group_check=True),
                            deps=deps if c == 0 else [],
                            sig="pe" if c == NCH - 1 else None)
                    t_amm = tok
                    slot4 = (oc // 4) % 2
                    deps = [t_amm, t_xld[gi * 4 + oc // 4]]
                    if k >= 8:
                        deps += [t_x2st.get(((k - 8) // NCH, ((k - 8) % NCH) // 4))]
                    t_x2[(gi, oc)] = p.do("vector",
                                          lambda e, oc=oc, pp=pp, slot4=slot4:
                                          e.tensor_tensor(
                                              X2ST[:, slot4 * 2048 + (oc % 4) * 512:
                                                   slot4 * 2048 + (oc % 4 + 1) * 512],
                                              XCHd[:, ((oc // 4) % 2) * 2048 +
                                                   (oc % 4) * 512:
                                                   ((oc // 4) % 2) * 2048 +
                                                   (oc % 4 + 1) * 512],
                                              pp[:, :], op=ALU.add),
                                          deps=deps, sig="dve")
                    deps = [t_x2[(gi, oc)]]
                    if k >= 2:
                        deps += [t_stat.get(k - 2)]
                    t_bfd[k] = p.do("vector", lambda e, oc=oc, slot4=slot4:
                                    e.tensor_copy(
                                        out=X2BF[:, (oc % 2) * 512:(oc % 2 + 1) * 512],
                                        in_=X2ST[:, slot4 * 2048 + (oc % 4) * 512:
                                                 slot4 * 2048 + (oc % 4 + 1) * 512]),
                                    deps=deps, sig="dve")
                    t_sq2 = p.do("gpsimd", lambda e, oc=oc: e.tensor_tensor(
                        out=X2SQ[:, (oc % 2) * 512:(oc % 2 + 1) * 512],
                        in0=X2BF[:, (oc % 2) * 512:(oc % 2 + 1) * 512],
                        in1=X2BF[:, (oc % 2) * 512:(oc % 2 + 1) * 512], op=ALU.mult),
                        deps=[t_bfd[k]], sig="pool")
                    p.do("tensor", lambda e, oc=oc: e.matmul(
                        pb[4][0:1, :], ones_col[:],
                        X2BF[:, (oc % 2) * 512:(oc % 2 + 1) * 512],
                        start=(oc == 0), stop=(oc == NCH - 1),
                        skip_group_check=True), deps=[t_bfd[k]])
                    t_stat[k] = p.do("tensor", lambda e, oc=oc: e.matmul(
                        pb[5][0:1, :], ones_col[:],
                        X2SQ[:, (oc % 2) * 512:(oc % 2 + 1) * 512],
                        start=(oc == 0), stop=(oc == NCH - 1),
                        skip_group_check=True), deps=[t_sq2], sig="pe")
                    if oc % 4 == 3 and oc // 4 < 2:
                        emit_xld(oc // 4 + 2)
                    if oc % 4 == 3:
                        t_x2st[(gi, oc // 4)] = p.do(
                            "gpsimd",
                            lambda e, gi=gi, oc=oc, slot4=slot4: e.dma_start(
                                out=x2_dram.rearrange("(c p) t -> p c t", p=128)
                                [:, oc - 3:oc + 1, gi * 512:(gi + 1) * 512],
                                in_=X2ST.rearrange("p (c t) -> p c t", t=512)
                                [:, slot4 * 4:slot4 * 4 + 4, :]),
                            deps=[t_x2[(gi, oc)]], sig=f"s2{slot4}", inc=16)
                p.do("vector", lambda e, gi=gi: e.tensor_copy(
                    out=srow[0:1, gi * 512:(gi + 1) * 512], in_=pb[4][0:1, :]),
                    deps=[t_stat[gi * NCH + NCH - 1]], sig="dve")
                p.do("vector", lambda e, gi=gi: e.tensor_copy(
                    out=s2row[0:1, gi * 512:(gi + 1) * 512], in_=pb[5][0:1, :]),
                    deps=[t_stat[gi * NCH + NCH - 1]], sig="dve")
            t_x2_all = None
            x2_done = [("s20", counts["s20"]), ("s21", counts["s21"])]
            if debug:
                p.do("gpsimd", lambda e: e.dma_start(out=wobf_dbg, in_=WOBF[:, :]),
                     deps=x2_done, sig="sc", inc=16)
                p.do("gpsimd", lambda e: e.dma_start(out=valt_dbg, in_=VALT[:, :]),
                     sig="sc", inc=16)

            p.do("vector", lambda e: e.tensor_scalar_mul(
                mrow[0:1, :], srow[0:1, :], 1.0 / D))
            p.do("vector", lambda e: e.tensor_scalar_mul(
                vrow[0:1, :], s2row[0:1, :], 1.0 / D))
            p.do("vector", lambda e: e.tensor_tensor(
                srow[0:1, :], mrow[0:1, :], mrow[0:1, :], op=ALU.mult))
            t_v2 = p.do("vector", lambda e: e.tensor_tensor(
                vrow[0:1, :], vrow[0:1, :], srow[0:1, :], op=ALU.subtract),
                sig="dve")
            t_ln2 = p.do("scalar", lambda e: e.activation(
                s2row[0:1, :], vrow[0:1, :], AF.Ln,
                bias=consts[0:1, CEPS:CEPS + 1]), deps=[t_v2], sig="act")
            t_rstd2 = p.do("scalar", lambda e: e.activation(
                rstdrow[0:1, :], s2row[0:1, :], AF.Exp,
                scale=consts[0:1, CNHALF:CNHALF + 1]), deps=[t_ln2], sig="act")
            t_mrstd2 = p.do("vector", lambda e: e.tensor_tensor(
                mrstdrow[0:1, :], mrow[0:1, :], rstdrow[0:1, :], op=ALU.mult),
                deps=[t_rstd2], sig="dve")
            p.run(nc)

        t_hst = {}
        with contextlib.ExitStack() as sc:
            XLNT = sc.enter_context(nc.sbuf_tensor("XLNT2", [128, NCH * T], BF16))
            XCH = sc.enter_context(nc.sbuf_tensor("XCH2", [128, 2 * T], F32))
            NRM = sc.enter_context(nc.sbuf_tensor("NRM2", [128, 512], F32))
            WGU8 = sc.enter_context(nc.sbuf_tensor("WGU8", [128, 2 * 8192], U8))
            WGBF = sc.enter_context(nc.sbuf_tensor("WGBF", [128, 8192], BF16))
            WUBF = sc.enter_context(nc.sbuf_tensor("WUBF", [128, 8192], BF16))
            GT = sc.enter_context(nc.sbuf_tensor("GT", [128, 2 * 512], BF16))
            HST = sc.enter_context(nc.sbuf_tensor("HST", [128, 2 * 2048], BF16))
            pb = [sc.enter_context(nc.psum_tensor(f"pq{i}", [128, 512], F32))
                  for i in range(8)]

            p = Plan(sems, counts, marks, entry_guards=st_guard())
            t_bc_done = ln_bcasts(p, t_rstd2, t_mrstd2, None, pb)
            t_norm = {}
            for i in range(NCH):
                par = i % 2
                po, pe_ = par * T, (par + 1) * T
                deps = [t_bc_done] + x2_done
                if i >= 2:
                    deps += [t_norm[i - 2]]
                t_l2 = p.do("sync", lambda e, i=i, po=po, pe_=pe_: e.dma_start(
                    out=XCH[:, po:pe_], in_=x2_dram[i * 128:(i + 1) * 128, :]),
                    deps=deps, sig=f"lx{i % 2}", inc=16)
                for g4 in range(4):
                    p.do("vector", lambda e, po=po, g4=g4: e.tensor_tensor(
                        NRM[:, :], XCH[:, po + g4 * 512:po + (g4 + 1) * 512],
                        pb[2 * g4][:, :], op=ALU.mult),
                        deps=[t_l2] if g4 == 0 else [])
                    tok = p.do("vector", lambda e, i=i, g4=g4: e.tensor_tensor(
                        XLNT[:, i * T + g4 * 512:i * T + (g4 + 1) * 512],
                        NRM[:, :], pb[2 * g4 + 1][:, :], op=ALU.subtract),
                        sig="dve" if g4 == 3 else None)
                t_norm[i] = tok

            t_gc, t_uc, t_mmg, t_mmu, t_h = {}, {}, {}, {}, {}
            pend = {}

            def emit_h(kk):
                tok, t_silu, cc, gi, ppu = pend[kk]
                deps = [tok, t_silu, t_scl]
                if gi == 0 and cc >= 2:
                    deps += [t_hst.get(cc - 2)]
                t_h[kk] = p.do("vector",
                               lambda e, cc=cc, gi=gi, ppu=ppu:
                               e.scalar_tensor_tensor(
                                   HST[:, (cc % 2) * 2048 + gi * 512:
                                       (cc % 2) * 2048 + (gi + 1) * 512],
                                   ppu[:, :], su[:, cc:cc + 1],
                                   GT[:, (gi % 2) * 512:(gi % 2 + 1) * 512],
                                   op0=ALU.mult, op1=ALU.mult),
                               deps=deps, sig="dve")

            for s in range(11):
                deps = [t_gc[s - 1]] if s >= 1 else []
                t_gl = p.do("sync", lambda e, s=s: e.dma_start(
                    out=WGU8[:, 0:8192], in_=wg_s[s * 128:(s + 1) * 128, :]),
                    deps=deps, sig="lg", inc=16)
                deps = [t_gl] + ([t_mmg[s - 1]] if s >= 1 else [])
                t_gc[s] = p.do("vector", lambda e: e.tensor_scalar_add(
                    WGBF[:, :], WGU8[:, 0:8192], -128.0), deps=deps, sig="dve")
                deps = [t_uc[s - 1]] if s >= 1 else []
                t_ul = p.do("sync", lambda e, s=s: e.dma_start(
                    out=WGU8[:, 8192:16384], in_=wu_s[s * 128:(s + 1) * 128, :]),
                    deps=deps, sig="lu", inc=16)
                deps = [t_ul] + ([t_mmu[s - 1]] if s >= 1 else [])
                t_uc[s] = p.do("vector", lambda e: e.tensor_scalar_add(
                    WUBF[:, :], WGU8[:, 8192:16384], -128.0), deps=deps, sig="dve")
                for hc in range(4):
                    cc = s * 4 + hc
                    for gi in range(4):
                        kk = cc * 4 + gi
                        ppg = pb[gi % 2]
                        ppu = pb[2 + gi % 2]
                        deps = [t_gc[s], t_norm[NCH - 1]]
                        if kk >= 2:
                            deps += [t_h.get(kk - 2)]
                        for c in range(NCH):
                            tok = p.do("tensor", lambda e, hc=hc, gi=gi, c=c,
                                       ppg=ppg: e.matmul(
                                           ppg[:, :],
                                           WGBF[:, c * 512 + hc * 128:
                                                c * 512 + (hc + 1) * 128],
                                           XLNT[:, c * T + gi * 512:
                                                c * T + (gi + 1) * 512],
                                           start=(c == 0), stop=(c == NCH - 1),
                                           skip_group_check=True),
                                       deps=deps if c == 0 else [],
                                       sig="pe" if c == NCH - 1 else None)
                        t_mmg[s] = tok
                        t_silu = p.do("scalar", lambda e, cc=cc, gi=gi, ppg=ppg:
                                      e.activation(
                                          GT[:, (gi % 2) * 512:(gi % 2 + 1) * 512],
                                          ppg[:, :], AF.Silu, scale=sg[:, cc:cc + 1]),
                                      deps=[tok, t_scl], sig="act")
                        deps = [t_uc[s]]
                        for c in range(NCH):
                            tok = p.do("tensor", lambda e, hc=hc, gi=gi, c=c,
                                       ppu=ppu: e.matmul(
                                           ppu[:, :],
                                           WUBF[:, c * 512 + hc * 128:
                                                c * 512 + (hc + 1) * 128],
                                           XLNT[:, c * T + gi * 512:
                                                c * T + (gi + 1) * 512],
                                           start=(c == 0), stop=(c == NCH - 1),
                                           skip_group_check=True),
                                       deps=deps if c == 0 else [],
                                       sig="pe" if c == NCH - 1 else None)
                        t_mmu[s] = tok
                        pend[kk] = (tok, t_silu, cc, gi, ppu)
                        if kk >= 1:
                            emit_h(kk - 1)
                    if cc == 43:
                        emit_h(cc * 4 + 3)
                    if cc >= 1:
                        t_hst[cc - 1] = p.do("gpsimd", lambda e, cc=cc: e.dma_start(
                            out=h_dram[(cc - 1) * 128:cc * 128, :],
                            in_=HST[:, ((cc - 1) % 2) * 2048:
                                    ((cc - 1) % 2) * 2048 + 2048]),
                            deps=[t_h[(cc - 1) * 4 + 3]], sig=f"sh{(cc - 1) % 2}",
                            inc=16)
                    if cc == 43:
                        t_hst[43] = p.do("gpsimd", lambda e: e.dma_start(
                            out=h_dram[43 * 128:44 * 128, :],
                            in_=HST[:, (43 % 2) * 2048:(43 % 2) * 2048 + 2048]),
                            deps=[t_h[43 * 4 + 3]], sig="sh1", inc=16)
            h_done = [("sh0", counts["sh0"]), ("sh1", counts["sh1"])]
            p.run(nc)

        with contextlib.ExitStack() as sc:
            WDU8 = sc.enter_context(nc.sbuf_tensor("WDU8", [128, 2 * 5632], U8))
            WDBF = sc.enter_context(nc.sbuf_tensor("WDBF", [128, 22528], BF16))
            HT = sc.enter_context(nc.sbuf_tensor("HT", [128, 2 * 11264], BF16))
            X2CH = sc.enter_context(nc.sbuf_tensor("X2CH", [128, 2 * 2048], F32))
            OUTS = sc.enter_context(nc.sbuf_tensor("OUTS", [128, 2 * 2048], F32))
            pb = [sc.enter_context(nc.psum_tensor(f"pf{i}", [128, 512], F32))
                  for i in range(4)]

            p = Plan(sems, counts, marks, entry_guards=st_guard())
            t_out, t_ost, t_f, t_htld, t_x2ld, t_wdc = {}, {}, {}, {}, {}, {}
            t_lastread = {}
            for dg in range(4):
                for piece in range(4):
                    kk = dg * 4 + piece
                    deps = [t_scl]
                    if kk >= 2:
                        deps += [t_wdc.get(kk - 2)]
                    t_wl = p.do("sync", lambda e, dg=dg, piece=piece: e.dma_start(
                        out=WDU8[:, (piece % 2) * 5632:(piece % 2 + 1) * 5632],
                        in_=wd_s[dg * 128:(dg + 1) * 128,
                                 piece * 5632:(piece + 1) * 5632]),
                        deps=deps, sig=f"lwd{piece % 2}", inc=16)
                    deps = [t_wl]
                    if dg >= 1 and piece == 0:
                        deps += [t_f.get((dg * 4 - 1, 3))]
                    t_wdc[kk] = p.do("vector",
                                     lambda e, piece=piece: e.tensor_scalar_add(
                                         WDBF[:, piece * 5632:(piece + 1) * 5632],
                                         WDU8[:, (piece % 2) * 5632:
                                              (piece % 2 + 1) * 5632],
                                         -128.0), deps=deps, sig="dve")
                for tg in range(4):
                    it = dg * 4 + tg
                    for hb in range(2):
                        for j in range(6):
                            n = min(4, 22 - j * 4)
                            hc0 = hb * 22 + j * 4
                            deps = list(h_done)
                            if it >= 1:
                                deps += [t_lastread.get((it - 1, hb))]
                            t_htld[(it, hb)] = p.do(
                                "sync",
                                lambda e, hc0=hc0, n=n, hb=hb, j=j, tg=tg: e.dma_start(
                                    out=HT.rearrange("p (c t) -> p c t", t=512)
                                    [:, hb * 22 + j * 4:hb * 22 + j * 4 + n, :],
                                    in_=h_dram.rearrange("(c p) t -> p c t", p=128)
                                    [:, hc0:hc0 + n, tg * 512:(tg + 1) * 512]),
                                deps=deps if j == 0 else [], sig=f"lh{hb}", inc=16)
                    deps = []
                    if it >= 2:
                        deps += [t_out.get((it - 2, 3))]
                    t_x2ld[it] = p.do("sync", lambda e, dg=dg, tg=tg, it=it: e.dma_start(
                        out=X2CH.rearrange("p (c t) -> p c t", t=512)
                        [:, (it % 2) * 4:(it % 2) * 4 + 4, :],
                        in_=x2_dram.rearrange("(c p) t -> p c t", p=128)
                        [:, dg * 4:dg * 4 + 4, tg * 512:(tg + 1) * 512]),
                        deps=deps + x2_done, sig=f"l2{it % 2}", inc=16)
                    for hb in range(2):
                        for hc22 in range(22):
                            hc = hb * 22 + hc22
                            for dc in range(4):
                                deps = []
                                if hb == 0 and hc22 == 0:
                                    deps += [t_wdc[dg * 4 + 3]]
                                    if it >= 1:
                                        deps += [t_out.get((it - 1, dc))]
                                if hc22 == 0 and dc == 0:
                                    deps += [t_htld[(it, hb)]]
                                sig = None
                                if hc == 43 or (hc22 == 21 and dc == 3):
                                    sig = "pe"
                                tok = p.do("tensor",
                                           lambda e, hc=hc, dc=dc, hb=hb,
                                           hc22=hc22: e.matmul(
                                               pb[dc][:, :],
                                               WDBF[:, hc * 512 + dc * 128:
                                                    hc * 512 + (dc + 1) * 128],
                                               HT[:, hb * 11264 + hc22 * 512:
                                                  hb * 11264 + (hc22 + 1) * 512],
                                               start=(hc == 0), stop=(hc == 43),
                                               skip_group_check=True),
                                           deps=deps, sig=sig)
                                if hc22 == 21 and dc == 3:
                                    t_lastread[(it, hb)] = tok
                                if hc == 43:
                                    t_f[(it, dc)] = tok
                    for dc in range(4):
                        deps = [t_f[(it, dc)], t_x2ld[it], t_scl]
                        if it >= 2:
                            deps += [t_ost.get(it - 2)]
                        t_out[(it, dc)] = p.do("vector",
                                               lambda e, dg=dg, dc=dc, it=it:
                                               e.scalar_tensor_tensor(
                                                   OUTS[:, (it % 2) * 2048 + dc * 512:
                                                        (it % 2) * 2048 +
                                                        (dc + 1) * 512],
                                                   pb[dc][:, :],
                                                   sd[:, dg * 4 + dc:dg * 4 + dc + 1],
                                                   X2CH[:, (it % 2) * 2048 + dc * 512:
                                                        (it % 2) * 2048 +
                                                        (dc + 1) * 512],
                                                   op0=ALU.mult, op1=ALU.add),
                                               deps=deps, sig="dve")
                    t_ost[it] = p.do("gpsimd", lambda e, dg=dg, tg=tg, it=it: e.dma_start(
                        out=outT.rearrange("(c p) t -> p c t", p=128)
                        [:, dg * 4:dg * 4 + 4, tg * 512:(tg + 1) * 512],
                        in_=OUTS.rearrange("p (c t) -> p c t", t=512)
                        [:, (it % 2) * 4:(it % 2) * 4 + 4, :]),
                        deps=[t_out[(it, 3)]], sig=f"so{it % 2}", inc=16)
            p.do("gpsimd", lambda e: e.nop(),
                 deps=[("so0", counts["so0"]), ("so1", counts["so1"])])
            p.run(nc)

    return nc


_NC_CACHE = None


def kernel(L0_attn_x, L0_attn_w_qkv, L0_attn_w_o, L0_ffn_wq_gate, L0_ffn_s_gate,
           L0_ffn_wq_up, L0_ffn_s_up, L0_ffn_wq_down, L0_ffn_s_down):
    global _NC_CACHE
    x = np.asarray(L0_attn_x)
    w_qkv = np.ascontiguousarray(np.asarray(L0_attn_w_qkv), dtype=np.float32)
    w_o = np.ascontiguousarray(np.asarray(L0_attn_w_o), dtype=np.float32)

    wqkv_s = np.ascontiguousarray(
        w_qkv.reshape(16, 128, 12, 512).transpose(2, 1, 0, 3).reshape(12 * 128, 8192))
    wo_s = np.ascontiguousarray(
        w_o.reshape(16, 128, 4, 512).transpose(2, 1, 0, 3).reshape(4 * 128, 8192))

    def u8view(a):
        a = np.ascontiguousarray(np.asarray(a, dtype=np.int32))
        return a.view(np.uint8)[:, ::4]

    wg8 = u8view(L0_ffn_wq_gate)
    wu8 = u8view(L0_ffn_wq_up)
    wd8 = u8view(L0_ffn_wq_down)
    wg_s = np.ascontiguousarray(
        wg8.reshape(11, 512, 16, 128).transpose(0, 3, 2, 1).reshape(11 * 128, 8192))
    wu_s = np.ascontiguousarray(
        wu8.reshape(11, 512, 16, 128).transpose(0, 3, 2, 1).reshape(11 * 128, 8192))
    wd_s = np.ascontiguousarray(
        wd8.reshape(4, 512, 44, 128).transpose(0, 3, 2, 1).reshape(4 * 128, 22528))
    sg_s = np.ascontiguousarray(
        np.asarray(L0_ffn_s_gate, dtype=np.float32).reshape(44, 128).T)
    su_s = np.ascontiguousarray(
        np.asarray(L0_ffn_s_up, dtype=np.float32).reshape(44, 128).T)
    sd_s = np.ascontiguousarray(
        np.asarray(L0_ffn_s_down, dtype=np.float32).reshape(16, 128).T)

    in_maps = []
    for c in range(8):
        b, hf = c // 2, c % 2
        own = x[b, hf * T:(hf + 1) * T]
        par = x[b, (1 - hf) * T:(2 - hf) * T]
        xT_c = np.ascontiguousarray(
            np.concatenate([own, par], axis=0).T.astype(np.float32, copy=False))
        in_maps.append({
            "xT": xT_c, "wqkv_s": wqkv_s, "wo_s": wo_s,
            "wg_s": wg_s, "wu_s": wu_s, "wd_s": wd_s,
            "sg_s": sg_s, "su_s": su_s, "sd_s": sd_s,
        })

    if _NC_CACHE is None:
        _NC_CACHE = build_program()
    res = run_bass_kernel_spmd(_NC_CACHE, in_maps, core_ids=list(range(8)))

    out = np.empty((4, S, D), dtype=np.float32)
    for c in range(8):
        b, hf = c // 2, c % 2
        out[b, hf * T:(hf + 1) * T, :] = res.results[c]["outT"].T
    return out


# version 06
# speedup vs baseline: 1.2514x; 1.0286x over previous
import contextlib
import numpy as np
import concourse.bass as bass
import concourse.mybir as mybir
from concourse.bass_utils import run_bass_kernel_spmd

F32 = mybir.dt.float32
BF16 = mybir.dt.bfloat16
U8 = mybir.dt.uint8
AF = mybir.ActivationFunctionType
ALU = mybir.AluOpType
AXX = mybir.AxisListType.X

D = 2048
S = 4096
T = 2048
HID = 5632
NH = 16
NCH = D // 128
EXPS = float(1.0 / np.sqrt(128.0))
EXPB = -12.0
EPS = 1e-5

ENGS = ("sync", "vector", "scalar", "tensor", "gpsimd")


class Plan:

    def __init__(self, sems, counts, marks, entry_guards=()):
        self.sems = sems
        self.counts = counts
        self.marks = marks
        self.ev = {e: [] for e in ENGS}
        self.entry = {e: list(entry_guards) for e in ENGS}

    def do(self, eng, fn, deps=(), sig=None, inc=1):
        evl = self.ev[eng]
        mk = self.marks[eng]
        alldeps = self.entry[eng] + list(deps)
        self.entry[eng] = []
        order = {"vector": "dve", "scalar": "act"}.get(eng)
        if eng == "gpsimd" and inc != 16:
            order = "pool"
        if order is not None:
            alldeps = alldeps + [(order, self.counts.get(order, 0))]
            if sig is None:
                sig = order
            assert sig == order, (eng, sig)
        best = {}
        for d in alldeps:
            if d is None:
                continue
            s, v = d
            if v > best.get(s, 0):
                best[s] = v
        for s, v in best.items():
            if v > 0 and mk.get(s, 0) < v:
                evl.append(("w", s, v))
                mk[s] = v
        tok = None
        if sig is not None:
            self.counts[sig] = self.counts.get(sig, 0) + inc
            tok = (sig, self.counts[sig])
        evl.append(("d", fn, sig, inc))
        return tok

    def run(self, nc):
        with nc.Block() as blk:
            for eng in ENGS:
                items = self.ev[eng]
                if not items:
                    continue

                def closure(e, items=items):
                    for it in items:
                        if it[0] == "w":
                            e.wait_ge(self.sems[it[1]], it[2])
                        else:
                            ins = it[1](e)
                            if it[2] is not None:
                                ins.then_inc(self.sems[it[2]], it[3])

                getattr(blk, eng)(closure)


def build_program(debug=False):
    nc = bass.Bass("TRN2", target_bir_lowering=False)
    KIND = "ExternalOutput" if debug else "Internal"

    xT = nc.dram_tensor("xT", [D, S], F32, kind="ExternalInput").ap()
    wqkv_s = nc.dram_tensor("wqkv_s", [12 * 128, 8192], F32, kind="ExternalInput").ap()
    wo_s = nc.dram_tensor("wo_s", [4 * 128, 8192], F32, kind="ExternalInput").ap()
    wg_s = nc.dram_tensor("wg_s", [11 * 128, 8192], U8, kind="ExternalInput").ap()
    wu_s = nc.dram_tensor("wu_s", [11 * 128, 8192], U8, kind="ExternalInput").ap()
    wd_s = nc.dram_tensor("wd_s", [4 * 128, 22528], U8, kind="ExternalInput").ap()
    sg_s = nc.dram_tensor("sg_s", [128, 44], F32, kind="ExternalInput").ap()
    su_s = nc.dram_tensor("su_s", [128, 44], F32, kind="ExternalInput").ap()
    sd_s = nc.dram_tensor("sd_s", [128, 16], F32, kind="ExternalInput").ap()

    q_dram = nc.dram_tensor("q_dram", [D, S], BF16, kind=KIND).ap()
    k_dram = nc.dram_tensor("k_dram", [D, S], BF16, kind=KIND).ap()
    v_dram = nc.dram_tensor("v_dram", [D, T], BF16, kind=KIND).ap()
    x2_dram = nc.dram_tensor("x2_dram", [D, T], F32, kind=KIND).ap()
    h_dram = nc.dram_tensor("h_dram", [HID, T], BF16, kind=KIND).ap()
    csum_dram = nc.dram_tensor("csum_dram", [NH, T], F32, kind=KIND).ap()

    outT = nc.dram_tensor("outT", [D, T], F32, kind="ExternalOutput").ap()
    wobf_dbg = (nc.dram_tensor("wobf_dbg", [128, 4 * 8192], BF16, kind="ExternalOutput").ap()
                if debug else None)
    wdbg = (nc.dram_tensor("wdbg", [128, 32], F32, kind="ExternalOutput").ap()
            if debug else None)
    valt_dbg = (nc.dram_tensor("valt_dbg", [128, 8192], BF16, kind="ExternalOutput").ap()
                if debug else None)

    counts = {}
    marks = {e: {} for e in ENGS}

    def st_guard():
        return [(s, counts.get(s, 0)) for s in
                ("sq0", "sq1", "sc", "s20", "s21", "sh0", "sh1", "so0", "so1")]

    with contextlib.ExitStack() as g:
        semnames = ("pe", "act", "dve", "pool", "ldi", "lx0", "lx1", "lw",
                    "sq0", "sq1", "lq0", "lq1", "sc", "lv0", "lv1", "lp0",
                    "lp1", "lc0", "lc1", "s20", "s21", "lg", "lu", "sh0",
                    "sh1", "lwd0", "lwd1", "lh0", "lh1", "l20", "l21",
                    "so0", "so1")
        sems = {n: g.enter_context(nc.semaphore(n)) for n in semnames}
        STORE_SEMS = ("sq0", "sq1", "sc", "s20", "s21", "sh0", "sh1",
                      "so0", "so1")

        ones_col = g.enter_context(nc.sbuf_tensor("ones_col", [128, 1], BF16))
        ones_row = g.enter_context(nc.sbuf_tensor("ones_row", [1, 128], F32))
        consts = g.enter_context(nc.sbuf_tensor("consts", [128, 4], F32))
        sg = g.enter_context(nc.sbuf_tensor("sg", [128, 44], F32))
        su = g.enter_context(nc.sbuf_tensor("su", [128, 44], F32))
        sd = g.enter_context(nc.sbuf_tensor("sd", [128, 16], F32))
        srow = g.enter_context(nc.sbuf_tensor("srow", [1, T], F32))
        s2row = g.enter_context(nc.sbuf_tensor("s2row", [1, T], F32))
        mrow = g.enter_context(nc.sbuf_tensor("mrow", [1, T], F32))
        rstdrow = g.enter_context(nc.sbuf_tensor("rstdrow", [1, T], F32))
        mrstdrow = g.enter_context(nc.sbuf_tensor("mrstdrow", [1, T], F32))
        cstage = g.enter_context(nc.sbuf_tensor("cstage", [1, T], F32))
        vrow = cstage
        csrow = g.enter_context(nc.sbuf_tensor("csrow", [1, 2 * 512], F32))
        dparts = g.enter_context(nc.sbuf_tensor("dparts", [128, 16], F32))
        dstage = g.enter_context(nc.sbuf_tensor("dstage", [128, 16], F32))
        actscr = g.enter_context(nc.sbuf_tensor("actscr", [128, 4], F32))
        denb = g.enter_context(nc.sbuf_tensor("denb", [128, 2], F32))
        povn = g.enter_context(nc.sbuf_tensor("povn", [128, 2], F32))
        pscr = g.enter_context(nc.sbuf_tensor("pscr", [128, 2], F32))
        wcolf = g.enter_context(nc.sbuf_tensor("wcolf", [128, 2], F32))
        wcolb = g.enter_context(nc.sbuf_tensor("wcolb", [128, 2], BF16))

        CEXPB, CEXPS, CEPS, CNHALF = 0, 1, 2, 3

        p = Plan(sems, counts, marks)
        t_ones = p.do("vector", lambda e: e.memset(ones_col[:], 1.0))
        p.do("vector", lambda e: e.memset(ones_row[:], 1.0))
        p.do("vector", lambda e: e.memset(consts[:, CEXPB:CEXPB + 1], EXPB))
        p.do("vector", lambda e: e.memset(consts[:, CEXPS:CEXPS + 1], EXPS))
        p.do("vector", lambda e: e.memset(consts[:, CEPS:CEPS + 1], EPS))
        t_cn = p.do("vector", lambda e: e.memset(consts[:, CNHALF:CNHALF + 1], -0.5),
                    sig="dve")
        p.do("sync", lambda e: e.dma_start(out=sg[:], in_=sg_s), sig="ldi", inc=16)
        p.do("sync", lambda e: e.dma_start(out=su[:], in_=su_s), sig="ldi", inc=16)
        t_scl = p.do("sync", lambda e: e.dma_start(out=sd[:], in_=sd_s), sig="ldi", inc=16)
        p.run(nc)

        def ln_rows(p, t_sum_done, t_sq_done, pb_s, pb_q):
            t_rows = []
            for gi in range(4):
                t_rows.append(p.do("vector", lambda e, gi=gi: e.tensor_copy(
                    out=srow[0:1, gi * 512:(gi + 1) * 512], in_=pb_s[gi][0:1, :]),
                    deps=[t_sum_done], sig="dve"))
                t_rows.append(p.do("vector", lambda e, gi=gi: e.tensor_copy(
                    out=s2row[0:1, gi * 512:(gi + 1) * 512], in_=pb_q[gi][0:1, :]),
                    deps=[t_sq_done], sig="dve"))
            p.do("vector", lambda e: e.tensor_scalar_mul(
                mrow[0:1, :], srow[0:1, :], 1.0 / D), deps=[t_rows[-1]])
            p.do("vector", lambda e: e.tensor_scalar_mul(
                vrow[0:1, :], s2row[0:1, :], 1.0 / D))
            p.do("vector", lambda e: e.tensor_tensor(
                srow[0:1, :], mrow[0:1, :], mrow[0:1, :], op=ALU.mult))
            t_var = p.do("vector", lambda e: e.tensor_tensor(
                vrow[0:1, :], vrow[0:1, :], srow[0:1, :], op=ALU.subtract),
                sig="dve")
            t_ln = p.do("scalar", lambda e: e.activation(
                s2row[0:1, :], vrow[0:1, :], AF.Ln,
                bias=consts[0:1, CEPS:CEPS + 1]), deps=[t_var], sig="act")
            t_rstd = p.do("scalar", lambda e: e.activation(
                rstdrow[0:1, :], s2row[0:1, :], AF.Exp,
                scale=consts[0:1, CNHALF:CNHALF + 1]), deps=[t_ln], sig="act")
            t_mrstd = p.do("vector", lambda e: e.tensor_tensor(
                mrstdrow[0:1, :], mrow[0:1, :], rstdrow[0:1, :], op=ALU.mult),
                deps=[t_rstd], sig="dve")
            return t_rows[-1], t_rstd, t_mrstd

        def ln_bcasts(p, t_rstd, t_mrstd, t_banks_free, pb):
            t_bc = []
            for gi in range(4):
                tm = p.do("tensor", lambda e, gi=gi: e.matmul(
                    pb[2 * gi][:, :], ones_row[:],
                    rstdrow[0:1, gi * 512:(gi + 1) * 512], start=True, stop=True),
                    deps=[t_rstd, t_banks_free], sig="pe")
                t_bc.append(tm)
                tm = p.do("tensor", lambda e, gi=gi: e.matmul(
                    pb[2 * gi + 1][:, :], ones_row[:],
                    mrstdrow[0:1, gi * 512:(gi + 1) * 512], start=True, stop=True),
                    deps=[t_mrstd], sig="pe")
                t_bc.append(tm)
            return t_bc[-1]

        qkv_last_store = [None]

        def phase_ab(half):
            with contextlib.ExitStack() as sc:
                XLNT = sc.enter_context(nc.sbuf_tensor(f"XLNT{half}", [128, NCH * T], BF16))
                XCH = sc.enter_context(nc.sbuf_tensor(f"XCH{half}", [128, 2 * T], F32))
                NRM = sc.enter_context(nc.sbuf_tensor(f"NRM{half}", [128, 512], F32))
                WF32 = sc.enter_context(nc.sbuf_tensor(f"WF32{half}", [128, 4096], F32))
                WBF = sc.enter_context(nc.sbuf_tensor(f"WBF{half}", [128, 2 * 8192], BF16))
                QSTG = sc.enter_context(nc.sbuf_tensor(f"QSTG{half}", [128, 2 * 2048], BF16))
                pb = [sc.enter_context(nc.psum_tensor(f"pb{half}_{i}", [128, 512], F32))
                      for i in range(8)]
                XBF = lambda par: WBF[:, par * 2048:(par + 1) * 2048]
                SQB = lambda par: WBF[:, 4096 + par * 2048: 4096 + (par + 1) * 2048]

                p = Plan(sems, counts, marks, entry_guards=st_guard())
                cols = slice(half * T, (half + 1) * T)

                t_ld, t_cast, t_sq, t_summ, t_sqm = {}, {}, {}, {}, {}
                for i in range(NCH):
                    par = i % 2
                    po, pe_ = par * T, (par + 1) * T
                    deps = [t_ones, t_cn]
                    if i >= 2:
                        deps += [t_cast[i - 2]]
                    t_ld[i] = p.do("sync", lambda e, i=i, po=po, pe_=pe_: e.dma_start(
                        out=XCH[:, po:pe_], in_=xT[i * 128:(i + 1) * 128, cols]),
                        deps=deps, sig=f"lx{i % 2}", inc=16)
                    deps = [t_ld[i]]
                    if i >= 2:
                        deps += [t_summ[i - 2], t_sq[i - 2]]
                    t_cast[i] = p.do("vector", lambda e, par=par, po=po, pe_=pe_:
                                     e.tensor_copy(out=XBF(par), in_=XCH[:, po:pe_]),
                                     deps=deps, sig="dve")
                    deps = [t_cast[i]]
                    if i >= 2:
                        deps += [t_sqm[i - 2]]
                    t_sq[i] = p.do("gpsimd", lambda e, par=par: e.tensor_tensor(
                        out=SQB(par), in0=XBF(par), in1=XBF(par), op=ALU.mult),
                        deps=deps, sig="pool")
                    for gi in range(4):
                        tok = p.do("tensor", lambda e, i=i, par=par, gi=gi: e.matmul(
                            pb[gi][0:1, :], ones_col[:],
                            XBF(par)[:, gi * 512:(gi + 1) * 512],
                            start=(i == 0), stop=(i == NCH - 1),
                            skip_group_check=True),
                            deps=[t_cast[i]] if gi == 0 else [],
                            sig="pe" if gi == 3 else None)
                    t_summ[i] = tok
                    for gi in range(4):
                        tok = p.do("tensor", lambda e, i=i, par=par, gi=gi: e.matmul(
                            pb[4 + gi][0:1, :], ones_col[:],
                            SQB(par)[:, gi * 512:(gi + 1) * 512],
                            start=(i == 0), stop=(i == NCH - 1),
                            skip_group_check=True),
                            deps=[t_sq[i]] if gi == 0 else [],
                            sig="pe" if gi == 3 else None)
                    t_sqm[i] = tok

                t_rows_done, t_rstd, t_mrstd = ln_rows(
                    p, t_summ[NCH - 1], t_sqm[NCH - 1], pb[0:4], pb[4:8])
                t_bc_done = ln_bcasts(p, t_rstd, t_mrstd, t_rows_done, pb)

                t_norm = {}
                for i in range(NCH):
                    par = i % 2
                    po, pe_ = par * T, (par + 1) * T
                    deps = [t_bc_done]
                    if i >= 2:
                        deps += [t_norm[i - 2]]
                    t_l2 = p.do("sync", lambda e, i=i, po=po, pe_=pe_: e.dma_start(
                        out=XCH[:, po:pe_], in_=xT[i * 128:(i + 1) * 128, cols]),
                        deps=deps, sig=f"lx{i % 2}", inc=16)
                    for g4 in range(4):
                        p.do("vector", lambda e, po=po, g4=g4: e.tensor_tensor(
                            NRM[:, :], XCH[:, po + g4 * 512:po + (g4 + 1) * 512],
                            pb[2 * g4][:, :], op=ALU.mult),
                            deps=[t_l2] if g4 == 0 else [])
                        tok = p.do("vector", lambda e, i=i, g4=g4: e.tensor_tensor(
                            XLNT[:, i * T + g4 * 512:i * T + (g4 + 1) * 512],
                            NRM[:, :], pb[2 * g4 + 1][:, :], op=ALU.subtract),
                            sig="dve" if g4 == 3 else None)
                    t_norm[i] = tok

                slabs = list(range(12)) if half == 0 else list(range(8))
                t_wcast, t_mmlast, t_ev, t_st = {}, {}, {}, {}
                for si, s in enumerate(slabs):
                    wslot = si % 2
                    for piece in range(2):
                        deps = []
                        if si == 0 and piece == 0:
                            deps += [t_summ[NCH - 1], t_sqm[NCH - 1]]
                        if si > 0 or piece > 0:
                            deps += [t_wcast.get((si, piece - 1),
                                                 t_wcast.get((si - 1, 1)))]
                        t_wl = p.do("sync", lambda e, s=s, piece=piece: e.dma_start(
                            out=WF32[:, :],
                            in_=wqkv_s[s * 128:(s + 1) * 128,
                                       piece * 4096:(piece + 1) * 4096]),
                            deps=deps, sig="lw", inc=16)
                        deps = [t_wl]
                        if si >= 2 and piece == 0:
                            deps += [t_mmlast[slabs[si - 2]]]
                        t_wcast[(si, piece)] = p.do(
                            "vector", lambda e, wslot=wslot, piece=piece: e.tensor_copy(
                                out=WBF[:, wslot * 8192 + piece * 4096:
                                        wslot * 8192 + (piece + 1) * 4096],
                                in_=WF32[:, :]),
                            deps=deps, sig="dve")
                    for fc in range(4):
                        rowi = si * 4 + fc
                        stg = rowi % 2
                        for tg in range(4):
                            key = rowi * 4 + tg
                            pp = pb[2 + key % 2]
                            deps = [t_wcast[(si, 1)], t_norm[NCH - 1]]
                            if key >= 2:
                                deps += [t_ev.get(key - 2)]
                            for c in range(NCH):
                                tok = p.do("tensor",
                                           lambda e, wslot=wslot, fc=fc, tg=tg, c=c,
                                           pp=pp: e.matmul(
                                               pp[:, :],
                                               WBF[:, wslot * 8192 + c * 512 + fc * 128:
                                                   wslot * 8192 + c * 512 + (fc + 1) * 128],
                                               XLNT[:, c * T + tg * 512:
                                                    c * T + (tg + 1) * 512],
                                               start=(c == 0), stop=(c == NCH - 1),
                                               skip_group_check=True),
                                           deps=deps if c == 0 else [],
                                           sig="pe" if c == NCH - 1 else None)
                            t_mm = tok
                            if fc == 3 and tg == 3:
                                t_mmlast[s] = t_mm
                            deps = [t_mm]
                            if rowi >= 2:
                                deps += [t_st.get(rowi - 2)]
                            t_ev[key] = p.do("scalar",
                                             lambda e, stg=stg, tg=tg, pp=pp: e.copy(
                                                 out=QSTG[:, stg * 2048 + tg * 512:
                                                          stg * 2048 + (tg + 1) * 512],
                                                 in_=pp[:, :]),
                                             deps=deps, sig="act")
                        if s < 4:
                            dst, r0, c0 = q_dram, s * 512 + fc * 128, half * T
                        elif s < 8:
                            dst, r0, c0 = k_dram, (s - 4) * 512 + fc * 128, half * T
                        else:
                            dst, r0, c0 = v_dram, (s - 8) * 512 + fc * 128, 0
                        t_st[rowi] = p.do("gpsimd",
                                          lambda e, dst=dst, r0=r0, c0=c0,
                                          stg=stg: e.dma_start(
                                              out=dst[r0:r0 + 128, c0:c0 + T],
                                              in_=QSTG[:, stg * 2048:stg * 2048 + 2048]),
                                          deps=[t_ev[rowi * 4 + 3]],
                                          sig=f"sq{rowi % 2}", inc=16)
                qkv_last_store[0] = [("sq0", counts["sq0"]), ("sq1", counts["sq1"])]
                p.run(nc)

        phase_ab(0)
        phase_ab(1)
        qkv_done = qkv_last_store[0]

        t_csum_done = None
        with contextlib.ExitStack() as sc:
            QH = sc.enter_context(nc.sbuf_tensor("QH", [128, 2 * S], BF16))
            KH = sc.enter_context(nc.sbuf_tensor("KH", [128, 2 * S], BF16))
            EOWN = sc.enter_context(nc.sbuf_tensor("EOWN", [128, 4 * T], BF16))
            ESCR = sc.enter_context(nc.sbuf_tensor("ESCR", [128, 2 * T], BF16))
            WDB = (sc.enter_context(nc.sbuf_tensor("WDB", [128, 32], F32))
                   if debug else None)
            sq = [sc.enter_context(nc.psum_tensor(f"sq{i}", [128, 1024], F32))
                  for i in range(3)]
            cs_ps = sc.enter_context(nc.psum_tensor("cs_ps", [128, 512], F32))

            p = Plan(sems, counts, marks, entry_guards=st_guard())
            t_qld, t_kld, t_head, t_csev, t_exp, t_w, t_cs = {}, {}, {}, {}, {}, {}, {}
            t_pr, t_dr, t_tail, t_wtail = {}, {}, {}, {}
            qseq = 0

            def emit_colsum(h, qc):
                gqc = h * 32 + qc
                deps = [t_w[gqc]]
                if qc == 0:
                    deps += [t_csev.get(h - 1)]
                for kc in range(4):
                    tok = p.do("tensor", lambda e, h=h, qc=qc, kc=kc: e.matmul(
                        cs_ps[32 * kc:32 * kc + 1, :],
                        wcolb[:, qc % 2:qc % 2 + 1],
                        EOWN[:, (qc % 4) * T + kc * 512:(qc % 4) * T + (kc + 1) * 512],
                        start=(qc == 0), stop=(qc == 31),
                        tile_position=(0, 32 * kc), skip_group_check=True),
                        deps=deps if kc == 0 else [],
                        sig="pe" if kc == 3 else None)
                t_cs[gqc] = tok

            for h in range(NH):
                hs = (h % 2) * S
                deps = list(qkv_done)
                if h >= 2:
                    deps += [t_head[h - 2]]
                t_qld[h] = p.do("sync", lambda e, h=h, hs=hs: e.dma_start(
                    out=QH[:, hs:hs + S], in_=q_dram[h * 128:(h + 1) * 128, :]),
                    deps=deps, sig=f"lq{h % 2}", inc=16)
                t_kld[h] = p.do("sync", lambda e, h=h, hs=hs: e.dma_start(
                    out=KH[:, hs:hs + S], in_=k_dram[h * 128:(h + 1) * 128, :]),
                    sig=f"lq{h % 2}", inc=16)

                def emit_denw(qc):
                    gqc = h * 32 + qc
                    e_tok = t_exp[(gqc + 1) * 4 - 1]
                    t_dcp = p.do("scalar", lambda e, qc=qc: e.copy(
                        out=dstage[:, (qc % 4) * 4:(qc % 4) * 4 + 4],
                        in_=dparts[:, (qc % 4) * 4:(qc % 4) * 4 + 4]),
                        deps=[e_tok] +
                        ([t_w.get(gqc - 2)] if gqc >= 2 else []), sig="act")
                    p.do("vector", lambda e, qc=qc: e.reduce_sum(
                        denb[:, qc % 2:qc % 2 + 1],
                        dstage[:, (qc % 4) * 4:(qc % 4) * 4 + 4], axis=AXX),
                        deps=[t_dcp])
                    p.do("vector", lambda e, qc=qc: e.reciprocal(
                        wcolf[:, qc % 2:qc % 2 + 1], denb[:, qc % 2:qc % 2 + 1]))
                    t_w[gqc] = p.do("vector", lambda e, qc=qc: e.tensor_copy(
                        out=wcolb[:, qc % 2:qc % 2 + 1],
                        in_=wcolf[:, qc % 2:qc % 2 + 1]),
                        deps=[t_cs.get(gqc - 2)] if gqc >= 2 else [], sig="dve")
                    if debug and h == 0:
                        p.do("vector", lambda e, qc=qc: e.tensor_copy(
                            out=WDB[:, qc:qc + 1],
                            in_=wcolf[:, qc % 2:qc % 2 + 1]))

                for qc in range(32):
                    gqc = h * 32 + qc
                    for q4 in range(4):
                        buf = sq[qseq % 3]
                        deps = [t_qld[h], t_kld[h]]
                        if qseq >= 3:
                            deps += [t_exp[qseq - 3]]
                        p.do("tensor", lambda e, hs=hs, qc=qc, q4=q4, buf=buf: e.matmul(
                            buf[:, 0:512],
                            QH[:, hs + qc * 128:hs + (qc + 1) * 128],
                            KH[:, hs + q4 * 1024:hs + q4 * 1024 + 512],
                            start=True, stop=True),
                            deps=deps)
                        t_sc = p.do("tensor", lambda e, hs=hs, qc=qc, q4=q4,
                                    buf=buf: e.matmul(
                                        buf[:, 512:1024],
                                        QH[:, hs + qc * 128:hs + (qc + 1) * 128],
                                        KH[:, hs + q4 * 1024 + 512:
                                           hs + (q4 + 1) * 1024],
                                        start=True, stop=True),
                                    sig="pe")
                        if q4 < 2:
                            dtile, dsl = EOWN, ((qc % 4) * T + q4 * 1024,
                                                (qc % 4) * T + (q4 + 1) * 1024)
                        else:
                            dtile, dsl = ESCR, ((qc % 2) * T + (q4 - 2) * 1024,
                                                (qc % 2) * T + (q4 - 1) * 1024)
                        deps = [t_sc]
                        if q4 == 0 and gqc >= 4:
                            deps += [t_cs.get(gqc - 4)]
                        t_exp[qseq] = p.do("scalar",
                                           lambda e, dtile=dtile, dsl=dsl, qc=qc,
                                           q4=q4, buf=buf: e.activation(
                                               dtile[:, dsl[0]:dsl[1]], buf[:, :],
                                               AF.Exp,
                                               bias=consts[:, CEXPB:CEXPB + 1],
                                               scale=consts[:, CEXPS:CEXPS + 1],
                                               accum_out=dparts[:, (qc % 4) * 4 + q4:
                                                                (qc % 4) * 4 + q4 + 1]),
                                           deps=deps, sig="act")
                        qseq += 1
                    if qc >= 1:
                        emit_denw(qc - 1)
                    if qc >= 2:
                        emit_colsum(h, qc - 2)
                emit_denw(31)
                emit_colsum(h, 30)
                emit_colsum(h, 31)
                t_head[h] = t_cs[h * 32 + 31]
                t_cp = {}
                for kc in range(4):
                    t_cp[kc] = p.do("vector", lambda e, kc=kc: e.tensor_copy(
                        out=cstage[0:1, kc * 512:(kc + 1) * 512],
                        in_=cs_ps[32 * kc:32 * kc + 1, :]),
                        deps=([t_head[h]] if kc == 0 else []) +
                        ([t_csev.get(h - 1)] if kc == 0 else []),
                        sig="dve" if kc == 3 else None)
                t_csev[h] = p.do("gpsimd", lambda e, h=h: e.dma_start(
                    out=csum_dram[h:h + 1, :], in_=cstage[0:1, :]),
                    deps=[t_cp[3]], sig="sc", inc=16)
            if debug:
                p.do("gpsimd", lambda e: e.dma_start(out=wdbg, in_=WDB[:, :]),
                     deps=[t_w[31]], sig="sc", inc=16)
            t_csum_done = t_csev[NH - 1]
            p.run(nc)

        with contextlib.ExitStack() as sc:
            WF32 = sc.enter_context(nc.sbuf_tensor("WF32d", [128, 2048], F32))
            WOBF = sc.enter_context(nc.sbuf_tensor("WOBF", [128, 4 * 8192], BF16))
            VAL = sc.enter_context(nc.sbuf_tensor("VAL", [128, 8192], BF16))
            VALT = sc.enter_context(nc.sbuf_tensor("VALT", [128, 8192], BF16))
            CBB = sc.enter_context(nc.sbuf_tensor("CBB", [128, 1024], BF16))
            XCHd = sc.enter_context(nc.sbuf_tensor("XCHd", [128, 2 * 2048], F32))
            X2ST = sc.enter_context(nc.sbuf_tensor("X2ST", [128, 2 * 2048], F32))
            X2BF = sc.enter_context(nc.sbuf_tensor("X2BF", [128, 2 * 512], BF16))
            X2SQ = sc.enter_context(nc.sbuf_tensor("X2SQ", [128, 2 * 512], BF16))
            pb = [sc.enter_context(nc.psum_tensor(f"pd{i}", [128, 512], F32))
                  for i in range(8)]

            p = Plan(sems, counts, marks, entry_guards=st_guard())
            t_wc = None
            for s in range(4):
                for piece in range(4):
                    deps = [t_wc] if t_wc else []
                    t_wl = p.do("sync", lambda e, s=s, piece=piece: e.dma_start(
                        out=WF32[:, :],
                        in_=wo_s[s * 128:(s + 1) * 128,
                                 piece * 2048:(piece + 1) * 2048]),
                        deps=deps, sig="lw", inc=16)
                    t_wc = p.do("vector", lambda e, s=s, piece=piece: e.tensor_copy(
                        out=WOBF[:, s * 8192 + piece * 2048:
                                s * 8192 + (piece + 1) * 2048],
                        in_=WF32[:, :]), deps=[t_wl], sig="dve")

            t_vld, t_val, t_x2, t_x2st, t_stat, t_bfd = {}, {}, {}, {}, {}, {}
            t_crl, t_bm = {}, {}
            t_xld = {}
            for gi in range(4):
                for h8 in range(2):
                    deps = list(qkv_done)
                    if gi >= 1:
                        deps += [t_val.get((gi - 1, h8 * 8 + 7))]
                    t_vld[(gi, h8)] = p.do("sync", lambda e, gi=gi, h8=h8: e.dma_start(
                        out=VAL.rearrange("p (c t) -> p c t", t=512)
                        [:, h8 * 8:(h8 + 1) * 8, :],
                        in_=v_dram.rearrange("(c p) t -> p c t", p=128)
                        [:, h8 * 8:(h8 + 1) * 8, gi * 512:(gi + 1) * 512]),
                        deps=deps, sig=f"lv{h8}", inc=16)
                def emit_xld(pc, gi=gi):
                    kk = gi * 4 + pc
                    deps = []
                    if kk >= 2:
                        kp = kk - 2
                        d = t_x2.get((kp // 4, (kp % 4) * 4 + 3))
                        assert d is not None
                        deps += [d]
                    t_xld[kk] = p.do("sync", lambda e, gi=gi, pc=pc: e.dma_start(
                        out=XCHd.rearrange("p (c t) -> p c t", t=512)
                        [:, (pc % 2) * 4:(pc % 2) * 4 + 4, :],
                        in_=xT.rearrange("(c p) t -> p c t", p=128)
                        [:, pc * 4:pc * 4 + 4, gi * 512:(gi + 1) * 512]),
                        deps=deps, sig=f"lp{pc % 2}", inc=16)
                for pc in range(2):
                    emit_xld(pc)
                t_cbd = {}

                def emit_val(h, gi=gi):
                    t_val[(gi, h)] = p.do("vector",
                                          lambda e, h=h: e.tensor_tensor(
                                              VALT[:, h * 512:(h + 1) * 512],
                                              VAL[:, h * 512:(h + 1) * 512],
                                              CBB[:, (h % 2) * 512:(h % 2 + 1) * 512],
                                              op=ALU.mult),
                                          deps=[t_cbd[h], t_cbd.get(h + 1),
                                                t_vld[(gi, h // 8)]], sig="dve")

                for h in range(NH):
                    deps = [t_csum_done]
                    if h >= 2:
                        deps += [t_bm.get((gi, h - 2))]
                    elif gi >= 1:
                        deps += [t_bm.get((gi - 1, NH - 2 + h))]
                    t_crl[(gi, h)] = p.do("sync", lambda e, gi=gi, h=h: e.dma_start(
                        out=csrow[0:1, (h % 2) * 512:(h % 2 + 1) * 512],
                        in_=csum_dram[h:h + 1, gi * 512:(gi + 1) * 512]),
                        deps=deps, sig=f"lc{h % 2}", inc=16)
                    deps = [t_crl[(gi, h)], t_wc]
                    if h >= 2:
                        deps += [t_val.get((gi, h - 2))]
                    elif gi >= 1:
                        deps += [t_val.get((gi - 1, NH - 2 + h))]
                    t_bm[(gi, h)] = p.do("tensor", lambda e, h=h: e.matmul(
                        pb[h % 2][:, :], ones_row[:],
                        csrow[0:1, (h % 2) * 512:(h % 2 + 1) * 512],
                        start=True, stop=True), deps=deps, sig="pe")
                    t_cbd[h] = p.do("scalar", lambda e, h=h: e.copy(
                        out=CBB[:, (h % 2) * 512:(h % 2 + 1) * 512],
                        in_=pb[h % 2][:, :]), deps=[t_bm[(gi, h)]], sig="act")
                    if h >= 1:
                        emit_val(h - 1)
                emit_val(NH - 1)
                for oc in range(NCH):
                    k = gi * NCH + oc
                    pp = pb[2 + k % 4 if k % 4 < 2 else 4 + k % 4]
                    deps = [t_val[(gi, NH - 1)]]
                    if k >= 4:
                        deps += [t_x2.get(((k - 4) // NCH, (k - 4) % NCH))]
                    for c in range(NCH):
                        tok = p.do("tensor", lambda e, oc=oc, c=c, pp=pp: e.matmul(
                            pp[:, :],
                            WOBF[:, (oc // 4) * 8192 + c * 512 + (oc % 4) * 128:
                                 (oc // 4) * 8192 + c * 512 + (oc % 4) * 128 + 128],
                            VALT[:, c * 512:(c + 1) * 512],
                            start=(c == 0), stop=(c == NCH - 1),
                            skip_group_check=True),
                            deps=deps if c == 0 else [],
                            sig="pe" if c == NCH - 1 else None)
                    t_amm = tok
                    slot4 = (oc // 4) % 2
                    deps = [t_amm, t_xld[gi * 4 + oc // 4]]
                    if k >= 8:
                        deps += [t_x2st.get(((k - 8) // NCH, ((k - 8) % NCH) // 4))]
                    t_x2[(gi, oc)] = p.do("vector",
                                          lambda e, oc=oc, pp=pp, slot4=slot4:
                                          e.tensor_tensor(
                                              X2ST[:, slot4 * 2048 + (oc % 4) * 512:
                                                   slot4 * 2048 + (oc % 4 + 1) * 512],
                                              XCHd[:, ((oc // 4) % 2) * 2048 +
                                                   (oc % 4) * 512:
                                                   ((oc // 4) % 2) * 2048 +
                                                   (oc % 4 + 1) * 512],
                                              pp[:, :], op=ALU.add),
                                          deps=deps, sig="dve")
                    deps = [t_x2[(gi, oc)]]
                    if k >= 2:
                        deps += [t_stat.get(k - 2)]
                    t_bfd[k] = p.do("vector", lambda e, oc=oc, slot4=slot4:
                                    e.tensor_copy(
                                        out=X2BF[:, (oc % 2) * 512:(oc % 2 + 1) * 512],
                                        in_=X2ST[:, slot4 * 2048 + (oc % 4) * 512:
                                                 slot4 * 2048 + (oc % 4 + 1) * 512]),
                                    deps=deps, sig="dve")
                    t_sq2 = p.do("gpsimd", lambda e, oc=oc: e.tensor_tensor(
                        out=X2SQ[:, (oc % 2) * 512:(oc % 2 + 1) * 512],
                        in0=X2BF[:, (oc % 2) * 512:(oc % 2 + 1) * 512],
                        in1=X2BF[:, (oc % 2) * 512:(oc % 2 + 1) * 512], op=ALU.mult),
                        deps=[t_bfd[k]], sig="pool")
                    p.do("tensor", lambda e, oc=oc: e.matmul(
                        pb[4][0:1, :], ones_col[:],
                        X2BF[:, (oc % 2) * 512:(oc % 2 + 1) * 512],
                        start=(oc == 0), stop=(oc == NCH - 1),
                        skip_group_check=True), deps=[t_bfd[k]])
                    t_stat[k] = p.do("tensor", lambda e, oc=oc: e.matmul(
                        pb[5][0:1, :], ones_col[:],
                        X2SQ[:, (oc % 2) * 512:(oc % 2 + 1) * 512],
                        start=(oc == 0), stop=(oc == NCH - 1),
                        skip_group_check=True), deps=[t_sq2], sig="pe")
                    if oc % 4 == 3 and oc // 4 < 2:
                        emit_xld(oc // 4 + 2)
                    if oc % 4 == 3:
                        t_x2st[(gi, oc // 4)] = p.do(
                            "gpsimd",
                            lambda e, gi=gi, oc=oc, slot4=slot4: e.dma_start(
                                out=x2_dram.rearrange("(c p) t -> p c t", p=128)
                                [:, oc - 3:oc + 1, gi * 512:(gi + 1) * 512],
                                in_=X2ST.rearrange("p (c t) -> p c t", t=512)
                                [:, slot4 * 4:slot4 * 4 + 4, :]),
                            deps=[t_x2[(gi, oc)]], sig=f"s2{slot4}", inc=16)
                p.do("vector", lambda e, gi=gi: e.tensor_copy(
                    out=srow[0:1, gi * 512:(gi + 1) * 512], in_=pb[4][0:1, :]),
                    deps=[t_stat[gi * NCH + NCH - 1]], sig="dve")
                p.do("vector", lambda e, gi=gi: e.tensor_copy(
                    out=s2row[0:1, gi * 512:(gi + 1) * 512], in_=pb[5][0:1, :]),
                    deps=[t_stat[gi * NCH + NCH - 1]], sig="dve")
            t_x2_all = None
            x2_done = [("s20", counts["s20"]), ("s21", counts["s21"])]
            if debug:
                p.do("gpsimd", lambda e: e.dma_start(out=wobf_dbg, in_=WOBF[:, :]),
                     deps=x2_done, sig="sc", inc=16)
                p.do("gpsimd", lambda e: e.dma_start(out=valt_dbg, in_=VALT[:, :]),
                     sig="sc", inc=16)

            p.do("vector", lambda e: e.tensor_scalar_mul(
                mrow[0:1, :], srow[0:1, :], 1.0 / D))
            p.do("vector", lambda e: e.tensor_scalar_mul(
                vrow[0:1, :], s2row[0:1, :], 1.0 / D))
            p.do("vector", lambda e: e.tensor_tensor(
                srow[0:1, :], mrow[0:1, :], mrow[0:1, :], op=ALU.mult))
            t_v2 = p.do("vector", lambda e: e.tensor_tensor(
                vrow[0:1, :], vrow[0:1, :], srow[0:1, :], op=ALU.subtract),
                sig="dve")
            t_ln2 = p.do("scalar", lambda e: e.activation(
                s2row[0:1, :], vrow[0:1, :], AF.Ln,
                bias=consts[0:1, CEPS:CEPS + 1]), deps=[t_v2], sig="act")
            t_rstd2 = p.do("scalar", lambda e: e.activation(
                rstdrow[0:1, :], s2row[0:1, :], AF.Exp,
                scale=consts[0:1, CNHALF:CNHALF + 1]), deps=[t_ln2], sig="act")
            t_mrstd2 = p.do("vector", lambda e: e.tensor_tensor(
                mrstdrow[0:1, :], mrow[0:1, :], rstdrow[0:1, :], op=ALU.mult),
                deps=[t_rstd2], sig="dve")
            p.run(nc)

        t_hst = {}
        with contextlib.ExitStack() as sc:
            XLNT = sc.enter_context(nc.sbuf_tensor("XLNT2", [128, NCH * T], BF16))
            XCH = sc.enter_context(nc.sbuf_tensor("XCH2", [128, 2 * T], F32))
            NRM = sc.enter_context(nc.sbuf_tensor("NRM2", [128, 512], F32))
            WGU8 = sc.enter_context(nc.sbuf_tensor("WGU8", [128, 2 * 8192], U8))
            WGBF = sc.enter_context(nc.sbuf_tensor("WGBF", [128, 8192], BF16))
            WUBF = sc.enter_context(nc.sbuf_tensor("WUBF", [128, 8192], BF16))
            GT = sc.enter_context(nc.sbuf_tensor("GT", [128, 2 * 512], BF16))
            HST = sc.enter_context(nc.sbuf_tensor("HST", [128, 2 * 2048], BF16))
            pb = [sc.enter_context(nc.psum_tensor(f"pq{i}", [128, 512], F32))
                  for i in range(8)]

            p = Plan(sems, counts, marks, entry_guards=st_guard())
            t_bc_done = ln_bcasts(p, t_rstd2, t_mrstd2, None, pb)
            t_norm = {}
            for i in range(NCH):
                par = i % 2
                po, pe_ = par * T, (par + 1) * T
                deps = [t_bc_done] + x2_done
                if i >= 2:
                    deps += [t_norm[i - 2]]
                t_l2 = p.do("sync", lambda e, i=i, po=po, pe_=pe_: e.dma_start(
                    out=XCH[:, po:pe_], in_=x2_dram[i * 128:(i + 1) * 128, :]),
                    deps=deps, sig=f"lx{i % 2}", inc=16)
                for g4 in range(4):
                    p.do("vector", lambda e, po=po, g4=g4: e.tensor_tensor(
                        NRM[:, :], XCH[:, po + g4 * 512:po + (g4 + 1) * 512],
                        pb[2 * g4][:, :], op=ALU.mult),
                        deps=[t_l2] if g4 == 0 else [])
                    tok = p.do("vector", lambda e, i=i, g4=g4: e.tensor_tensor(
                        XLNT[:, i * T + g4 * 512:i * T + (g4 + 1) * 512],
                        NRM[:, :], pb[2 * g4 + 1][:, :], op=ALU.subtract),
                        sig="dve" if g4 == 3 else None)
                t_norm[i] = tok

            t_gc, t_uc, t_mmg, t_mmu, t_h = {}, {}, {}, {}, {}
            pend = {}

            def emit_h(kk):
                tok, t_silu, cc, gi, ppu = pend[kk]
                deps = [tok, t_silu, t_scl]
                if gi == 0 and cc >= 2:
                    deps += [t_hst.get(cc - 2)]
                t_h[kk] = p.do("vector",
                               lambda e, cc=cc, gi=gi, ppu=ppu:
                               e.scalar_tensor_tensor(
                                   HST[:, (cc % 2) * 2048 + gi * 512:
                                       (cc % 2) * 2048 + (gi + 1) * 512],
                                   ppu[:, :], su[:, cc:cc + 1],
                                   GT[:, (gi % 2) * 512:(gi % 2 + 1) * 512],
                                   op0=ALU.mult, op1=ALU.mult),
                               deps=deps, sig="dve")

            for s in range(11):
                deps = [t_gc[s - 1]] if s >= 1 else []
                t_gl = p.do("sync", lambda e, s=s: e.dma_start(
                    out=WGU8[:, 0:8192], in_=wg_s[s * 128:(s + 1) * 128, :]),
                    deps=deps, sig="lg", inc=16)
                deps = [t_gl] + ([t_mmg[s - 1]] if s >= 1 else [])
                t_gc[s] = p.do("vector", lambda e: e.tensor_scalar_add(
                    WGBF[:, :], WGU8[:, 0:8192], -128.0), deps=deps, sig="dve")
                deps = [t_uc[s - 1]] if s >= 1 else []
                t_ul = p.do("sync", lambda e, s=s: e.dma_start(
                    out=WGU8[:, 8192:16384], in_=wu_s[s * 128:(s + 1) * 128, :]),
                    deps=deps, sig="lu", inc=16)
                deps = [t_ul] + ([t_mmu[s - 1]] if s >= 1 else [])
                t_uc[s] = p.do("vector", lambda e: e.tensor_scalar_add(
                    WUBF[:, :], WGU8[:, 8192:16384], -128.0), deps=deps, sig="dve")
                for hc in range(4):
                    cc = s * 4 + hc
                    for gi in range(4):
                        kk = cc * 4 + gi
                        ppg = pb[gi % 2]
                        ppu = pb[2 + gi % 2]
                        deps = [t_gc[s], t_norm[NCH - 1]]
                        if kk >= 2:
                            deps += [t_h.get(kk - 2)]
                        for c in range(NCH):
                            tok = p.do("tensor", lambda e, hc=hc, gi=gi, c=c,
                                       ppg=ppg: e.matmul(
                                           ppg[:, :],
                                           WGBF[:, c * 512 + hc * 128:
                                                c * 512 + (hc + 1) * 128],
                                           XLNT[:, c * T + gi * 512:
                                                c * T + (gi + 1) * 512],
                                           start=(c == 0), stop=(c == NCH - 1),
                                           skip_group_check=True),
                                       deps=deps if c == 0 else [],
                                       sig="pe" if c == NCH - 1 else None)
                        t_mmg[s] = tok
                        t_silu = p.do("scalar", lambda e, cc=cc, gi=gi, ppg=ppg:
                                      e.activation(
                                          GT[:, (gi % 2) * 512:(gi % 2 + 1) * 512],
                                          ppg[:, :], AF.Silu, scale=sg[:, cc:cc + 1]),
                                      deps=[tok, t_scl], sig="act")
                        deps = [t_uc[s]]
                        for c in range(NCH):
                            tok = p.do("tensor", lambda e, hc=hc, gi=gi, c=c,
                                       ppu=ppu: e.matmul(
                                           ppu[:, :],
                                           WUBF[:, c * 512 + hc * 128:
                                                c * 512 + (hc + 1) * 128],
                                           XLNT[:, c * T + gi * 512:
                                                c * T + (gi + 1) * 512],
                                           start=(c == 0), stop=(c == NCH - 1),
                                           skip_group_check=True),
                                       deps=deps if c == 0 else [],
                                       sig="pe" if c == NCH - 1 else None)
                        t_mmu[s] = tok
                        pend[kk] = (tok, t_silu, cc, gi, ppu)
                        if kk >= 1:
                            emit_h(kk - 1)
                    if cc == 43:
                        emit_h(cc * 4 + 3)
                    if cc >= 1:
                        t_hst[cc - 1] = p.do("gpsimd", lambda e, cc=cc: e.dma_start(
                            out=h_dram[(cc - 1) * 128:cc * 128, :],
                            in_=HST[:, ((cc - 1) % 2) * 2048:
                                    ((cc - 1) % 2) * 2048 + 2048]),
                            deps=[t_h[(cc - 1) * 4 + 3]], sig=f"sh{(cc - 1) % 2}",
                            inc=16)
                    if cc == 43:
                        t_hst[43] = p.do("gpsimd", lambda e: e.dma_start(
                            out=h_dram[43 * 128:44 * 128, :],
                            in_=HST[:, (43 % 2) * 2048:(43 % 2) * 2048 + 2048]),
                            deps=[t_h[43 * 4 + 3]], sig="sh1", inc=16)
            h_done = [("sh0", counts["sh0"]), ("sh1", counts["sh1"])]
            p.run(nc)

        with contextlib.ExitStack() as sc:
            WDU8 = sc.enter_context(nc.sbuf_tensor("WDU8", [128, 2 * 5632], U8))
            WDBF = sc.enter_context(nc.sbuf_tensor("WDBF", [128, 22528], BF16))
            HT = sc.enter_context(nc.sbuf_tensor("HT", [128, 2 * 11264], BF16))
            X2CH = sc.enter_context(nc.sbuf_tensor("X2CH", [128, 2 * 2048], F32))
            OUTS = sc.enter_context(nc.sbuf_tensor("OUTS", [128, 2 * 2048], F32))
            pb = [sc.enter_context(nc.psum_tensor(f"pf{i}", [128, 512], F32))
                  for i in range(4)]

            p = Plan(sems, counts, marks, entry_guards=st_guard())
            t_out, t_ost, t_f, t_htld, t_x2ld, t_wdc = {}, {}, {}, {}, {}, {}
            t_lastread = {}
            for dg in range(4):
                for piece in range(4):
                    kk = dg * 4 + piece
                    deps = [t_scl]
                    if kk >= 2:
                        deps += [t_wdc.get(kk - 2)]
                    t_wl = p.do("sync", lambda e, dg=dg, piece=piece: e.dma_start(
                        out=WDU8[:, (piece % 2) * 5632:(piece % 2 + 1) * 5632],
                        in_=wd_s[dg * 128:(dg + 1) * 128,
                                 piece * 5632:(piece + 1) * 5632]),
                        deps=deps, sig=f"lwd{piece % 2}", inc=16)
                    deps = [t_wl]
                    if dg >= 1 and piece == 0:
                        deps += [t_f.get((dg * 4 - 1, 3))]
                    t_wdc[kk] = p.do("vector",
                                     lambda e, piece=piece: e.tensor_scalar_add(
                                         WDBF[:, piece * 5632:(piece + 1) * 5632],
                                         WDU8[:, (piece % 2) * 5632:
                                              (piece % 2 + 1) * 5632],
                                         -128.0), deps=deps, sig="dve")
                for tg in range(4):
                    it = dg * 4 + tg
                    for hb in range(2):
                        for j in range(6):
                            n = min(4, 22 - j * 4)
                            hc0 = hb * 22 + j * 4
                            deps = list(h_done)
                            if it >= 1:
                                deps += [t_lastread.get((it - 1, hb))]
                            t_htld[(it, hb)] = p.do(
                                "sync",
                                lambda e, hc0=hc0, n=n, hb=hb, j=j, tg=tg: e.dma_start(
                                    out=HT.rearrange("p (c t) -> p c t", t=512)
                                    [:, hb * 22 + j * 4:hb * 22 + j * 4 + n, :],
                                    in_=h_dram.rearrange("(c p) t -> p c t", p=128)
                                    [:, hc0:hc0 + n, tg * 512:(tg + 1) * 512]),
                                deps=deps if j == 0 else [], sig=f"lh{hb}", inc=16)
                    deps = []
                    if it >= 2:
                        deps += [t_out.get((it - 2, 3))]
                    t_x2ld[it] = p.do("sync", lambda e, dg=dg, tg=tg, it=it: e.dma_start(
                        out=X2CH.rearrange("p (c t) -> p c t", t=512)
                        [:, (it % 2) * 4:(it % 2) * 4 + 4, :],
                        in_=x2_dram.rearrange("(c p) t -> p c t", p=128)
                        [:, dg * 4:dg * 4 + 4, tg * 512:(tg + 1) * 512]),
                        deps=deps + x2_done, sig=f"l2{it % 2}", inc=16)
                    for hb in range(2):
                        for hc22 in range(22):
                            hc = hb * 22 + hc22
                            for dc in range(4):
                                deps = []
                                if hb == 0 and hc22 == 0:
                                    deps += [t_wdc[dg * 4 + 3]]
                                    if it >= 1:
                                        deps += [t_out.get((it - 1, dc))]
                                if hc22 == 0 and dc == 0:
                                    deps += [t_htld[(it, hb)]]
                                sig = None
                                if hc == 43 or (hc22 == 21 and dc == 3):
                                    sig = "pe"
                                tok = p.do("tensor",
                                           lambda e, hc=hc, dc=dc, hb=hb,
                                           hc22=hc22: e.matmul(
                                               pb[dc][:, :],
                                               WDBF[:, hc * 512 + dc * 128:
                                                    hc * 512 + (dc + 1) * 128],
                                               HT[:, hb * 11264 + hc22 * 512:
                                                  hb * 11264 + (hc22 + 1) * 512],
                                               start=(hc == 0), stop=(hc == 43),
                                               skip_group_check=True),
                                           deps=deps, sig=sig)
                                if hc22 == 21 and dc == 3:
                                    t_lastread[(it, hb)] = tok
                                if hc == 43:
                                    t_f[(it, dc)] = tok
                    for dc in range(4):
                        deps = [t_f[(it, dc)], t_x2ld[it], t_scl]
                        if it >= 2:
                            deps += [t_ost.get(it - 2)]
                        t_out[(it, dc)] = p.do("vector",
                                               lambda e, dg=dg, dc=dc, it=it:
                                               e.scalar_tensor_tensor(
                                                   OUTS[:, (it % 2) * 2048 + dc * 512:
                                                        (it % 2) * 2048 +
                                                        (dc + 1) * 512],
                                                   pb[dc][:, :],
                                                   sd[:, dg * 4 + dc:dg * 4 + dc + 1],
                                                   X2CH[:, (it % 2) * 2048 + dc * 512:
                                                        (it % 2) * 2048 +
                                                        (dc + 1) * 512],
                                                   op0=ALU.mult, op1=ALU.add),
                                               deps=deps, sig="dve")
                    t_ost[it] = p.do("gpsimd", lambda e, dg=dg, tg=tg, it=it: e.dma_start(
                        out=outT.rearrange("(c p) t -> p c t", p=128)
                        [:, dg * 4:dg * 4 + 4, tg * 512:(tg + 1) * 512],
                        in_=OUTS.rearrange("p (c t) -> p c t", t=512)
                        [:, (it % 2) * 4:(it % 2) * 4 + 4, :]),
                        deps=[t_out[(it, 3)]], sig=f"so{it % 2}", inc=16)
            p.do("gpsimd", lambda e: e.nop(),
                 deps=[("so0", counts["so0"]), ("so1", counts["so1"])])
            p.run(nc)

    return nc


_NC_CACHE = None


def kernel(L0_attn_x, L0_attn_w_qkv, L0_attn_w_o, L0_ffn_wq_gate, L0_ffn_s_gate,
           L0_ffn_wq_up, L0_ffn_s_up, L0_ffn_wq_down, L0_ffn_s_down):
    global _NC_CACHE
    x = np.asarray(L0_attn_x)
    w_qkv = np.ascontiguousarray(np.asarray(L0_attn_w_qkv), dtype=np.float32)
    w_o = np.ascontiguousarray(np.asarray(L0_attn_w_o), dtype=np.float32)

    wqkv_s = np.ascontiguousarray(
        w_qkv.reshape(16, 128, 12, 512).transpose(2, 1, 0, 3).reshape(12 * 128, 8192))
    wo_s = np.ascontiguousarray(
        w_o.reshape(16, 128, 4, 512).transpose(2, 1, 0, 3).reshape(4 * 128, 8192))

    def u8view(a):
        a = np.ascontiguousarray(np.asarray(a, dtype=np.int32))
        return a.view(np.uint8)[:, ::4]

    wg8 = u8view(L0_ffn_wq_gate)
    wu8 = u8view(L0_ffn_wq_up)
    wd8 = u8view(L0_ffn_wq_down)
    wg_s = np.ascontiguousarray(
        wg8.reshape(11, 512, 16, 128).transpose(0, 3, 2, 1).reshape(11 * 128, 8192))
    wu_s = np.ascontiguousarray(
        wu8.reshape(11, 512, 16, 128).transpose(0, 3, 2, 1).reshape(11 * 128, 8192))
    wd_s = np.ascontiguousarray(
        wd8.reshape(4, 512, 44, 128).transpose(0, 3, 2, 1).reshape(4 * 128, 22528))
    sg_s = np.ascontiguousarray(
        np.asarray(L0_ffn_s_gate, dtype=np.float32).reshape(44, 128).T)
    su_s = np.ascontiguousarray(
        np.asarray(L0_ffn_s_up, dtype=np.float32).reshape(44, 128).T)
    sd_s = np.ascontiguousarray(
        np.asarray(L0_ffn_s_down, dtype=np.float32).reshape(16, 128).T)

    in_maps = []
    for c in range(8):
        b, hf = c // 2, c % 2
        own = x[b, hf * T:(hf + 1) * T]
        par = x[b, (1 - hf) * T:(2 - hf) * T]
        xT_c = np.ascontiguousarray(
            np.concatenate([own, par], axis=0).T.astype(np.float32, copy=False))
        in_maps.append({
            "xT": xT_c, "wqkv_s": wqkv_s, "wo_s": wo_s,
            "wg_s": wg_s, "wu_s": wu_s, "wd_s": wd_s,
            "sg_s": sg_s, "su_s": su_s, "sd_s": sd_s,
        })

    if _NC_CACHE is None:
        _NC_CACHE = build_program()
    res = run_bass_kernel_spmd(_NC_CACHE, in_maps, core_ids=list(range(8)))

    out = np.empty((4, S, D), dtype=np.float32)
    for c in range(8):
        b, hf = c // 2, c % 2
        out[b, hf * T:(hf + 1) * T, :] = res.results[c]["outT"].T
    return out


# version 07
# speedup vs baseline: 1.3332x; 1.0653x over previous
import contextlib
import numpy as np
import concourse.bass as bass
import concourse.mybir as mybir
from concourse.bass_utils import run_bass_kernel_spmd

F32 = mybir.dt.float32
BF16 = mybir.dt.bfloat16
U8 = mybir.dt.uint8
AF = mybir.ActivationFunctionType
ALU = mybir.AluOpType
AXX = mybir.AxisListType.X

D = 2048
S = 4096
T = 2048
HID = 5632
NH = 16
NCH = D // 128
EXPS = float(1.0 / np.sqrt(128.0))
EXPB = -12.0
EPS = 1e-5

ENGS = ("sync", "vector", "scalar", "tensor", "gpsimd")
SELF_SYNC = False


class Plan:

    def __init__(self, sems, counts, marks, entry_guards=()):
        self.sems = sems
        self.counts = counts
        self.marks = marks
        self.ev = {e: [] for e in ENGS}
        self.entry = {e: list(entry_guards) for e in ENGS}

    def do(self, eng, fn, deps=(), sig=None, inc=1):
        evl = self.ev[eng]
        mk = self.marks[eng]
        alldeps = self.entry[eng] + list(deps)
        self.entry[eng] = []
        order = {"vector": "dve", "scalar": "act"}.get(eng) if SELF_SYNC else None
        if SELF_SYNC and eng == "gpsimd" and inc != 16:
            order = "pool"
        if order is not None:
            alldeps = alldeps + [(order, self.counts.get(order, 0))]
            if sig is None:
                sig = order
            assert sig == order, (eng, sig)
        best = {}
        for d in alldeps:
            if d is None:
                continue
            s, v = d
            if v > best.get(s, 0):
                best[s] = v
        for s, v in best.items():
            if v > 0 and mk.get(s, 0) < v:
                evl.append(("w", s, v))
                mk[s] = v
        tok = None
        if sig is not None:
            self.counts[sig] = self.counts.get(sig, 0) + inc
            tok = (sig, self.counts[sig])
        evl.append(("d", fn, sig, inc))
        return tok

    def run(self, nc):
        with nc.Block() as blk:
            for eng in ENGS:
                items = self.ev[eng]
                if not items:
                    continue

                def closure(e, items=items):
                    for it in items:
                        if it[0] == "w":
                            e.wait_ge(self.sems[it[1]], it[2])
                        else:
                            ins = it[1](e)
                            if it[2] is not None:
                                ins.then_inc(self.sems[it[2]], it[3])

                getattr(blk, eng)(closure)


def build_program(debug=False):
    nc = bass.Bass("TRN2", target_bir_lowering=False)
    KIND = "ExternalOutput" if debug else "Internal"

    xT = nc.dram_tensor("xT", [D, S], F32, kind="ExternalInput").ap()
    wqkv_s = nc.dram_tensor("wqkv_s", [12 * 128, 8192], F32, kind="ExternalInput").ap()
    wo_s = nc.dram_tensor("wo_s", [4 * 128, 8192], F32, kind="ExternalInput").ap()
    wg_s = nc.dram_tensor("wg_s", [11 * 128, 8192], U8, kind="ExternalInput").ap()
    wu_s = nc.dram_tensor("wu_s", [11 * 128, 8192], U8, kind="ExternalInput").ap()
    wd_s = nc.dram_tensor("wd_s", [4 * 128, 22528], U8, kind="ExternalInput").ap()
    sg_s = nc.dram_tensor("sg_s", [128, 44], F32, kind="ExternalInput").ap()
    su_s = nc.dram_tensor("su_s", [128, 44], F32, kind="ExternalInput").ap()
    sd_s = nc.dram_tensor("sd_s", [128, 16], F32, kind="ExternalInput").ap()

    q_dram = nc.dram_tensor("q_dram", [D, S], BF16, kind=KIND).ap()
    k_dram = nc.dram_tensor("k_dram", [D, S], BF16, kind=KIND).ap()
    v_dram = nc.dram_tensor("v_dram", [D, T], BF16, kind=KIND).ap()
    x2_dram = nc.dram_tensor("x2_dram", [D, T], F32, kind=KIND).ap()
    h_dram = nc.dram_tensor("h_dram", [HID, T], BF16, kind=KIND).ap()
    csum_dram = nc.dram_tensor("csum_dram", [NH, T], F32, kind=KIND).ap()

    outT = nc.dram_tensor("outT", [D, T], F32, kind="ExternalOutput").ap()
    wobf_dbg = (nc.dram_tensor("wobf_dbg", [128, 4 * 8192], BF16, kind="ExternalOutput").ap()
                if debug else None)
    wdbg = (nc.dram_tensor("wdbg", [128, 32], F32, kind="ExternalOutput").ap()
            if debug else None)
    valt_dbg = (nc.dram_tensor("valt_dbg", [128, 8192], BF16, kind="ExternalOutput").ap()
                if debug else None)

    counts = {}
    marks = {e: {} for e in ENGS}

    def st_guard():
        return [(s, counts.get(s, 0)) for s in
                ("sq0", "sq1", "sc", "s20", "s21", "sh0", "sh1", "so0", "so1")]

    with contextlib.ExitStack() as g:
        semnames = ("pe", "act", "dve", "pool", "ldi", "lx0", "lx1", "lw",
                    "sq0", "sq1", "lq0", "lq1", "sc", "lv0", "lv1", "lp0",
                    "lp1", "lc0", "lc1", "s20", "s21", "lg", "lu", "sh0",
                    "sh1", "lwd0", "lwd1", "lh0", "lh1", "l20", "l21",
                    "so0", "so1")
        sems = {n: g.enter_context(nc.semaphore(n)) for n in semnames}
        STORE_SEMS = ("sq0", "sq1", "sc", "s20", "s21", "sh0", "sh1",
                      "so0", "so1")

        ones_col = g.enter_context(nc.sbuf_tensor("ones_col", [128, 1], BF16))
        ones_row = g.enter_context(nc.sbuf_tensor("ones_row", [1, 128], F32))
        consts = g.enter_context(nc.sbuf_tensor("consts", [128, 4], F32))
        sg = g.enter_context(nc.sbuf_tensor("sg", [128, 44], F32))
        su = g.enter_context(nc.sbuf_tensor("su", [128, 44], F32))
        sd = g.enter_context(nc.sbuf_tensor("sd", [128, 16], F32))
        srow = g.enter_context(nc.sbuf_tensor("srow", [1, T], F32))
        s2row = g.enter_context(nc.sbuf_tensor("s2row", [1, T], F32))
        mrow = g.enter_context(nc.sbuf_tensor("mrow", [1, T], F32))
        rstdrow = g.enter_context(nc.sbuf_tensor("rstdrow", [1, T], F32))
        mrstdrow = g.enter_context(nc.sbuf_tensor("mrstdrow", [1, T], F32))
        cstage = g.enter_context(nc.sbuf_tensor("cstage", [1, T], F32))
        vrow = cstage
        csrow = g.enter_context(nc.sbuf_tensor("csrow", [1, 2 * 512], F32))
        dparts = g.enter_context(nc.sbuf_tensor("dparts", [128, 16], F32))
        dstage = g.enter_context(nc.sbuf_tensor("dstage", [128, 16], F32))
        actscr = g.enter_context(nc.sbuf_tensor("actscr", [128, 4], F32))
        denb = g.enter_context(nc.sbuf_tensor("denb", [128, 2], F32))
        povn = g.enter_context(nc.sbuf_tensor("povn", [128, 2], F32))
        pscr = g.enter_context(nc.sbuf_tensor("pscr", [128, 2], F32))
        wcolf = g.enter_context(nc.sbuf_tensor("wcolf", [128, 2], F32))
        wcolb = g.enter_context(nc.sbuf_tensor("wcolb", [128, 2], BF16))

        CEXPB, CEXPS, CEPS, CNHALF = 0, 1, 2, 3

        p = Plan(sems, counts, marks)
        t_ones = p.do("vector", lambda e: e.memset(ones_col[:], 1.0))
        p.do("vector", lambda e: e.memset(ones_row[:], 1.0))
        p.do("vector", lambda e: e.memset(consts[:, CEXPB:CEXPB + 1], EXPB))
        p.do("vector", lambda e: e.memset(consts[:, CEXPS:CEXPS + 1], EXPS))
        p.do("vector", lambda e: e.memset(consts[:, CEPS:CEPS + 1], EPS))
        t_cn = p.do("vector", lambda e: e.memset(consts[:, CNHALF:CNHALF + 1], -0.5),
                    sig="dve")
        p.do("sync", lambda e: e.dma_start(out=sg[:], in_=sg_s), sig="ldi", inc=16)
        p.do("sync", lambda e: e.dma_start(out=su[:], in_=su_s), sig="ldi", inc=16)
        t_scl = p.do("sync", lambda e: e.dma_start(out=sd[:], in_=sd_s), sig="ldi", inc=16)
        p.run(nc)

        def ln_rows(p, t_sum_done, t_sq_done, pb_s, pb_q):
            t_rows = []
            for gi in range(4):
                t_rows.append(p.do("vector", lambda e, gi=gi: e.tensor_copy(
                    out=srow[0:1, gi * 512:(gi + 1) * 512], in_=pb_s[gi][0:1, :]),
                    deps=[t_sum_done], sig="dve"))
                t_rows.append(p.do("vector", lambda e, gi=gi: e.tensor_copy(
                    out=s2row[0:1, gi * 512:(gi + 1) * 512], in_=pb_q[gi][0:1, :]),
                    deps=[t_sq_done], sig="dve"))
            p.do("vector", lambda e: e.tensor_scalar_mul(
                mrow[0:1, :], srow[0:1, :], 1.0 / D), deps=[t_rows[-1]])
            p.do("vector", lambda e: e.tensor_scalar_mul(
                vrow[0:1, :], s2row[0:1, :], 1.0 / D))
            p.do("vector", lambda e: e.tensor_tensor(
                srow[0:1, :], mrow[0:1, :], mrow[0:1, :], op=ALU.mult))
            t_var = p.do("vector", lambda e: e.tensor_tensor(
                vrow[0:1, :], vrow[0:1, :], srow[0:1, :], op=ALU.subtract),
                sig="dve")
            t_ln = p.do("scalar", lambda e: e.activation(
                s2row[0:1, :], vrow[0:1, :], AF.Ln,
                bias=consts[0:1, CEPS:CEPS + 1]), deps=[t_var], sig="act")
            t_rstd = p.do("scalar", lambda e: e.activation(
                rstdrow[0:1, :], s2row[0:1, :], AF.Exp,
                scale=consts[0:1, CNHALF:CNHALF + 1]), deps=[t_ln], sig="act")
            t_mrstd = p.do("vector", lambda e: e.tensor_tensor(
                mrstdrow[0:1, :], mrow[0:1, :], rstdrow[0:1, :], op=ALU.mult),
                deps=[t_rstd], sig="dve")
            return t_rows[-1], t_rstd, t_mrstd

        def ln_bcasts(p, t_rstd, t_mrstd, t_banks_free, pb):
            t_bc = []
            for gi in range(4):
                tm = p.do("tensor", lambda e, gi=gi: e.matmul(
                    pb[2 * gi][:, :], ones_row[:],
                    rstdrow[0:1, gi * 512:(gi + 1) * 512], start=True, stop=True),
                    deps=[t_rstd, t_banks_free], sig="pe")
                t_bc.append(tm)
                tm = p.do("tensor", lambda e, gi=gi: e.matmul(
                    pb[2 * gi + 1][:, :], ones_row[:],
                    mrstdrow[0:1, gi * 512:(gi + 1) * 512], start=True, stop=True),
                    deps=[t_mrstd], sig="pe")
                t_bc.append(tm)
            return t_bc[-1]

        qkv_last_store = [None]

        def phase_ab(half):
            with contextlib.ExitStack() as sc:
                XLNT = sc.enter_context(nc.sbuf_tensor(f"XLNT{half}", [128, NCH * T], BF16))
                XCH = sc.enter_context(nc.sbuf_tensor(f"XCH{half}", [128, 2 * T], F32))
                NRM = sc.enter_context(nc.sbuf_tensor(f"NRM{half}", [128, 512], F32))
                WF32 = sc.enter_context(nc.sbuf_tensor(f"WF32{half}", [128, 4096], F32))
                WBF = sc.enter_context(nc.sbuf_tensor(f"WBF{half}", [128, 2 * 8192], BF16))
                QSTG = sc.enter_context(nc.sbuf_tensor(f"QSTG{half}", [128, 2 * 2048], BF16))
                pb = [sc.enter_context(nc.psum_tensor(f"pb{half}_{i}", [128, 512], F32))
                      for i in range(8)]
                XBF = lambda par: WBF[:, par * 2048:(par + 1) * 2048]
                SQB = lambda par: WBF[:, 4096 + par * 2048: 4096 + (par + 1) * 2048]

                p = Plan(sems, counts, marks, entry_guards=st_guard())
                cols = slice(half * T, (half + 1) * T)

                t_ld, t_cast, t_sq, t_summ, t_sqm = {}, {}, {}, {}, {}
                for i in range(NCH):
                    par = i % 2
                    po, pe_ = par * T, (par + 1) * T
                    deps = [t_ones, t_cn]
                    if i >= 2:
                        deps += [t_cast[i - 2]]
                    t_ld[i] = p.do("sync", lambda e, i=i, po=po, pe_=pe_: e.dma_start(
                        out=XCH[:, po:pe_], in_=xT[i * 128:(i + 1) * 128, cols]),
                        deps=deps, sig=f"lx{i % 2}", inc=16)
                    deps = [t_ld[i]]
                    if i >= 2:
                        deps += [t_summ[i - 2], t_sq[i - 2]]
                    t_cast[i] = p.do("vector", lambda e, par=par, po=po, pe_=pe_:
                                     e.tensor_copy(out=XBF(par), in_=XCH[:, po:pe_]),
                                     deps=deps, sig="dve")
                    deps = [t_cast[i]]
                    if i >= 2:
                        deps += [t_sqm[i - 2]]
                    t_sq[i] = p.do("gpsimd", lambda e, par=par: e.tensor_tensor(
                        out=SQB(par), in0=XBF(par), in1=XBF(par), op=ALU.mult),
                        deps=deps, sig="pool")
                    for gi in range(4):
                        tok = p.do("tensor", lambda e, i=i, par=par, gi=gi: e.matmul(
                            pb[gi][0:1, :], ones_col[:],
                            XBF(par)[:, gi * 512:(gi + 1) * 512],
                            start=(i == 0), stop=(i == NCH - 1),
                            skip_group_check=True),
                            deps=[t_cast[i]] if gi == 0 else [],
                            sig="pe" if gi == 3 else None)
                    t_summ[i] = tok
                    for gi in range(4):
                        tok = p.do("tensor", lambda e, i=i, par=par, gi=gi: e.matmul(
                            pb[4 + gi][0:1, :], ones_col[:],
                            SQB(par)[:, gi * 512:(gi + 1) * 512],
                            start=(i == 0), stop=(i == NCH - 1),
                            skip_group_check=True),
                            deps=[t_sq[i]] if gi == 0 else [],
                            sig="pe" if gi == 3 else None)
                    t_sqm[i] = tok

                t_rows_done, t_rstd, t_mrstd = ln_rows(
                    p, t_summ[NCH - 1], t_sqm[NCH - 1], pb[0:4], pb[4:8])
                t_bc_done = ln_bcasts(p, t_rstd, t_mrstd, t_rows_done, pb)

                t_norm = {}
                for i in range(NCH):
                    par = i % 2
                    po, pe_ = par * T, (par + 1) * T
                    deps = [t_bc_done]
                    if i >= 2:
                        deps += [t_norm[i - 2]]
                    t_l2 = p.do("sync", lambda e, i=i, po=po, pe_=pe_: e.dma_start(
                        out=XCH[:, po:pe_], in_=xT[i * 128:(i + 1) * 128, cols]),
                        deps=deps, sig=f"lx{i % 2}", inc=16)
                    for g4 in range(4):
                        p.do("vector", lambda e, po=po, g4=g4: e.tensor_tensor(
                            NRM[:, :], XCH[:, po + g4 * 512:po + (g4 + 1) * 512],
                            pb[2 * g4][:, :], op=ALU.mult),
                            deps=[t_l2] if g4 == 0 else [])
                        tok = p.do("vector", lambda e, i=i, g4=g4: e.tensor_tensor(
                            XLNT[:, i * T + g4 * 512:i * T + (g4 + 1) * 512],
                            NRM[:, :], pb[2 * g4 + 1][:, :], op=ALU.subtract),
                            sig="dve" if g4 == 3 else None)
                    t_norm[i] = tok

                slabs = list(range(12)) if half == 0 else list(range(8))
                t_wcast, t_mmlast, t_ev, t_st = {}, {}, {}, {}
                for si, s in enumerate(slabs):
                    wslot = si % 2
                    for piece in range(2):
                        deps = []
                        if si == 0 and piece == 0:
                            deps += [t_summ[NCH - 1], t_sqm[NCH - 1]]
                        if si > 0 or piece > 0:
                            deps += [t_wcast.get((si, piece - 1),
                                                 t_wcast.get((si - 1, 1)))]
                        t_wl = p.do("sync", lambda e, s=s, piece=piece: e.dma_start(
                            out=WF32[:, :],
                            in_=wqkv_s[s * 128:(s + 1) * 128,
                                       piece * 4096:(piece + 1) * 4096]),
                            deps=deps, sig="lw", inc=16)
                        deps = [t_wl]
                        if si >= 2 and piece == 0:
                            deps += [t_mmlast[slabs[si - 2]]]
                        t_wcast[(si, piece)] = p.do(
                            "vector", lambda e, wslot=wslot, piece=piece: e.tensor_copy(
                                out=WBF[:, wslot * 8192 + piece * 4096:
                                        wslot * 8192 + (piece + 1) * 4096],
                                in_=WF32[:, :]),
                            deps=deps, sig="dve")
                    for fc in range(4):
                        rowi = si * 4 + fc
                        stg = rowi % 2
                        for tg in range(4):
                            key = rowi * 4 + tg
                            pp = pb[2 + key % 2]
                            deps = [t_wcast[(si, 1)], t_norm[NCH - 1]]
                            if key >= 2:
                                deps += [t_ev.get(key - 2)]
                            for c in range(NCH):
                                tok = p.do("tensor",
                                           lambda e, wslot=wslot, fc=fc, tg=tg, c=c,
                                           pp=pp: e.matmul(
                                               pp[:, :],
                                               WBF[:, wslot * 8192 + c * 512 + fc * 128:
                                                   wslot * 8192 + c * 512 + (fc + 1) * 128],
                                               XLNT[:, c * T + tg * 512:
                                                    c * T + (tg + 1) * 512],
                                               start=(c == 0), stop=(c == NCH - 1),
                                               skip_group_check=True),
                                           deps=deps if c == 0 else [],
                                           sig="pe" if c == NCH - 1 else None)
                            t_mm = tok
                            if fc == 3 and tg == 3:
                                t_mmlast[s] = t_mm
                            deps = [t_mm]
                            if rowi >= 2:
                                deps += [t_st.get(rowi - 2)]
                            t_ev[key] = p.do("scalar",
                                             lambda e, stg=stg, tg=tg, pp=pp: e.copy(
                                                 out=QSTG[:, stg * 2048 + tg * 512:
                                                          stg * 2048 + (tg + 1) * 512],
                                                 in_=pp[:, :]),
                                             deps=deps, sig="act")
                        if s < 4:
                            dst, r0, c0 = q_dram, s * 512 + fc * 128, half * T
                        elif s < 8:
                            dst, r0, c0 = k_dram, (s - 4) * 512 + fc * 128, half * T
                        else:
                            dst, r0, c0 = v_dram, (s - 8) * 512 + fc * 128, 0
                        t_st[rowi] = p.do("gpsimd",
                                          lambda e, dst=dst, r0=r0, c0=c0,
                                          stg=stg: e.dma_start(
                                              out=dst[r0:r0 + 128, c0:c0 + T],
                                              in_=QSTG[:, stg * 2048:stg * 2048 + 2048]),
                                          deps=[t_ev[rowi * 4 + 3]],
                                          sig=f"sq{rowi % 2}", inc=16)
                qkv_last_store[0] = [("sq0", counts["sq0"]), ("sq1", counts["sq1"])]
                p.run(nc)

        phase_ab(0)
        phase_ab(1)
        qkv_done = qkv_last_store[0]

        t_csum_done = None
        with contextlib.ExitStack() as sc:
            QH = sc.enter_context(nc.sbuf_tensor("QH", [128, 2 * S], BF16))
            KH = sc.enter_context(nc.sbuf_tensor("KH", [128, 2 * S], BF16))
            EOWN = sc.enter_context(nc.sbuf_tensor("EOWN", [128, 4 * T], BF16))
            ESCR = sc.enter_context(nc.sbuf_tensor("ESCR", [128, 2 * T], BF16))
            WDB = (sc.enter_context(nc.sbuf_tensor("WDB", [128, 32], F32))
                   if debug else None)
            sq = [sc.enter_context(nc.psum_tensor(f"sq{i}", [128, 1024], F32))
                  for i in range(3)]
            cs_ps = sc.enter_context(nc.psum_tensor("cs_ps", [128, 512], F32))

            p = Plan(sems, counts, marks, entry_guards=st_guard())
            t_qld, t_kld, t_head, t_csev, t_exp, t_w, t_cs = {}, {}, {}, {}, {}, {}, {}
            t_pr, t_dr, t_tail, t_wtail = {}, {}, {}, {}
            qseq = 0

            def emit_colsum(h, qc):
                gqc = h * 32 + qc
                deps = [t_w[gqc]]
                if qc == 0:
                    deps += [t_csev.get(h - 1)]
                for kc in range(4):
                    tok = p.do("tensor", lambda e, h=h, qc=qc, kc=kc: e.matmul(
                        cs_ps[32 * kc:32 * kc + 1, :],
                        wcolb[:, qc % 2:qc % 2 + 1],
                        EOWN[:, (qc % 4) * T + kc * 512:(qc % 4) * T + (kc + 1) * 512],
                        start=(qc == 0), stop=(qc == 31),
                        tile_position=(0, 32 * kc), skip_group_check=True),
                        deps=deps if kc == 0 else [],
                        sig="pe" if kc == 3 else None)
                t_cs[gqc] = tok

            for h in range(NH):
                hs = (h % 2) * S
                deps = list(qkv_done)
                if h >= 2:
                    deps += [t_head[h - 2]]
                t_qld[h] = p.do("sync", lambda e, h=h, hs=hs: e.dma_start(
                    out=QH[:, hs:hs + S], in_=q_dram[h * 128:(h + 1) * 128, :]),
                    deps=deps, sig=f"lq{h % 2}", inc=16)
                t_kld[h] = p.do("sync", lambda e, h=h, hs=hs: e.dma_start(
                    out=KH[:, hs:hs + S], in_=k_dram[h * 128:(h + 1) * 128, :]),
                    sig=f"lq{h % 2}", inc=16)

                def emit_denw(qc):
                    gqc = h * 32 + qc
                    e_tok = t_exp[(gqc + 1) * 4 - 1]
                    t_dcp = p.do("scalar", lambda e, qc=qc: e.copy(
                        out=dstage[:, (qc % 4) * 4:(qc % 4) * 4 + 4],
                        in_=dparts[:, (qc % 4) * 4:(qc % 4) * 4 + 4]),
                        deps=[e_tok] +
                        ([t_w.get(gqc - 2)] if gqc >= 2 else []), sig="act")
                    p.do("vector", lambda e, qc=qc: e.reduce_sum(
                        denb[:, qc % 2:qc % 2 + 1],
                        dstage[:, (qc % 4) * 4:(qc % 4) * 4 + 4], axis=AXX),
                        deps=[t_dcp])
                    p.do("vector", lambda e, qc=qc: e.reciprocal(
                        wcolf[:, qc % 2:qc % 2 + 1], denb[:, qc % 2:qc % 2 + 1]))
                    t_w[gqc] = p.do("vector", lambda e, qc=qc: e.tensor_copy(
                        out=wcolb[:, qc % 2:qc % 2 + 1],
                        in_=wcolf[:, qc % 2:qc % 2 + 1]),
                        deps=[t_cs.get(gqc - 2)] if gqc >= 2 else [], sig="dve")
                    if debug and h == 0:
                        p.do("vector", lambda e, qc=qc: e.tensor_copy(
                            out=WDB[:, qc:qc + 1],
                            in_=wcolf[:, qc % 2:qc % 2 + 1]))

                for qc in range(32):
                    gqc = h * 32 + qc
                    for q4 in range(4):
                        buf = sq[qseq % 3]
                        deps = [t_qld[h], t_kld[h]]
                        if qseq >= 3:
                            deps += [t_exp[qseq - 3]]
                        p.do("tensor", lambda e, hs=hs, qc=qc, q4=q4, buf=buf: e.matmul(
                            buf[:, 0:512],
                            QH[:, hs + qc * 128:hs + (qc + 1) * 128],
                            KH[:, hs + q4 * 1024:hs + q4 * 1024 + 512],
                            start=True, stop=True),
                            deps=deps)
                        t_sc = p.do("tensor", lambda e, hs=hs, qc=qc, q4=q4,
                                    buf=buf: e.matmul(
                                        buf[:, 512:1024],
                                        QH[:, hs + qc * 128:hs + (qc + 1) * 128],
                                        KH[:, hs + q4 * 1024 + 512:
                                           hs + (q4 + 1) * 1024],
                                        start=True, stop=True),
                                    sig="pe")
                        if q4 < 2:
                            dtile, dsl = EOWN, ((qc % 4) * T + q4 * 1024,
                                                (qc % 4) * T + (q4 + 1) * 1024)
                        else:
                            dtile, dsl = ESCR, ((qc % 2) * T + (q4 - 2) * 1024,
                                                (qc % 2) * T + (q4 - 1) * 1024)
                        deps = [t_sc]
                        if q4 == 0 and gqc >= 4:
                            deps += [t_cs.get(gqc - 4)]
                        t_exp[qseq] = p.do("scalar",
                                           lambda e, dtile=dtile, dsl=dsl, qc=qc,
                                           q4=q4, buf=buf: e.activation(
                                               dtile[:, dsl[0]:dsl[1]], buf[:, :],
                                               AF.Exp,
                                               bias=consts[:, CEXPB:CEXPB + 1],
                                               scale=consts[:, CEXPS:CEXPS + 1],
                                               accum_out=dparts[:, (qc % 4) * 4 + q4:
                                                                (qc % 4) * 4 + q4 + 1]),
                                           deps=deps, sig="act")
                        qseq += 1
                    if qc >= 1:
                        emit_denw(qc - 1)
                    if qc >= 2:
                        emit_colsum(h, qc - 2)
                emit_denw(31)
                emit_colsum(h, 30)
                emit_colsum(h, 31)
                t_head[h] = t_cs[h * 32 + 31]
                t_cp = {}
                for kc in range(4):
                    t_cp[kc] = p.do("vector", lambda e, kc=kc: e.tensor_copy(
                        out=cstage[0:1, kc * 512:(kc + 1) * 512],
                        in_=cs_ps[32 * kc:32 * kc + 1, :]),
                        deps=([t_head[h]] if kc == 0 else []) +
                        ([t_csev.get(h - 1)] if kc == 0 else []),
                        sig="dve" if kc == 3 else None)
                t_csev[h] = p.do("gpsimd", lambda e, h=h: e.dma_start(
                    out=csum_dram[h:h + 1, :], in_=cstage[0:1, :]),
                    deps=[t_cp[3]], sig="sc", inc=16)
            if debug:
                p.do("gpsimd", lambda e: e.dma_start(out=wdbg, in_=WDB[:, :]),
                     deps=[t_w[31]], sig="sc", inc=16)
            t_csum_done = t_csev[NH - 1]
            p.run(nc)

        with contextlib.ExitStack() as sc:
            WF32 = sc.enter_context(nc.sbuf_tensor("WF32d", [128, 2048], F32))
            WOBF = sc.enter_context(nc.sbuf_tensor("WOBF", [128, 4 * 8192], BF16))
            VAL = sc.enter_context(nc.sbuf_tensor("VAL", [128, 8192], BF16))
            VALT = sc.enter_context(nc.sbuf_tensor("VALT", [128, 8192], BF16))
            CBB = sc.enter_context(nc.sbuf_tensor("CBB", [128, 1024], BF16))
            XCHd = sc.enter_context(nc.sbuf_tensor("XCHd", [128, 2 * 2048], F32))
            X2ST = sc.enter_context(nc.sbuf_tensor("X2ST", [128, 2 * 2048], F32))
            X2BF = sc.enter_context(nc.sbuf_tensor("X2BF", [128, 2 * 512], BF16))
            X2SQ = sc.enter_context(nc.sbuf_tensor("X2SQ", [128, 2 * 512], BF16))
            pb = [sc.enter_context(nc.psum_tensor(f"pd{i}", [128, 512], F32))
                  for i in range(8)]

            p = Plan(sems, counts, marks, entry_guards=st_guard())
            t_wc = None
            for s in range(4):
                for piece in range(4):
                    deps = [t_wc] if t_wc else []
                    t_wl = p.do("sync", lambda e, s=s, piece=piece: e.dma_start(
                        out=WF32[:, :],
                        in_=wo_s[s * 128:(s + 1) * 128,
                                 piece * 2048:(piece + 1) * 2048]),
                        deps=deps, sig="lw", inc=16)
                    t_wc = p.do("vector", lambda e, s=s, piece=piece: e.tensor_copy(
                        out=WOBF[:, s * 8192 + piece * 2048:
                                s * 8192 + (piece + 1) * 2048],
                        in_=WF32[:, :]), deps=[t_wl], sig="dve")

            t_vld, t_val, t_x2, t_x2st, t_stat, t_bfd = {}, {}, {}, {}, {}, {}
            t_crl, t_bm = {}, {}
            t_xld = {}
            for gi in range(4):
                for h8 in range(2):
                    deps = list(qkv_done)
                    if gi >= 1:
                        deps += [t_val.get((gi - 1, h8 * 8 + 7))]
                    t_vld[(gi, h8)] = p.do("sync", lambda e, gi=gi, h8=h8: e.dma_start(
                        out=VAL.rearrange("p (c t) -> p c t", t=512)
                        [:, h8 * 8:(h8 + 1) * 8, :],
                        in_=v_dram.rearrange("(c p) t -> p c t", p=128)
                        [:, h8 * 8:(h8 + 1) * 8, gi * 512:(gi + 1) * 512]),
                        deps=deps, sig=f"lv{h8}", inc=16)
                def emit_xld(pc, gi=gi):
                    kk = gi * 4 + pc
                    deps = []
                    if kk >= 2:
                        kp = kk - 2
                        d = t_x2.get((kp // 4, (kp % 4) * 4 + 3))
                        assert d is not None
                        deps += [d]
                    t_xld[kk] = p.do("sync", lambda e, gi=gi, pc=pc: e.dma_start(
                        out=XCHd.rearrange("p (c t) -> p c t", t=512)
                        [:, (pc % 2) * 4:(pc % 2) * 4 + 4, :],
                        in_=xT.rearrange("(c p) t -> p c t", p=128)
                        [:, pc * 4:pc * 4 + 4, gi * 512:(gi + 1) * 512]),
                        deps=deps, sig=f"lp{pc % 2}", inc=16)
                for pc in range(2):
                    emit_xld(pc)
                t_cbd = {}

                def emit_val(h, gi=gi):
                    t_val[(gi, h)] = p.do("vector",
                                          lambda e, h=h: e.tensor_tensor(
                                              VALT[:, h * 512:(h + 1) * 512],
                                              VAL[:, h * 512:(h + 1) * 512],
                                              CBB[:, (h % 2) * 512:(h % 2 + 1) * 512],
                                              op=ALU.mult),
                                          deps=[t_cbd[h], t_cbd.get(h + 1),
                                                t_vld[(gi, h // 8)]], sig="dve")

                for h in range(NH):
                    deps = [t_csum_done]
                    if h >= 2:
                        deps += [t_bm.get((gi, h - 2))]
                    elif gi >= 1:
                        deps += [t_bm.get((gi - 1, NH - 2 + h))]
                    t_crl[(gi, h)] = p.do("sync", lambda e, gi=gi, h=h: e.dma_start(
                        out=csrow[0:1, (h % 2) * 512:(h % 2 + 1) * 512],
                        in_=csum_dram[h:h + 1, gi * 512:(gi + 1) * 512]),
                        deps=deps, sig=f"lc{h % 2}", inc=16)
                    deps = [t_crl[(gi, h)], t_wc]
                    if h >= 2:
                        deps += [t_val.get((gi, h - 2))]
                    elif gi >= 1:
                        deps += [t_val.get((gi - 1, NH - 2 + h))]
                    t_bm[(gi, h)] = p.do("tensor", lambda e, h=h: e.matmul(
                        pb[h % 2][:, :], ones_row[:],
                        csrow[0:1, (h % 2) * 512:(h % 2 + 1) * 512],
                        start=True, stop=True), deps=deps, sig="pe")
                    t_cbd[h] = p.do("scalar", lambda e, h=h: e.copy(
                        out=CBB[:, (h % 2) * 512:(h % 2 + 1) * 512],
                        in_=pb[h % 2][:, :]), deps=[t_bm[(gi, h)]], sig="act")
                    if h >= 1:
                        emit_val(h - 1)
                emit_val(NH - 1)
                for oc in range(NCH):
                    k = gi * NCH + oc
                    pp = pb[2 + k % 4 if k % 4 < 2 else 4 + k % 4]
                    deps = [t_val[(gi, NH - 1)]]
                    if k >= 4:
                        deps += [t_x2.get(((k - 4) // NCH, (k - 4) % NCH))]
                    for c in range(NCH):
                        tok = p.do("tensor", lambda e, oc=oc, c=c, pp=pp: e.matmul(
                            pp[:, :],
                            WOBF[:, (oc // 4) * 8192 + c * 512 + (oc % 4) * 128:
                                 (oc // 4) * 8192 + c * 512 + (oc % 4) * 128 + 128],
                            VALT[:, c * 512:(c + 1) * 512],
                            start=(c == 0), stop=(c == NCH - 1),
                            skip_group_check=True),
                            deps=deps if c == 0 else [],
                            sig="pe" if c == NCH - 1 else None)
                    t_amm = tok
                    slot4 = (oc // 4) % 2
                    deps = [t_amm, t_xld[gi * 4 + oc // 4]]
                    if k >= 8:
                        deps += [t_x2st.get(((k - 8) // NCH, ((k - 8) % NCH) // 4))]
                    t_x2[(gi, oc)] = p.do("vector",
                                          lambda e, oc=oc, pp=pp, slot4=slot4:
                                          e.tensor_tensor(
                                              X2ST[:, slot4 * 2048 + (oc % 4) * 512:
                                                   slot4 * 2048 + (oc % 4 + 1) * 512],
                                              XCHd[:, ((oc // 4) % 2) * 2048 +
                                                   (oc % 4) * 512:
                                                   ((oc // 4) % 2) * 2048 +
                                                   (oc % 4 + 1) * 512],
                                              pp[:, :], op=ALU.add),
                                          deps=deps, sig="dve")
                    deps = [t_x2[(gi, oc)]]
                    if k >= 2:
                        deps += [t_stat.get(k - 2)]
                    t_bfd[k] = p.do("vector", lambda e, oc=oc, slot4=slot4:
                                    e.tensor_copy(
                                        out=X2BF[:, (oc % 2) * 512:(oc % 2 + 1) * 512],
                                        in_=X2ST[:, slot4 * 2048 + (oc % 4) * 512:
                                                 slot4 * 2048 + (oc % 4 + 1) * 512]),
                                    deps=deps, sig="dve")
                    t_sq2 = p.do("gpsimd", lambda e, oc=oc: e.tensor_tensor(
                        out=X2SQ[:, (oc % 2) * 512:(oc % 2 + 1) * 512],
                        in0=X2BF[:, (oc % 2) * 512:(oc % 2 + 1) * 512],
                        in1=X2BF[:, (oc % 2) * 512:(oc % 2 + 1) * 512], op=ALU.mult),
                        deps=[t_bfd[k]], sig="pool")
                    p.do("tensor", lambda e, oc=oc: e.matmul(
                        pb[4][0:1, :], ones_col[:],
                        X2BF[:, (oc % 2) * 512:(oc % 2 + 1) * 512],
                        start=(oc == 0), stop=(oc == NCH - 1),
                        skip_group_check=True), deps=[t_bfd[k]])
                    t_stat[k] = p.do("tensor", lambda e, oc=oc: e.matmul(
                        pb[5][0:1, :], ones_col[:],
                        X2SQ[:, (oc % 2) * 512:(oc % 2 + 1) * 512],
                        start=(oc == 0), stop=(oc == NCH - 1),
                        skip_group_check=True), deps=[t_sq2], sig="pe")
                    if oc % 4 == 3 and oc // 4 < 2:
                        emit_xld(oc // 4 + 2)
                    if oc % 4 == 3:
                        t_x2st[(gi, oc // 4)] = p.do(
                            "gpsimd",
                            lambda e, gi=gi, oc=oc, slot4=slot4: e.dma_start(
                                out=x2_dram.rearrange("(c p) t -> p c t", p=128)
                                [:, oc - 3:oc + 1, gi * 512:(gi + 1) * 512],
                                in_=X2ST.rearrange("p (c t) -> p c t", t=512)
                                [:, slot4 * 4:slot4 * 4 + 4, :]),
                            deps=[t_x2[(gi, oc)]], sig=f"s2{slot4}", inc=16)
                p.do("vector", lambda e, gi=gi: e.tensor_copy(
                    out=srow[0:1, gi * 512:(gi + 1) * 512], in_=pb[4][0:1, :]),
                    deps=[t_stat[gi * NCH + NCH - 1]], sig="dve")
                p.do("vector", lambda e, gi=gi: e.tensor_copy(
                    out=s2row[0:1, gi * 512:(gi + 1) * 512], in_=pb[5][0:1, :]),
                    deps=[t_stat[gi * NCH + NCH - 1]], sig="dve")
            t_x2_all = None
            x2_done = [("s20", counts["s20"]), ("s21", counts["s21"])]
            if debug:
                p.do("gpsimd", lambda e: e.dma_start(out=wobf_dbg, in_=WOBF[:, :]),
                     deps=x2_done, sig="sc", inc=16)
                p.do("gpsimd", lambda e: e.dma_start(out=valt_dbg, in_=VALT[:, :]),
                     sig="sc", inc=16)

            p.do("vector", lambda e: e.tensor_scalar_mul(
                mrow[0:1, :], srow[0:1, :], 1.0 / D))
            p.do("vector", lambda e: e.tensor_scalar_mul(
                vrow[0:1, :], s2row[0:1, :], 1.0 / D))
            p.do("vector", lambda e: e.tensor_tensor(
                srow[0:1, :], mrow[0:1, :], mrow[0:1, :], op=ALU.mult))
            t_v2 = p.do("vector", lambda e: e.tensor_tensor(
                vrow[0:1, :], vrow[0:1, :], srow[0:1, :], op=ALU.subtract),
                sig="dve")
            t_ln2 = p.do("scalar", lambda e: e.activation(
                s2row[0:1, :], vrow[0:1, :], AF.Ln,
                bias=consts[0:1, CEPS:CEPS + 1]), deps=[t_v2], sig="act")
            t_rstd2 = p.do("scalar", lambda e: e.activation(
                rstdrow[0:1, :], s2row[0:1, :], AF.Exp,
                scale=consts[0:1, CNHALF:CNHALF + 1]), deps=[t_ln2], sig="act")
            t_mrstd2 = p.do("vector", lambda e: e.tensor_tensor(
                mrstdrow[0:1, :], mrow[0:1, :], rstdrow[0:1, :], op=ALU.mult),
                deps=[t_rstd2], sig="dve")
            p.run(nc)

        t_hst = {}
        with contextlib.ExitStack() as sc:
            XLNT = sc.enter_context(nc.sbuf_tensor("XLNT2", [128, NCH * T], BF16))
            XCH = sc.enter_context(nc.sbuf_tensor("XCH2", [128, 2 * T], F32))
            NRM = sc.enter_context(nc.sbuf_tensor("NRM2", [128, 512], F32))
            WGU8 = sc.enter_context(nc.sbuf_tensor("WGU8", [128, 2 * 8192], U8))
            WGBF = sc.enter_context(nc.sbuf_tensor("WGBF", [128, 8192], BF16))
            WUBF = sc.enter_context(nc.sbuf_tensor("WUBF", [128, 8192], BF16))
            GT = sc.enter_context(nc.sbuf_tensor("GT", [128, 2 * 512], BF16))
            HST = sc.enter_context(nc.sbuf_tensor("HST", [128, 2 * 2048], BF16))
            pb = [sc.enter_context(nc.psum_tensor(f"pq{i}", [128, 512], F32))
                  for i in range(8)]

            p = Plan(sems, counts, marks, entry_guards=st_guard())
            t_bc_done = ln_bcasts(p, t_rstd2, t_mrstd2, None, pb)
            t_norm = {}
            for i in range(NCH):
                par = i % 2
                po, pe_ = par * T, (par + 1) * T
                deps = [t_bc_done] + x2_done
                if i >= 2:
                    deps += [t_norm[i - 2]]
                t_l2 = p.do("sync", lambda e, i=i, po=po, pe_=pe_: e.dma_start(
                    out=XCH[:, po:pe_], in_=x2_dram[i * 128:(i + 1) * 128, :]),
                    deps=deps, sig=f"lx{i % 2}", inc=16)
                for g4 in range(4):
                    p.do("vector", lambda e, po=po, g4=g4: e.tensor_tensor(
                        NRM[:, :], XCH[:, po + g4 * 512:po + (g4 + 1) * 512],
                        pb[2 * g4][:, :], op=ALU.mult),
                        deps=[t_l2] if g4 == 0 else [])
                    tok = p.do("vector", lambda e, i=i, g4=g4: e.tensor_tensor(
                        XLNT[:, i * T + g4 * 512:i * T + (g4 + 1) * 512],
                        NRM[:, :], pb[2 * g4 + 1][:, :], op=ALU.subtract),
                        sig="dve" if g4 == 3 else None)
                t_norm[i] = tok

            t_gc, t_uc, t_mmg, t_mmu, t_h = {}, {}, {}, {}, {}
            pend = {}

            def emit_h(kk):
                tok, t_silu, cc, gi, ppu = pend[kk]
                deps = [tok, t_silu, t_scl]
                if gi == 0 and cc >= 2:
                    deps += [t_hst.get(cc - 2)]
                t_h[kk] = p.do("vector",
                               lambda e, cc=cc, gi=gi, ppu=ppu:
                               e.scalar_tensor_tensor(
                                   HST[:, (cc % 2) * 2048 + gi * 512:
                                       (cc % 2) * 2048 + (gi + 1) * 512],
                                   ppu[:, :], su[:, cc:cc + 1],
                                   GT[:, (gi % 2) * 512:(gi % 2 + 1) * 512],
                                   op0=ALU.mult, op1=ALU.mult),
                               deps=deps, sig="dve")

            for s in range(11):
                deps = [t_gc[s - 1]] if s >= 1 else []
                t_gl = p.do("sync", lambda e, s=s: e.dma_start(
                    out=WGU8[:, 0:8192], in_=wg_s[s * 128:(s + 1) * 128, :]),
                    deps=deps, sig="lg", inc=16)
                deps = [t_gl] + ([t_mmg[s - 1]] if s >= 1 else [])
                t_gc[s] = p.do("vector", lambda e: e.tensor_scalar_add(
                    WGBF[:, :], WGU8[:, 0:8192], -128.0), deps=deps, sig="dve")
                deps = [t_uc[s - 1]] if s >= 1 else []
                t_ul = p.do("sync", lambda e, s=s: e.dma_start(
                    out=WGU8[:, 8192:16384], in_=wu_s[s * 128:(s + 1) * 128, :]),
                    deps=deps, sig="lu", inc=16)
                deps = [t_ul] + ([t_mmu[s - 1]] if s >= 1 else [])
                t_uc[s] = p.do("vector", lambda e: e.tensor_scalar_add(
                    WUBF[:, :], WGU8[:, 8192:16384], -128.0), deps=deps, sig="dve")
                for hc in range(4):
                    cc = s * 4 + hc
                    for gi in range(4):
                        kk = cc * 4 + gi
                        ppg = pb[gi % 2]
                        ppu = pb[2 + gi % 2]
                        deps = [t_gc[s], t_norm[NCH - 1]]
                        if kk >= 2:
                            deps += [t_h.get(kk - 2)]
                        for c in range(NCH):
                            tok = p.do("tensor", lambda e, hc=hc, gi=gi, c=c,
                                       ppg=ppg: e.matmul(
                                           ppg[:, :],
                                           WGBF[:, c * 512 + hc * 128:
                                                c * 512 + (hc + 1) * 128],
                                           XLNT[:, c * T + gi * 512:
                                                c * T + (gi + 1) * 512],
                                           start=(c == 0), stop=(c == NCH - 1),
                                           skip_group_check=True),
                                       deps=deps if c == 0 else [],
                                       sig="pe" if c == NCH - 1 else None)
                        t_mmg[s] = tok
                        t_silu = p.do("scalar", lambda e, cc=cc, gi=gi, ppg=ppg:
                                      e.activation(
                                          GT[:, (gi % 2) * 512:(gi % 2 + 1) * 512],
                                          ppg[:, :], AF.Silu, scale=sg[:, cc:cc + 1]),
                                      deps=[tok, t_scl], sig="act")
                        deps = [t_uc[s]]
                        for c in range(NCH):
                            tok = p.do("tensor", lambda e, hc=hc, gi=gi, c=c,
                                       ppu=ppu: e.matmul(
                                           ppu[:, :],
                                           WUBF[:, c * 512 + hc * 128:
                                                c * 512 + (hc + 1) * 128],
                                           XLNT[:, c * T + gi * 512:
                                                c * T + (gi + 1) * 512],
                                           start=(c == 0), stop=(c == NCH - 1),
                                           skip_group_check=True),
                                       deps=deps if c == 0 else [],
                                       sig="pe" if c == NCH - 1 else None)
                        t_mmu[s] = tok
                        pend[kk] = (tok, t_silu, cc, gi, ppu)
                        if kk >= 1:
                            emit_h(kk - 1)
                    if cc == 43:
                        emit_h(cc * 4 + 3)
                    if cc >= 1:
                        t_hst[cc - 1] = p.do("gpsimd", lambda e, cc=cc: e.dma_start(
                            out=h_dram[(cc - 1) * 128:cc * 128, :],
                            in_=HST[:, ((cc - 1) % 2) * 2048:
                                    ((cc - 1) % 2) * 2048 + 2048]),
                            deps=[t_h[(cc - 1) * 4 + 3]], sig=f"sh{(cc - 1) % 2}",
                            inc=16)
                    if cc == 43:
                        t_hst[43] = p.do("gpsimd", lambda e: e.dma_start(
                            out=h_dram[43 * 128:44 * 128, :],
                            in_=HST[:, (43 % 2) * 2048:(43 % 2) * 2048 + 2048]),
                            deps=[t_h[43 * 4 + 3]], sig="sh1", inc=16)
            h_done = [("sh0", counts["sh0"]), ("sh1", counts["sh1"])]
            p.run(nc)

        with contextlib.ExitStack() as sc:
            WDU8 = sc.enter_context(nc.sbuf_tensor("WDU8", [128, 2 * 5632], U8))
            WDBF = sc.enter_context(nc.sbuf_tensor("WDBF", [128, 22528], BF16))
            HT = sc.enter_context(nc.sbuf_tensor("HT", [128, 2 * 11264], BF16))
            X2CH = sc.enter_context(nc.sbuf_tensor("X2CH", [128, 2 * 2048], F32))
            OUTS = sc.enter_context(nc.sbuf_tensor("OUTS", [128, 2 * 2048], F32))
            pb = [sc.enter_context(nc.psum_tensor(f"pf{i}", [128, 512], F32))
                  for i in range(4)]

            p = Plan(sems, counts, marks, entry_guards=st_guard())
            t_out, t_ost, t_f, t_htld, t_x2ld, t_wdc = {}, {}, {}, {}, {}, {}
            t_lastread = {}
            for dg in range(4):
                for piece in range(4):
                    kk = dg * 4 + piece
                    deps = [t_scl]
                    if kk >= 2:
                        deps += [t_wdc.get(kk - 2)]
                    t_wl = p.do("sync", lambda e, dg=dg, piece=piece: e.dma_start(
                        out=WDU8[:, (piece % 2) * 5632:(piece % 2 + 1) * 5632],
                        in_=wd_s[dg * 128:(dg + 1) * 128,
                                 piece * 5632:(piece + 1) * 5632]),
                        deps=deps, sig=f"lwd{piece % 2}", inc=16)
                    deps = [t_wl]
                    if dg >= 1 and piece == 0:
                        deps += [t_f.get((dg * 4 - 1, 3))]
                    t_wdc[kk] = p.do("vector",
                                     lambda e, piece=piece: e.tensor_scalar_add(
                                         WDBF[:, piece * 5632:(piece + 1) * 5632],
                                         WDU8[:, (piece % 2) * 5632:
                                              (piece % 2 + 1) * 5632],
                                         -128.0), deps=deps, sig="dve")
                for tg in range(4):
                    it = dg * 4 + tg
                    for hb in range(2):
                        for j in range(6):
                            n = min(4, 22 - j * 4)
                            hc0 = hb * 22 + j * 4
                            deps = list(h_done)
                            if it >= 1:
                                deps += [t_lastread.get((it - 1, hb))]
                            t_htld[(it, hb)] = p.do(
                                "sync",
                                lambda e, hc0=hc0, n=n, hb=hb, j=j, tg=tg: e.dma_start(
                                    out=HT.rearrange("p (c t) -> p c t", t=512)
                                    [:, hb * 22 + j * 4:hb * 22 + j * 4 + n, :],
                                    in_=h_dram.rearrange("(c p) t -> p c t", p=128)
                                    [:, hc0:hc0 + n, tg * 512:(tg + 1) * 512]),
                                deps=deps if j == 0 else [], sig=f"lh{hb}", inc=16)
                    deps = []
                    if it >= 2:
                        deps += [t_out.get((it - 2, 3))]
                    t_x2ld[it] = p.do("sync", lambda e, dg=dg, tg=tg, it=it: e.dma_start(
                        out=X2CH.rearrange("p (c t) -> p c t", t=512)
                        [:, (it % 2) * 4:(it % 2) * 4 + 4, :],
                        in_=x2_dram.rearrange("(c p) t -> p c t", p=128)
                        [:, dg * 4:dg * 4 + 4, tg * 512:(tg + 1) * 512]),
                        deps=deps + x2_done, sig=f"l2{it % 2}", inc=16)
                    for hb in range(2):
                        for hc22 in range(22):
                            hc = hb * 22 + hc22
                            for dc in range(4):
                                deps = []
                                if hb == 0 and hc22 == 0:
                                    deps += [t_wdc[dg * 4 + 3]]
                                    if it >= 1:
                                        deps += [t_out.get((it - 1, dc))]
                                if hc22 == 0 and dc == 0:
                                    deps += [t_htld[(it, hb)]]
                                sig = None
                                if hc == 43 or (hc22 == 21 and dc == 3):
                                    sig = "pe"
                                tok = p.do("tensor",
                                           lambda e, hc=hc, dc=dc, hb=hb,
                                           hc22=hc22: e.matmul(
                                               pb[dc][:, :],
                                               WDBF[:, hc * 512 + dc * 128:
                                                    hc * 512 + (dc + 1) * 128],
                                               HT[:, hb * 11264 + hc22 * 512:
                                                  hb * 11264 + (hc22 + 1) * 512],
                                               start=(hc == 0), stop=(hc == 43),
                                               skip_group_check=True),
                                           deps=deps, sig=sig)
                                if hc22 == 21 and dc == 3:
                                    t_lastread[(it, hb)] = tok
                                if hc == 43:
                                    t_f[(it, dc)] = tok
                    for dc in range(4):
                        deps = [t_f[(it, dc)], t_x2ld[it], t_scl]
                        if it >= 2:
                            deps += [t_ost.get(it - 2)]
                        t_out[(it, dc)] = p.do("vector",
                                               lambda e, dg=dg, dc=dc, it=it:
                                               e.scalar_tensor_tensor(
                                                   OUTS[:, (it % 2) * 2048 + dc * 512:
                                                        (it % 2) * 2048 +
                                                        (dc + 1) * 512],
                                                   pb[dc][:, :],
                                                   sd[:, dg * 4 + dc:dg * 4 + dc + 1],
                                                   X2CH[:, (it % 2) * 2048 + dc * 512:
                                                        (it % 2) * 2048 +
                                                        (dc + 1) * 512],
                                                   op0=ALU.mult, op1=ALU.add),
                                               deps=deps, sig="dve")
                    t_ost[it] = p.do("gpsimd", lambda e, dg=dg, tg=tg, it=it: e.dma_start(
                        out=outT.rearrange("(c p) t -> p c t", p=128)
                        [:, dg * 4:dg * 4 + 4, tg * 512:(tg + 1) * 512],
                        in_=OUTS.rearrange("p (c t) -> p c t", t=512)
                        [:, (it % 2) * 4:(it % 2) * 4 + 4, :]),
                        deps=[t_out[(it, 3)]], sig=f"so{it % 2}", inc=16)
            p.do("gpsimd", lambda e: e.nop(),
                 deps=[("so0", counts["so0"]), ("so1", counts["so1"])])
            p.run(nc)

    return nc


_NC_CACHE = None


def kernel(L0_attn_x, L0_attn_w_qkv, L0_attn_w_o, L0_ffn_wq_gate, L0_ffn_s_gate,
           L0_ffn_wq_up, L0_ffn_s_up, L0_ffn_wq_down, L0_ffn_s_down):
    global _NC_CACHE
    x = np.asarray(L0_attn_x)
    w_qkv = np.ascontiguousarray(np.asarray(L0_attn_w_qkv), dtype=np.float32)
    w_o = np.ascontiguousarray(np.asarray(L0_attn_w_o), dtype=np.float32)

    wqkv_s = np.ascontiguousarray(
        w_qkv.reshape(16, 128, 12, 512).transpose(2, 1, 0, 3).reshape(12 * 128, 8192))
    wo_s = np.ascontiguousarray(
        w_o.reshape(16, 128, 4, 512).transpose(2, 1, 0, 3).reshape(4 * 128, 8192))

    def u8view(a):
        a = np.ascontiguousarray(np.asarray(a, dtype=np.int32))
        return a.view(np.uint8)[:, ::4]

    wg8 = u8view(L0_ffn_wq_gate)
    wu8 = u8view(L0_ffn_wq_up)
    wd8 = u8view(L0_ffn_wq_down)
    wg_s = np.ascontiguousarray(
        wg8.reshape(11, 512, 16, 128).transpose(0, 3, 2, 1).reshape(11 * 128, 8192))
    wu_s = np.ascontiguousarray(
        wu8.reshape(11, 512, 16, 128).transpose(0, 3, 2, 1).reshape(11 * 128, 8192))
    wd_s = np.ascontiguousarray(
        wd8.reshape(4, 512, 44, 128).transpose(0, 3, 2, 1).reshape(4 * 128, 22528))
    sg_s = np.ascontiguousarray(
        np.asarray(L0_ffn_s_gate, dtype=np.float32).reshape(44, 128).T)
    su_s = np.ascontiguousarray(
        np.asarray(L0_ffn_s_up, dtype=np.float32).reshape(44, 128).T)
    sd_s = np.ascontiguousarray(
        np.asarray(L0_ffn_s_down, dtype=np.float32).reshape(16, 128).T)

    in_maps = []
    for c in range(8):
        b, hf = c // 2, c % 2
        own = x[b, hf * T:(hf + 1) * T]
        par = x[b, (1 - hf) * T:(2 - hf) * T]
        xT_c = np.ascontiguousarray(
            np.concatenate([own, par], axis=0).T.astype(np.float32, copy=False))
        in_maps.append({
            "xT": xT_c, "wqkv_s": wqkv_s, "wo_s": wo_s,
            "wg_s": wg_s, "wu_s": wu_s, "wd_s": wd_s,
            "sg_s": sg_s, "su_s": su_s, "sd_s": sd_s,
        })

    if _NC_CACHE is None:
        _NC_CACHE = build_program()
    res = run_bass_kernel_spmd(_NC_CACHE, in_maps, core_ids=list(range(8)))

    out = np.empty((4, S, D), dtype=np.float32)
    for c in range(8):
        b, hf = c // 2, c % 2
        out[b, hf * T:(hf + 1) * T, :] = res.results[c]["outT"].T
    return out
